# version 07
# speedup vs baseline: 1.1700x; 1.0174x over previous
import math
from contextlib import ExitStack

import numpy as np

import concourse.bass as bass
import concourse.mybir as mybir
from concourse.bass_utils import run_bass_kernel_spmd

F32 = mybir.dt.float32
BF16 = mybir.dt.bfloat16
U32 = mybir.dt.uint32
AF = mybir.ActivationFunctionType
ALU = mybir.AluOpType
AX = mybir.AxisListType

D = 1024
T = 4096
KT = 8
TC = 512
NCH = T // TC


class Prog:

    NPOOL = 64

    def __init__(self, nc):
        self.nc = nc
        self.ctx = ExitStack()
        self._persist = {}
        self._pool = []
        self._free_slots = ([], [], [])
        self._slot_counts = {}
        self._base = {}
        self._name2slot = {}
        self.counts = {}
        self._uid = 0

    def uid(self):
        self._uid += 1
        return self._uid

    @staticmethod
    def _kind(name):
        if "~g" in name:
            return 1
        if "~s" in name:
            return 2
        return 0

    def sem(self, name):
        if name.startswith("!"):
            if name not in self._persist:
                self._persist[name] = self.ctx.enter_context(
                    self.nc.semaphore(name.strip("!")))
                self.counts[name] = 0
                self._base[name] = 0
            return self._persist[name]
        if name not in self._name2slot:
            kind = self._kind(name)
            if self._free_slots[kind]:
                slot = self._free_slots[kind].pop()
            else:
                slot = len(self._pool)
                self._pool.append(self.ctx.enter_context(
                    self.nc.semaphore(f"pool{slot}")))
                assert slot < self.NPOOL, "semaphore pool exhausted"
            self._name2slot[name] = slot
            self.counts[name] = self._slot_counts.get(slot, 0)
            self._base[name] = self.counts[name]
        return self._pool[self._name2slot[name]]

    def phase_reset(self):
        if not self._name2slot:
            return
        items = [(n, self._pool[s]) for n, s in self._name2slot.items()]
        with self.nc.Block() as blk:
            @blk.gpsimd
            def _(g):
                for name, h in items:
                    g.wait_ge(h, self.counts[name])
        for name, slot in self._name2slot.items():
            self._slot_counts[slot] = self.counts[name]
            del self.counts[name]
            self._free_slots[self._kind(name)].append(slot)
        self._name2slot = {}

    def mark(self, name):
        self.sem(name)
        return self.counts[name]

    def inc(self, instr, name, n=1):
        instr.then_inc(self.sem(name), n)
        self.counts[name] += n
        return instr

    def wait(self, eng, name, val=None):
        eng.wait_ge(self.sem(name), self.counts[name] if val is None else val)

    def waitb(self, eng, name, off):
        self.sem(name)
        eng.wait_ge(self.sem(name), self._base[name] + off)

    def sbuf(self, name, shape, dtype):
        return self.ctx.enter_context(self.nc.sbuf_tensor(name, shape, dtype))

    def dram(self, name, shape, dtype, kind="Internal"):
        return self.nc.dram_tensor(name, shape, dtype, kind=kind)



def build_globals(p, xrd=None):
    nc = p.nc
    p.XRD = xrd if xrd is not None else p.dram("xrd", [128, KT, T], F32)
    p.XHD = p.dram("xhd", [128, KT, T + 1], BF16)
    p.ident_bf = p.sbuf("ident_bf", [128, 128], BF16)
    p.ident_f32 = p.sbuf("ident_f32", [128, 128], F32)
    p.ones_col_f32 = p.sbuf("ones_col_f32", [128, 8], F32)
    p.ones_col_bf = p.sbuf("ones_col_bf", [128, 8], BF16)
    p.ones_row_bf = p.sbuf("ones_row_bf", [1, 512], BF16)
    p.ones_row_f32 = p.sbuf("ones_row_f32", [1, 512], F32)
    p.sel8_f32 = p.sbuf("sel8_f32", [8, 128], F32)
    p.zero_col = p.sbuf("zero_col", [128, KT, 1], BF16)
    p.eps_col = p.sbuf("eps_col", [128, 1], F32)
    iota_f = p.sbuf("iota_f", [128, 128], F32)
    iota_p = p.sbuf("iota_p", [128, 1], F32)
    p.iota_p_g = iota_p
    p.iota8 = p.sbuf("iota8", [128, 8], F32)
    with nc.Block() as blk:
        @blk.gpsimd
        def _(g):
            g.iota(iota_f[:], [[1, 128]], channel_multiplier=0,
                   allow_small_or_imprecise_dtypes=True)
            g.iota(iota_p[:], [[0, 1]], channel_multiplier=1,
                   allow_small_or_imprecise_dtypes=True)
            g.iota(p.iota8[:], [[1, 8]], channel_multiplier=0,
                   allow_small_or_imprecise_dtypes=True)
            g.memset(p.ones_col_f32[:], 1.0)
            g.memset(p.ones_col_bf[:], 1.0)
            g.memset(p.ones_row_bf[:], 1.0)
            g.memset(p.ones_row_f32[:], 1.0)
            p.inc(g.memset(p.sel8_f32[:], 0.0), "!init", 1)
            p.wait(g, "!init")
            g.memset(p.sel8_f32[0:1, :], 1.0)
            g.memset(p.eps_col[:], 1e-5)
            p.inc(g.memset(p.zero_col[:], 0.0), "!init", 1)
    with nc.Block() as blk:
        @blk.vector
        def _(v):
            v.tensor_scalar(p.ident_bf[:], iota_f[:], iota_p[:, 0:1], None,
                            ALU.is_equal)
            v.tensor_scalar(p.ident_f32[:], iota_f[:], iota_p[:, 0:1], None,
                            ALU.is_equal)

        @blk.gpsimd
        def _(g):
            p.inc(g.dma_start(p.XHD[:, :, 0:1], p.zero_col[:],
                              allow_slow_non_contiguous=True), "!initd", 16)
            p.wait(g, "!initd")



def emit_load_x(p, x_dram):
    nc = p.nc
    u = p.uid()
    sD, sP, sE, sO = f"lx{u}D~s", f"lx{u}P", f"lx{u}E", f"lx{u}O~s"
    nrow = T // 128
    bP = p.mark(sP)
    bE = p.mark(sE)
    with nc.sbuf_tensor(f"lxst{u}", [128, 2, D], F32) as st, \
         nc.sbuf_tensor(f"lxot{u}", [128, 2, KT, 128], F32) as ot, \
         nc.psum_tensor(f"lxps0{u}", [128, 128], F32) as psA, \
         nc.psum_tensor(f"lxps1{u}", [128, 128], F32) as psB, \
         nc.Block() as blk:
        psl = [psA, psB]

        @blk.sync
        def _(s):
            for i in range(nrow):
                if i >= 2:
                    p.wait(s, sP, val=bP + 8 * (i - 1))
                p.inc(s.dma_start(st[:, i % 2, :],
                                  x_dram[i * 128:(i + 1) * 128, :]),
                      sD + str(i % 2), 16)
                p.wait(s, sE, val=bE + 8 * (i + 1))
                p.inc(s.dma_start(p.XRD[:, :, i * 128:(i + 1) * 128],
                                  ot[:, i % 2, :, :]), sO + str(i % 2), 16)

        @blk.tensor
        def _(t):
            for i in range(nrow):
                p.waitb(t, sD + str(i % 2), 16 * (i // 2 + 1))
                for k in range(KT):
                    if 8 * i + k >= 2:
                        p.wait(t, sE, val=bE + 8 * i + k - 1)
                    p.inc(t.transpose(psl[(8 * i + k) % 2][:],
                                      st[:, i % 2, k * 128:(k + 1) * 128],
                                      p.ident_f32[:]), sP, 1)

        @blk.vector
        def _(v):
            for i in range(nrow):
                if i >= 2:
                    p.waitb(v, sO + str(i % 2), 16 * (i // 2))
                for k in range(KT):
                    p.wait(v, sP, val=bP + 8 * i + k + 1)
                    p.inc(v.tensor_copy(ot[:, i % 2, k, :],
                                        psl[(8 * i + k) % 2][:]), sE, 1)

        @blk.gpsimd
        def _(g):
            p.wait(g, sO + "0")
            p.wait(g, sO + "1")


def emit_store_x(p, out_dram):
    nc = p.nc
    u = p.uid()
    sP, sE, sD, sI = f"sx{u}P", f"sx{u}E", f"sx{u}D~s", f"sx{u}I~"
    nrow = T // 128
    bP = p.mark(sP)
    bE = p.mark(sE)
    with nc.sbuf_tensor(f"sxst{u}", [128, 2, D], F32) as st, \
         nc.sbuf_tensor(f"sxin{u}", [128, 2, KT, 128], F32) as xin, \
         nc.psum_tensor(f"sxps0{u}", [128, 128], F32) as psA, \
         nc.psum_tensor(f"sxps1{u}", [128, 128], F32) as psB, \
         nc.Block() as blk:
        psl = [psA, psB]

        @blk.gpsimd
        def _(g):
            for i in range(nrow):
                if i >= 2:
                    p.wait(g, sP, val=bP + 8 * (i - 1))
                p.inc(g.dma_start(xin[:, i % 2, :, :],
                                  p.XRD[:, :, i * 128:(i + 1) * 128]),
                      sI + str(i % 2) + "~g", 16)

        @blk.tensor
        def _(t):
            for i in range(nrow):
                p.waitb(t, sI + str(i % 2) + "~g", 16 * (i // 2 + 1))
                for k in range(KT):
                    if 8 * i + k >= 2:
                        p.wait(t, sE, val=bE + 8 * i + k - 1)
                    p.inc(t.transpose(psl[(8 * i + k) % 2][:],
                                      xin[:, i % 2, k, :],
                                      p.ident_f32[:]), sP, 1)

        @blk.vector
        def _(v):
            for i in range(nrow):
                if i >= 2:
                    p.waitb(v, sD + str(i % 2), 16 * (i // 2))
                for k in range(KT):
                    p.wait(v, sP, val=bP + 8 * i + k + 1)
                    p.inc(v.tensor_copy(st[:, i % 2, k * 128:(k + 1) * 128],
                                        psl[(8 * i + k) % 2][:]), sE, 1)

        @blk.sync
        def _(s):
            for i in range(nrow):
                p.wait(s, sE, val=bE + 8 * (i + 1))
                p.inc(s.dma_start(out_dram[i * 128:(i + 1) * 128, :],
                                  st[:, i % 2, :]), sD + str(i % 2), 16)
            p.wait(s, sD + "0")
            p.wait(s, sD + "1")



def emit_ln(p, src=None, gnw=None, gnb=None, dst=None, dst_off=1,
            src_bf=False, src_off=0, mul_dram=None):
    nc = p.nc
    srcd = src if src is not None else p.XRD
    XHD = dst if dst is not None else p.XHD
    u = p.uid()
    with nc.sbuf_tensor(f"lnx{u}", [128, 4, KT, TC],
                        BF16 if src_bf else F32) as xrb, \
         nc.sbuf_tensor(f"lnsq{u}", [128, 2, KT, TC], BF16) as sqb, \
         nc.sbuf_tensor(f"lnxh{u}", [128, 2, KT, TC], BF16) as xhb, \
         nc.sbuf_tensor(f"lnsb{u}", [8, 2, 2, TC], F32) as ssb, \
         nc.sbuf_tensor(f"lnMb{u}", [128, 2, TC], F32) as Mb, \
         nc.sbuf_tensor(f"lnRb{u}", [128, 2, TC], F32) as Rb, \
         nc.sbuf_tensor(f"lnVb{u}", [128, 2, TC], F32) as Vb, \
         nc.psum_tensor(f"lnS0{u}", [8, TC], F32) as psS0, \
         nc.psum_tensor(f"lnS1{u}", [8, TC], F32) as psS1, \
         nc.psum_tensor(f"lnQ0{u}", [8, TC], F32) as psQ0, \
         nc.psum_tensor(f"lnQ1{u}", [8, TC], F32) as psQ1, \
         nc.psum_tensor(f"lnB10{u}", [128, TC], F32) as psB10, \
         nc.psum_tensor(f"lnB11{u}", [128, TC], F32) as psB11, \
         nc.psum_tensor(f"lnB20{u}", [128, TC], F32) as psB20, \
         nc.psum_tensor(f"lnB21{u}", [128, TC], F32) as psB21:
        psS = [psS0, psS1]
        psQ = [psQ0, psQ1]
        psB1 = [psB10, psB11]
        psB2 = [psB20, psB21]
        sA, sP, sV, sOut = "lnA", "lnP", "lnV", "lnO~s"
        if mul_dram is not None:
            mctx = nc.sbuf_tensor(f"lnm{u}", [128, 4, KT, TC], BF16)
            mrb = mctx.__enter__()
        npk = 2 + (1 if gnw is not None else 0) + \
            (1 if mul_dram is not None else 0)
        vc = ac = pc = 0
        a_sq = {}; a_sqrt = {}
        p_sum1 = {}; p_sum2 = {}; p_b1 = {}; p_b2 = {}
        v_cp1 = {}; v_cp2 = {}; v_mb = {}; v_vb = {}; v_end = {}
        for t2 in range(NCH + 2):
            if t2 < NCH:
                ac += 1; a_sq[t2] = ac
            if 0 <= t2 - 2 < NCH:
                ac += 1; a_sqrt[t2 - 2] = ac
            if t2 < NCH:
                pc += 1; p_sum1[t2] = pc
                pc += 1; p_sum2[t2] = pc
            if 0 <= t2 - 1 < NCH:
                pc += 1; p_b1[t2 - 1] = pc
                pc += 1; p_b2[t2 - 1] = pc
            if t2 < NCH:
                vc += 1; v_cp1[t2] = vc
                vc += 1; v_cp2[t2] = vc
            if 0 <= t2 - 1 < NCH:
                vc += 3; v_mb[t2 - 1] = vc - 2; v_vb[t2 - 1] = vc
            if 0 <= t2 - 2 < NCH:
                vc += 1 + npk * KT + 1
                v_end[t2 - 2] = vc

        with nc.Block() as blk:
            @blk.gpsimd
            def _(g):
                for c in range(NCH):
                    if c >= 4:
                        p.waitb(g, sV, v_end[c - 4])
                    lo = c * TC
                    p.inc(g.dma_start(
                        xrb[:, c % 4, :, :],
                        srcd[:, :, src_off + lo:src_off + lo + TC]),
                        f"lnX{c % 4}~g", 16)
                    if mul_dram is not None:
                        p.inc(g.dma_start(mrb[:, c % 4, :, :],
                                          mul_dram[:, :, lo:lo + TC]),
                              f"lnM{c % 4}~g", 16)

            @blk.scalar
            def _(a):
                for t2 in range(NCH + 2):
                    c, c2 = t2, t2 - 2
                    if c < NCH:
                        p.waitb(a, f"lnX{c % 4}~g", 16 * (c // 4 + 1))
                        if c >= 2:
                            p.waitb(a, sP, p_sum2[c - 2])
                        for k in range(KT):
                            ins = a.activation(sqb[:, c % 2, k, :],
                                               xrb[:, c % 4, k, :], AF.Square)
                        p.inc(ins, sA, 1)
                    if 0 <= c2 < NCH:
                        p.waitb(a, sV, v_vb[c2])
                        p.inc(a.activation(Vb[:, c2 % 2, :], Vb[:, c2 % 2, :],
                                           AF.Sqrt, bias=p.eps_col[:, 0:1]),
                              sA, 1)

            @blk.tensor
            def _(t):
                for t2 in range(NCH + 2):
                    c, c1 = t2, t2 - 1
                    if c < NCH:
                        p.waitb(t, f"lnX{c % 4}~g", 16 * (c // 4 + 1))
                        if c >= 2:
                            p.waitb(t, sV, v_cp1[c - 2])
                        for k in range(KT):
                            ins = t.matmul(psS[c % 2][:],
                                           p.ones_col_bf[:] if src_bf
                                           else p.ones_col_f32[:],
                                           xrb[:, c % 4, k, :],
                                           start=(k == 0), stop=(k == KT - 1))
                        p.inc(ins, sP, 1)
                        p.waitb(t, sA, a_sq[c])
                        if c >= 2:
                            p.waitb(t, sV, v_cp2[c - 2])
                        for k in range(KT):
                            ins = t.matmul(psQ[c % 2][:], p.ones_col_bf[:],
                                           sqb[:, c % 2, k, :],
                                           start=(k == 0), stop=(k == KT - 1))
                        p.inc(ins, sP, 1)
                    if 0 <= c1 < NCH:
                        p.waitb(t, sV, v_cp1[c1])
                        if c1 >= 2:
                            p.waitb(t, sV, v_mb[c1 - 2])
                        p.inc(t.matmul(psB1[c1 % 2][:], p.sel8_f32[:],
                                       ssb[:, c1 % 2, 0, :],
                                       start=True, stop=True), sP, 1)
                        p.waitb(t, sV, v_cp2[c1])
                        if c1 >= 2:
                            p.waitb(t, sV, v_vb[c1 - 2])
                        p.inc(t.matmul(psB2[c1 % 2][:], p.sel8_f32[:],
                                       ssb[:, c1 % 2, 1, :],
                                       start=True, stop=True), sP, 1)

            @blk.vector
            def _(v):
                for t2 in range(NCH + 2):
                    c, c1, c2 = t2, t2 - 1, t2 - 2
                    if c < NCH:
                        p.waitb(v, sP, p_sum1[c])
                        if c >= 2:
                            p.waitb(v, sP, p_b2[c - 2])
                        p.inc(v.tensor_copy(ssb[:, c % 2, 0, :],
                                            psS[c % 2][:]), sV, 1)
                        p.waitb(v, sP, p_sum2[c])
                        p.inc(v.tensor_copy(ssb[:, c % 2, 1, :],
                                            psQ[c % 2][:]), sV, 1)
                    if 0 <= c1 < NCH:
                        r = c1 % 2
                        p.waitb(v, sP, p_b1[c1])
                        if c1 >= 2:
                            p.waitb(v, sV, v_end[c1 - 2])
                        p.inc(v.tensor_scalar(Mb[:, r, :], psB1[r][:],
                                              1.0 / D, None, ALU.mult), sV, 1)
                        p.wait(v, sV)
                        p.inc(v.tensor_tensor(Rb[:, r, :], Mb[:, r, :],
                                              Mb[:, r, :], ALU.mult), sV, 1)
                        p.waitb(v, sP, p_b2[c1])
                        p.wait(v, sV)
                        p.inc(v.scalar_tensor_tensor(Vb[:, r, :], psB2[r][:],
                                                     1.0 / D, Rb[:, r, :],
                                                     ALU.mult, ALU.subtract),
                              sV, 1)
                    if 0 <= c2 < NCH:
                        r = c2 % 2
                        lo = c2 * TC
                        p.waitb(v, sA, a_sqrt[c2])
                        p.inc(v.reciprocal(Rb[:, r, :], Vb[:, r, :]), sV, 1)
                        p.wait(v, sV)
                        if c2 >= 2:
                            p.waitb(v, sOut, 16 * (c2 - 1))
                        for k in range(KT):
                            p.inc(v.tensor_tensor(Vb[:, r, :],
                                                  xrb[:, c2 % 4, k, :],
                                                  Mb[:, r, :], ALU.subtract),
                                  sV, 1)
                            p.wait(v, sV)
                            dsl = xhb[:, c2 % 2, k, :]
                            if gnw is not None:
                                p.inc(v.tensor_tensor(Vb[:, r, :], Vb[:, r, :],
                                                      Rb[:, r, :], ALU.mult),
                                      sV, 1)
                                p.wait(v, sV)
                                p.inc(v.tensor_scalar(dsl, Vb[:, r, :],
                                                      gnw[:, k:k + 1],
                                                      gnb[:, k:k + 1],
                                                      ALU.mult, ALU.add),
                                      sV, 1)
                            else:
                                p.inc(v.tensor_tensor(dsl, Vb[:, r, :],
                                                      Rb[:, r, :], ALU.mult),
                                      sV, 1)
                            p.wait(v, sV)
                            if mul_dram is not None:
                                if k == 0:
                                    p.waitb(v, f"lnM{c2 % 4}~g",
                                            16 * (c2 // 4 + 1))
                                p.inc(v.tensor_tensor(dsl, dsl,
                                                      mrb[:, c2 % 4, k, :],
                                                      ALU.mult), sV, 1)
                                p.wait(v, sV)
                        p.inc(v.engine_nop(), sV, 1)

            @blk.sync
            def _(s):
                for t2 in range(NCH + 2):
                    c2 = t2 - 2
                    if 0 <= c2 < NCH:
                        lo = c2 * TC
                        p.waitb(s, sV, v_end[c2])
                        p.inc(s.dma_start(
                            XHD[:, :, lo + dst_off:lo + TC + dst_off],
                            xhb[:, c2 % 2, :, :]), sOut, 16)
        with nc.Block() as blk:
            @blk.sync
            def _(s):
                p.wait(s, sOut)
        if mul_dram is not None:
            mctx.__exit__(None, None, None)



class ProjPlan:

    def __init__(self, p, name, nk, nm, nn, psums, lhsT_fn, rhs_fn):
        self.p, self.name, self.nk, self.nm, self.nn = p, name, nk, nm, nn
        self.psums = psums
        self.lhsT_fn, self.rhs_fn = lhsT_fn, rhs_fn
        self.sP, self.sE = name + "P", name + "E"

    def tensor_section(self, t, pre_wait=None):
        p = self.p
        nb = len(self.psums)
        idx = 0
        for m in range(self.nm):
            for n in range(self.nn):
                if pre_wait is not None:
                    pre_wait(t, m, n)
                if idx >= nb:
                    p.wait(t, self.sE, val=idx - nb + 1)
                ps = self.psums[idx % nb]
                for k in range(self.nk):
                    ins = t.matmul(ps, self.lhsT_fn(k, m), self.rhs_fn(k, n),
                                   start=(k == 0), stop=(k == self.nk - 1))
                p.inc(ins, self.sP, 1)
                idx += 1

    def evac_section(self, e, evac_fn):
        p = self.p
        nb = len(self.psums)
        idx = 0
        for m in range(self.nm):
            for n in range(self.nn):
                p.wait(e, self.sP, val=idx + 1)
                evac_fn(e, m, n, self.psums[idx % nb])
                idx += 1



def emit_l0_ffn(p, W, scale=0.5):
    nc = p.nc
    u = p.uid()
    C = 256
    nch = T // C
    with nc.sbuf_tensor(f"f0xh{u}", [128, 2, KT, C + 1], BF16) as xh, \
         nc.sbuf_tensor(f"f0xr2{u}", [128, 2, KT, C], F32) as xrb, \
         nc.sbuf_tensor(f"f0xr{u}", [128, KT, C], BF16) as xr, \
         nc.sbuf_tensor(f"f0xk{u}", [128, KT, C], BF16) as xk, \
         nc.sbuf_tensor(f"f0r{u}", [128, KT, C], BF16) as rr, \
         nc.sbuf_tensor(f"f0ka{u}", [128, KT, C], BF16) as ka, \
         nc.sbuf_tensor(f"f0t{u}", [128, C], F32) as tmpf, \
         nc.psum_tensor(f"f0p0{u}", [128, C], F32) as ps0, \
         nc.psum_tensor(f"f0p1{u}", [128, C], F32) as ps1, \
         nc.psum_tensor(f"f0p2{u}", [128, C], F32) as ps2, \
         nc.psum_tensor(f"f0p3{u}", [128, C], F32) as ps3:
        psl = [ps0, ps1, ps2, ps3]
        sLd, sMix, sPrk, sErk, sPv, sEv = ("f0Ld~s", "f0M", "f0P", "f0E",
                                           "f0VP", "f0VE")
        with nc.Block() as blk:
            @blk.sync
            def _(s):
                p.wait(s, "!initd")
                p.inc(s.dma_start(xh[:, 0, :, :], p.XHD[:, :, 0:C + 1]),
                      sLd + "0", 16)

            @blk.gpsimd
            def _(g):
                p.inc(g.dma_start(xrb[:, 0, :, :], p.XRD[:, :, 0:C]),
                      "f0X0~g", 16)
        f0o_uses = [0, 0]
        for n in range(nch):
            lo = n * C
            with nc.Block() as blk:
                bM = p.mark(sMix)

                @blk.sync
                def _(s):
                    if n + 1 < nch:
                        p.inc(s.dma_start(xh[:, (n + 1) % 2, :, :],
                                          p.XHD[:, :, lo + C:lo + 2 * C + 1]),
                              sLd + str((n + 1) % 2), 16)

                @blk.gpsimd
                def _(g):
                    if n + 1 < nch:
                        p.waitb(g, "f0O" + str((n + 1) % 2) + "~s",
                                16 * KT * f0o_uses[(n + 1) % 2])
                        p.inc(g.dma_start(xrb[:, (n + 1) % 2, :, :],
                                          p.XRD[:, :, lo + C:lo + 2 * C]),
                              "f0X" + str((n + 1) % 2) + "~g", 16)

                @blk.vector
                def _(v):
                    b = n % 2
                    p.waitb(v, sLd + str(n % 2), 16 * (n // 2 + 1))
                    for k in range(KT):
                        p.wait(v, sMix)
                        ins = v.tensor_tensor(tmpf[:], xh[:, b, k, 1:C + 1],
                                              xh[:, b, k, 0:C], ALU.subtract)
                        p.inc(ins, sMix, 1)
                        p.wait(v, sMix)
                        ins = v.scalar_tensor_tensor(xr[:, k, :], tmpf[:],
                                                     W["mu_r"][:, k:k + 1],
                                                     xh[:, b, k, 0:C],
                                                     ALU.mult, ALU.add)
                        p.inc(ins, sMix, 1)
                        ins = v.scalar_tensor_tensor(xk[:, k, :], tmpf[:],
                                                     W["mu_k"][:, k:k + 1],
                                                     xh[:, b, k, 0:C],
                                                     ALU.mult, ALU.add)
                        p.inc(ins, sMix, 1)

            with nc.Block() as blk:
                bP = p.mark(sPrk); bE = p.mark(sErk)

                @blk.tensor
                def _(t):
                    idx = 0
                    for m in range(KT):
                        for which in range(2):
                            w = W["wrT"] if which == 0 else W["wkT"]
                            srcb = xr if which == 0 else xk
                            if idx >= 4:
                                p.wait(t, sErk, val=bE + idx - 3)
                            ps = psl[idx % 4]
                            for k in range(KT):
                                ins = t.matmul(ps[:], w[:, k, m * 128:(m + 1) * 128],
                                               srcb[:, k, :], start=(k == 0),
                                               stop=(k == KT - 1))
                            p.inc(ins, sPrk, 1)
                            idx += 1

                @blk.scalar
                def _(a):
                    idx = 0
                    for m in range(KT):
                        for which in range(2):
                            p.wait(a, sPrk, val=bP + idx + 1)
                            ps = psl[idx % 4]
                            if which == 0:
                                p.inc(a.activation(rr[:, m, :], ps[:], AF.Sigmoid),
                                      sErk, 1)
                            else:
                                p.inc(a.activation(ka[:, m, :], ps[:], AF.Relu),
                                      sErk, 1)
                                p.wait(a, sErk)
                                a.activation(ka[:, m, :], ka[:, m, :], AF.Square)
                            idx += 1

            with nc.Block() as blk:
                bPv = p.mark(sPv); bEv = p.mark(sEv)

                @blk.tensor
                def _(t):
                    for m in range(KT):
                        if m >= 4:
                            p.wait(t, sEv, val=bEv + 2 * (m - 4) + 1)
                        ps = psl[m % 4]
                        for k in range(KT):
                            ins = t.matmul(ps[:],
                                           W["wvT"][:, k, m * 128:(m + 1) * 128],
                                           ka[:, k, :], start=(k == 0),
                                           stop=(k == KT - 1))
                        p.inc(ins, sPv, 1)

                @blk.vector
                def _(v):
                    p.waitb(v, "f0X" + str(n % 2) + "~g", 16 * (n // 2 + 1))
                    for m in range(KT):
                        p.wait(v, sPv, val=bPv + m + 1)
                        p.wait(v, sEv)
                        ps = psl[m % 4]
                        ins = v.scalar_tensor_tensor(tmpf[:], ps[:], scale,
                                                     rr[:, m, :], ALU.mult,
                                                     ALU.mult)
                        p.inc(ins, sEv, 1)
                        p.wait(v, sEv)
                        ins = v.tensor_tensor(xrb[:, n % 2, m, :],
                                              xrb[:, n % 2, m, :],
                                              tmpf[:], ALU.add)
                        p.inc(ins, sEv, 1)

                @blk.sync
                def _(s):
                    for m in range(KT):
                        p.wait(s, sEv, val=bEv + 2 * (m + 1))
                        p.inc(s.dma_start(p.XRD[:, m, lo:lo + C],
                                          xrb[:, n % 2, m, :]),
                              "f0O" + str(n % 2) + "~s", 16)
                f0o_uses[n % 2] += 1
        with nc.Block() as blk:
            @blk.sync
            def _(s):
                p.wait(s, "f0O0~s")
                p.wait(s, "f0O1~s")



def emit_proj_residual(p, src_dram, src_off, wT, bias=None, scale=1.0,
                       nm=KT, dst=None):
    nc = p.nc
    u = p.uid()
    dstd = dst if dst is not None else p.XRD
    with nc.sbuf_tensor(f"pjxh{u}", [128, 2, KT, TC], BF16) as xh, \
         nc.sbuf_tensor(f"pjxr{u}", [128, 2, KT, TC], F32) as xrb, \
         nc.sbuf_tensor(f"pjt{u}", [128, TC], F32) as tmpf, \
         nc.psum_tensor(f"pjp0{u}", [128, TC], F32) as ps0, \
         nc.psum_tensor(f"pjp1{u}", [128, TC], F32) as ps1, \
         nc.psum_tensor(f"pjp2{u}", [128, TC], F32) as ps2, \
         nc.psum_tensor(f"pjp3{u}", [128, TC], F32) as ps3:
        psl = [ps0, ps1, ps2, ps3]
        sLd, sP, sE = "pjLd~s", "pjP", "pjE"
        with nc.Block() as blk:
            @blk.sync
            def _(s):
                p.inc(s.dma_start(xh[:, 0, :, :],
                                  src_dram[:, :, src_off:src_off + TC]),
                      sLd + "0", 16)

            @blk.gpsimd
            def _(g):
                p.inc(g.dma_start(xrb[:, 0, :, :], p.XRD[:, :, 0:TC]
                                  if dst is None else dstd[:, :, 0:TC]),
                      "pjX0~g", 16)
        pjo_uses = [0, 0]
        for n in range(NCH):
            lo = n * TC
            with nc.Block() as blk:
                bP = p.mark(sP); bE = p.mark(sE)

                @blk.sync
                def _(s):
                    if n + 1 < NCH:
                        p.inc(s.dma_start(
                            xh[:, (n + 1) % 2, :, :],
                            src_dram[:, :, src_off + lo + TC:src_off + lo + 2 * TC]),
                            sLd + str((n + 1) % 2), 16)
                    for m in range(nm):
                        p.wait(s, sE, val=bE + 2 * (m + 1))
                        p.inc(s.dma_start(dstd[:, m, lo:lo + TC],
                                          xrb[:, n % 2, m, :]),
                              "pjO" + str(n % 2) + "~s", 16)
                pjo_uses[n % 2] += 1

                @blk.gpsimd
                def _(g):
                    if n + 1 < NCH:
                        p.waitb(g, "pjO" + str((n + 1) % 2) + "~s",
                                16 * nm * pjo_uses[(n + 1) % 2])
                        p.inc(g.dma_start(xrb[:, (n + 1) % 2, :, :],
                                          dstd[:, :, lo + TC:lo + 2 * TC]),
                              "pjX" + str((n + 1) % 2) + "~g", 16)

                @blk.tensor
                def _(t):
                    p.waitb(t, sLd + str(n % 2), 16 * (n // 2 + 1))
                    for m in range(nm):
                        if m >= 4:
                            p.wait(t, sE, val=bE + 2 * (m - 4) + 1)
                        ps = psl[m % 4]
                        for k in range(KT):
                            ins = t.matmul(ps[:],
                                           wT[:, k, m * 128:(m + 1) * 128],
                                           xh[:, n % 2, k, :], start=(k == 0),
                                           stop=(k == KT - 1))
                        p.inc(ins, sP, 1)

                @blk.vector
                def _(v):
                    p.waitb(v, "pjX" + str(n % 2) + "~g", 16 * (n // 2 + 1))
                    for m in range(nm):
                        p.wait(v, sP, val=bP + m + 1)
                        p.wait(v, sE)
                        ps = psl[m % 4]
                        if bias is not None:
                            ins = v.tensor_scalar(tmpf[:], ps[:],
                                                  bias[:, m:m + 1], None, ALU.add)
                        else:
                            ins = v.tensor_copy(tmpf[:], ps[:])
                        p.inc(ins, sE, 1)
                        p.wait(v, sE)
                        ins = v.scalar_tensor_tensor(xrb[:, n % 2, m, :],
                                                     tmpf[:], scale,
                                                     xrb[:, n % 2, m, :],
                                                     ALU.mult, ALU.add)
                        p.inc(ins, sE, 1)



def emit_proj_to_staging(p, w_bf_dram, wcols, jobs, src_dram=None, src_off=1):
    nc = p.nc
    u = p.uid()
    srcd = src_dram if src_dram is not None else p.XHD
    NJ = len(jobs)
    with nc.sbuf_tensor(f"paw{u}", [128, KT, wcols], BF16) as wsb, \
         nc.sbuf_tensor(f"paxh{u}", [128, 2, KT, TC], BF16) as xh, \
         nc.sbuf_tensor(f"past{u}", [128, 4, TC], BF16) as stg, \
         nc.psum_tensor(f"pap0{u}", [128, TC], F32) as ps0, \
         nc.psum_tensor(f"pap1{u}", [128, TC], F32) as ps1, \
         nc.psum_tensor(f"pap2{u}", [128, TC], F32) as ps2, \
         nc.psum_tensor(f"pap3{u}", [128, TC], F32) as ps3:
        psl = [ps0, ps1, ps2, ps3]
        sP, sE = "paP", "paE"
        with nc.Block() as blk:
            @blk.sync
            def _(s):
                p.inc(s.dma_start(wsb[:],
                                  w_bf_dram.rearrange("(k p) m -> p k m", p=128)),
                      "paW~s", 16)
                p.inc(s.dma_start(xh[:, 0, :, :],
                                  srcd[:, :, src_off:src_off + TC]), "paLd0~s", 16)
        for n in range(NCH):
            lo = n * TC
            with nc.Block() as blk:
                bP = p.mark(sP); bE = p.mark(sE)

                @blk.sync
                def _(s):
                    if n + 1 < NCH:
                        p.inc(s.dma_start(
                            xh[:, (n + 1) % 2, :, :],
                            srcd[:, :, src_off + lo + TC:src_off + lo + 2 * TC]),
                            "paLd" + str((n + 1) % 2) + "~s", 16)
                    for idx, job in enumerate(jobs):
                        p.wait(s, sE, val=bE + idx + 1)
                        j = idx % 4
                        if job[0] == "cm":
                            _, _wm, dstd, dm = job
                            dap = dstd[:, dm, lo:lo + TC]
                        else:
                            _, _wo, dstd, tt, cc = job
                            dap = dstd[lo + tt * 128:lo + (tt + 1) * 128,
                                       cc * 512:(cc + 1) * 512]
                        p.inc(s.dma_start(dap, stg[:, j, :]), f"paD{j}~s", 16)

                @blk.tensor
                def _(t):
                    if n == 0:
                        p.wait(t, "paW~s")
                    p.waitb(t, "paLd" + str(n % 2) + "~s", 16 * (n // 2 + 1))
                    for idx, job in enumerate(jobs):
                        if idx >= 4:
                            p.wait(t, sE, val=bE + idx - 3)
                        ps = psl[idx % 4]
                        if job[0] == "cm":
                            wm = job[1]
                            for k in range(KT):
                                ins = t.matmul(
                                    ps[:], wsb[:, k, wm * 128:(wm + 1) * 128],
                                    xh[:, n % 2, k, :], start=(k == 0),
                                    stop=(k == KT - 1))
                        else:
                            wo, tt, cc = job[1], job[3], job[4]
                            for k in range(KT):
                                ins = t.matmul(
                                    ps[:], xh[:, n % 2, k, tt * 128:(tt + 1) * 128],
                                    wsb[:, k, wo + cc * 512:wo + (cc + 1) * 512],
                                    start=(k == 0), stop=(k == KT - 1))
                        p.inc(ins, sP, 1)

                @blk.vector
                def _(v):
                    nuse = (NJ + 3) // 4
                    for idx in range(NJ):
                        j = idx % 4
                        p.wait(v, sP, val=bP + idx + 1)
                        p.waitb(v, f"paD{j}~s", 16 * (nuse * n + idx // 4))
                        p.inc(v.tensor_copy(stg[:, j, :], psl[j][:]), sE, 1)

        with nc.Block() as blk:
            @blk.sync
            def _(s):
                for j in range(4):
                    if f"paD{j}~s" in p.counts:
                        p.wait(s, f"paD{j}~s")
        p.phase_reset()



def emit_rope_rotate(p, wswT_bf, dstd, rope2_sb):
    nc = p.nc
    u = p.uid()
    with nc.sbuf_tensor(f"rrw{u}", [128, KT, D], BF16) as wsb, \
         nc.sbuf_tensor(f"rrxh{u}", [128, 2, KT, TC], BF16) as xh, \
         nc.sbuf_tensor(f"rrqt{u}", [128, KT, TC], BF16) as qt, \
         nc.sbuf_tensor(f"rrt1{u}", [128, TC], F32) as t1, \
         nc.sbuf_tensor(f"rrt2{u}", [128, TC], F32) as t2, \
         nc.sbuf_tensor(f"rrst{u}", [128, 4, TC], BF16) as stg, \
         nc.psum_tensor(f"rrp0{u}", [128, TC], F32) as ps0, \
         nc.psum_tensor(f"rrp1{u}", [128, TC], F32) as ps1, \
         nc.psum_tensor(f"rrp2{u}", [128, TC], F32) as ps2, \
         nc.psum_tensor(f"rrp3{u}", [128, TC], F32) as ps3:
        psl = [ps0, ps1, ps2, ps3]
        sP, sE = "rrP", "rrE"
        with nc.Block() as blk:
            @blk.sync
            def _(s):
                p.inc(s.dma_start(wsb[:],
                                  wswT_bf.rearrange("(k p) m -> p k m", p=128)),
                      "rrW~s", 16)
                p.inc(s.dma_start(xh[:, 0, :, :], p.XHD[:, :, 1:1 + TC]),
                      "rrLd0~s", 16)
        for n in range(NCH):
            lo = n * TC
            with nc.Block() as blk:
                bP = p.mark(sP); bE = p.mark(sE)

                @blk.sync
                def _(s):
                    p.inc(s.dma_start(qt[:], dstd[:, :, lo:lo + TC]), "rrQ~s", 16)
                    if n + 1 < NCH:
                        p.inc(s.dma_start(
                            xh[:, (n + 1) % 2, :, :],
                            p.XHD[:, :, 1 + lo + TC:1 + lo + 2 * TC]),
                            "rrLd" + str((n + 1) % 2) + "~s", 16)
                    for m in range(KT):
                        p.wait(s, sE, val=bE + 3 * m + 3)
                        p.inc(s.dma_start(dstd[:, m, lo:lo + TC],
                                          stg[:, m % 4, :]), f"rrD{m % 4}~s", 16)

                @blk.tensor
                def _(t):
                    if n == 0:
                        p.wait(t, "rrW~s")
                    p.waitb(t, "rrLd" + str(n % 2) + "~s", 16 * (n // 2 + 1))
                    for m in range(KT):
                        if m >= 4:
                            p.wait(t, sE, val=bE + 3 * (m - 4) + 1)
                        ps = psl[m % 4]
                        for k in range(KT):
                            ins = t.matmul(ps[:], wsb[:, k, m * 128:(m + 1) * 128],
                                           xh[:, n % 2, k, :], start=(k == 0),
                                           stop=(k == KT - 1))
                        p.inc(ins, sP, 1)

                @blk.vector
                def _(v):
                    p.waitb(v, "rrQ~s", 16 * (n + 1))
                    for m in range(KT):
                        p.wait(v, sP, val=bP + m + 1)
                        p.wait(v, sE)
                        ins = v.tensor_tensor(t1[:], psl[m % 4][:],
                                              rope2_sb[:, 1, lo:lo + TC],
                                              ALU.mult)
                        p.inc(ins, sE, 1)
                        ins = v.tensor_tensor(t2[:], qt[:, m, :],
                                              rope2_sb[:, 0, lo:lo + TC],
                                              ALU.mult)
                        p.inc(ins, sE, 1)
                        p.wait(v, sE)
                        p.waitb(v, f"rrD{m % 4}~s", 16 * (2 * n + m // 4))
                        p.inc(v.tensor_tensor(stg[:, m % 4, :], t1[:], t2[:],
                                              ALU.add), sE, 1)

        with nc.Block() as blk:
            @blk.sync
            def _(s):
                for j in range(4):
                    p.wait(s, f"rrD{j}~s")
        p.phase_reset()



def emit_proj_to_staging(p, w_bf_dram, wcols, jobs, src_dram=None, src_off=1):
    nc = p.nc
    u = p.uid()
    srcd = src_dram if src_dram is not None else p.XHD
    NJ = len(jobs)
    with nc.sbuf_tensor(f"paw{u}", [128, KT, wcols], BF16) as wsb, \
         nc.sbuf_tensor(f"paxh{u}", [128, 2, KT, TC], BF16) as xh, \
         nc.sbuf_tensor(f"past{u}", [128, 4, TC], BF16) as stg, \
         nc.psum_tensor(f"pap0{u}", [128, TC], F32) as ps0, \
         nc.psum_tensor(f"pap1{u}", [128, TC], F32) as ps1, \
         nc.psum_tensor(f"pap2{u}", [128, TC], F32) as ps2, \
         nc.psum_tensor(f"pap3{u}", [128, TC], F32) as ps3:
        psl = [ps0, ps1, ps2, ps3]
        sP, sE = "paP", "paE"
        with nc.Block() as blk:
            @blk.sync
            def _(s):
                p.inc(s.dma_start(wsb[:],
                                  w_bf_dram.rearrange("(k p) m -> p k m", p=128)),
                      "paW~s", 16)
                p.inc(s.dma_start(xh[:, 0, :, :],
                                  srcd[:, :, src_off:src_off + TC]), "paLd0~s", 16)
        for n in range(NCH):
            lo = n * TC
            with nc.Block() as blk:
                bP = p.mark(sP); bE = p.mark(sE)

                @blk.sync
                def _(s):
                    if n + 1 < NCH:
                        p.inc(s.dma_start(
                            xh[:, (n + 1) % 2, :, :],
                            srcd[:, :, src_off + lo + TC:src_off + lo + 2 * TC]),
                            "paLd" + str((n + 1) % 2) + "~s", 16)
                    for idx, job in enumerate(jobs):
                        p.wait(s, sE, val=bE + idx + 1)
                        j = idx % 4
                        if job[0] == "cm":
                            dap = job[2][:, job[3], lo:lo + TC]
                        else:
                            tt, cc = job[3], job[4]
                            dap = job[2][lo + tt * 128:lo + (tt + 1) * 128,
                                         cc * 512:(cc + 1) * 512]
                        p.inc(s.dma_start(dap, stg[:, j, :]), f"paD{j}~s", 16)

                @blk.tensor
                def _(t):
                    if n == 0:
                        p.wait(t, "paW~s")
                    p.waitb(t, "paLd" + str(n % 2) + "~s", 16 * (n // 2 + 1))
                    for idx, job in enumerate(jobs):
                        if idx >= 4:
                            p.wait(t, sE, val=bE + idx - 3)
                        ps = psl[idx % 4]
                        if job[0] == "cm":
                            wm = job[1]
                            for k in range(KT):
                                ins = t.matmul(
                                    ps[:], wsb[:, k, wm * 128:(wm + 1) * 128],
                                    xh[:, n % 2, k, :], start=(k == 0),
                                    stop=(k == KT - 1))
                        else:
                            wo, tt, cc = job[1], job[3], job[4]
                            for k in range(KT):
                                ins = t.matmul(
                                    ps[:], xh[:, n % 2, k, tt * 128:(tt + 1) * 128],
                                    wsb[:, k, wo + cc * 512:wo + (cc + 1) * 512],
                                    start=(k == 0), stop=(k == KT - 1))
                        p.inc(ins, sP, 1)

                @blk.vector
                def _(v):
                    nuse = (NJ + 3) // 4
                    for idx in range(NJ):
                        j = idx % 4
                        p.wait(v, sP, val=bP + idx + 1)
                        prior = nuse * n + idx // 4
                        if prior > 0:
                            p.waitb(v, f"paD{j}~s", 16 * prior)
                        p.inc(v.tensor_copy(stg[:, j, :], psl[j][:]), sE, 1)

        with nc.Block() as blk:
            @blk.sync
            def _(s):
                for j in range(4):
                    if f"paD{j}~s" in p.counts:
                        p.wait(s, f"paD{j}~s")
        p.phase_reset()



def emit_mha(p, qkvT_bf_dram, w_outT_bf_dram, b_out=None, causal=False,
             rope2_dram=None, qkswT_bf_dram=None, scale_qk=0.125):
    nc = p.nc
    u = p.uid()
    H, HD = 16, 64
    qTd = p.dram(f"mhq{u}", [128, 8, T], BF16)
    kTd = p.dram(f"mhk{u}", [128, 8, T], BF16)
    vd = p.dram(f"mhv{u}", [T, D], BF16)
    oad = p.dram(f"mho{u}", [128, 8, T], BF16)
    rope = rope2_dram is not None

    jobs = [("cm", m, qTd, m) for m in range(8)]
    jobs += [("cm", 8 + m, kTd, m) for m in range(8)]
    jobs += [("tm", 2048, vd, tt, cc) for tt in range(4) for cc in range(2)]
    emit_proj_to_staging(p, qkvT_bf_dram, 3072, jobs)
    if rope:
        with nc.sbuf_tensor(f"rrtb{u}", [128, 2, T], BF16) as rope2_sb:
            with nc.Block() as blk:
                @blk.sync
                def _(s):
                    p.inc(s.dma_start(rope2_sb[:], rope2_dram.rearrange(
                        "c p t -> p c t")), "rrT~s", 16)
                    p.wait(s, "rrT~s")
            emit_rope_rotate(p, qkswT_bf_dram[:, 0:D], qTd, rope2_sb)
            emit_rope_rotate(p, qkswT_bf_dram[:, D:2 * D], kTd, rope2_sb)

    nkt = T // 128
    with ExitStack() as bctx:
        qb = bctx.enter_context(nc.sbuf_tensor(f"mbq{u}", [128, 2, T], BF16))
        kb = bctx.enter_context(nc.sbuf_tensor(f"mbk{u}", [128, 2, T], BF16))
        vh = bctx.enter_context(nc.sbuf_tensor(f"mbv{u}",
                                               [128, 2, nkt, HD + 1], BF16))
        E = bctx.enter_context(nc.sbuf_tensor(f"mbe{u}", [128, 2, nkt, TC],
                                              BF16))
        rd = bctx.enter_context(nc.sbuf_tensor(f"mbrd{u}", [1, 2, TC], F32))
        Rb = bctx.enter_context(nc.sbuf_tensor(f"mbrb{u}", [64, 2, TC], F32))
        osb = bctx.enter_context(nc.sbuf_tensor(f"mbo{u}", [64, 2, TC], BF16))
        if causal:
            cmask = bctx.enter_context(nc.sbuf_tensor(f"mbcm{u}", [128, 4, TC],
                                                      BF16))
            iota512 = bctx.enter_context(nc.sbuf_tensor(f"mbio{u}", [128, TC],
                                                        F32))
            ccol = bctx.enter_context(nc.sbuf_tensor(f"mbcc{u}", [128, 1], F32))
        psS0 = bctx.enter_context(nc.psum_tensor(f"mbs0{u}", [128, TC], F32))
        psS1 = bctx.enter_context(nc.psum_tensor(f"mbs1{u}", [128, TC], F32))
        psS2 = bctx.enter_context(nc.psum_tensor(f"mbs2{u}", [128, TC], F32))
        psS3 = bctx.enter_context(nc.psum_tensor(f"mbs3{u}", [128, TC], F32))
        psO = [bctx.enter_context(nc.psum_tensor(f"mbso{i}{u}", [65, TC],
                                                 F32)) for i in range(2)]
        psB = bctx.enter_context(nc.psum_tensor(f"mbsb{u}", [64, TC], F32))
        psSl = [psS0, psS1, psS2, psS3]

        with nc.Block() as blk:
            @blk.gpsimd
            def _(g):
                p.inc(g.memset(vh[:, 0, :, HD:HD + 1], 1.0), "mbI", 1)
                p.inc(g.memset(vh[:, 1, :, HD:HD + 1], 1.0), "mbI", 1)
                p.inc(g.memset(qb.ap()[64:128, :, :], 0.0), "mbI", 1)
                p.inc(g.memset(kb.ap()[64:128, :, :], 0.0), "mbI", 1)
                if causal:
                    p.inc(g.iota(iota512[:], [[1, TC]], channel_multiplier=0,
                                 allow_small_or_imprecise_dtypes=True), "mbI", 1)
            if causal:
                @blk.vector
                def _(v):
                    p.wait(v, "mbI", val=2)
                    for j in range(4):
                        p.inc(v.tensor_scalar_add(ccol[:], p.iota_p_g[:],
                                                  float(128 * j)), "mbI", 1)
                        p.wait(v, "mbI")
                        p.inc(v.tensor_scalar(cmask[:, j, :], iota512[:],
                                              ccol[:, 0:1], None, ALU.is_ge),
                              "mbI", 1)
                        p.wait(v, "mbI")

        sRdy, sAV, sRec, sBc, sRb, sEvO = ("mbRdy", "mbAV", "mbRec", "mbBc",
                                           "mbRb", "mbEv")
        iters = [(h, qc) for h in range(H) for qc in range(NCH)]
        NI = len(iters)
        ktrs = [list(range(min(nkt, 4 * (qc + 1)))) if causal
                else list(range(nkt)) for (h, qc) in iters]
        nks = [len(kk) for kk in ktrs]
        base_Sc = [0] * (NI + 1)
        for s2 in range(NI):
            base_Sc[s2 + 1] = base_Sc[s2] + nks[s2]
        mask_cnt = {}
        exp_waits = [[] for _ in range(NI)]
        last_E = {}
        mcnt = 0
        for s2, (h2, qc2) in enumerate(iters):
            for j2, kt2 in enumerate(ktrs[s2]):
                exp_waits[s2].append(list(last_E.get((kt2, s2 % 2), [])))
                acc = [(sAV, base_Sc[s2] + j2 + 1)]
                if causal and kt2 >= 4 * qc2:
                    mcnt += 1
                    mask_cnt[(s2, kt2)] = mcnt
                    acc.append(("mbMk2", mcnt))
                last_E[(kt2, s2 % 2)] = acc
        sc_end = [base_Sc[min((h2 + 1) * NCH, NI)] for h2 in range(H)]
        od_uses = [0, 0]

        with nc.Block() as blk:
            @blk.tensor
            def _(t):
                for sl in range(NI + 2):
                    si = sl if sl < NI else None
                    ai = sl - 1 if 0 <= sl - 1 < NI else None
                    bi = sl - 2 if 0 <= sl - 2 < NI else None
                    if si is not None and iters[si][1] == 0:
                        h2 = iters[si][0]
                        p.waitb(t, f"mbQL{h2 % 2}~s", 16 * (h2 // 2 + 1))
                        p.waitb(t, f"mbKL{h2 % 2}~s", 16 * (h2 // 2 + 1))
                    if ai is not None and iters[ai][1] == 0:
                        h2 = iters[ai][0]
                        p.waitb(t, f"mbVL{h2 % 2}~s", 16 * (h2 // 2 + 1))
                    n_s = nks[si] if si is not None else 0
                    n_a = nks[ai] if ai is not None else 0
                    for j in range(max(n_s, n_a)):
                        if ai is not None and j < n_a:
                            kt = ktrs[ai][j]
                            p.waitb(t, sRdy, base_Sc[ai] + j + 1)
                            if causal and (ai, kt) in mask_cnt:
                                p.waitb(t, "mbMk2", mask_cnt[(ai, kt)])
                            if j == 0 and ai >= 2:
                                p.waitb(t, sEvO, ai - 1)
                            ins = t.matmul(psO[ai % 2][:],
                                           vh[:, iters[ai][0] % 2, kt, :],
                                           E[:, ai % 2, kt, :],
                                           start=(j == 0),
                                           stop=(j == n_a - 1))
                            p.inc(ins, sAV, 1)
                        if si is not None and j < n_s:
                            kt = ktrs[si][j]
                            g = base_Sc[si] + j
                            if g >= 4:
                                p.waitb(t, sRdy, g - 3)
                            h2, qc2 = iters[si]
                            ins = t.matmul(
                                psSl[g % 4][:],
                                kb[:, h2 % 2, kt * 128:(kt + 1) * 128],
                                qb[:, h2 % 2, qc2 * TC:(qc2 + 1) * TC],
                                start=True, stop=True)
                            p.inc(ins, "mbSc", 1)
                    if bi is not None:
                        p.waitb(t, sRec, bi + 1)
                        p.waitb(t, sRb, bi)
                        p.inc(t.matmul(psB[:], p.ones_row_f32[0:1, 0:64],
                                       rd[0:1, bi % 2, :],
                                       start=True, stop=True), sBc, 1)

            @blk.scalar
            def _(a):
                for sl in range(NI + 2):
                    si = sl if sl < NI else None
                    ri = sl - 2 if 0 <= sl - 2 < NI else None
                    if si is not None:
                        for j, kt in enumerate(ktrs[si]):
                            g = base_Sc[si] + j
                            p.waitb(a, "mbSc", g + 1)
                            for semn, cnt in exp_waits[si][j]:
                                p.waitb(a, semn, cnt)
                            ins = a.activation(E[:, si % 2, kt, :],
                                               psSl[g % 4][:],
                                               AF.Exp, scale=scale_qk)
                            p.inc(ins, sRdy, 1)
                    if ri is not None:
                        p.waitb(a, sBc, ri + 1)
                        if ri >= 2:
                            p.waitb(a, sEvO, ri - 1)
                        p.inc(a.activation(Rb[:, ri % 2, :], psB[:], AF.Copy),
                              sRb, 1)

            @blk.vector
            def _(v):
                for sl in range(NI + 2):
                    si = sl if sl < NI else None
                    di = sl - 1 if 0 <= sl - 1 < NI else None
                    ei = sl - 2 if 0 <= sl - 2 < NI else None
                    if causal and si is not None:
                        h2, qc2 = iters[si]
                        for j, kt in enumerate(ktrs[si]):
                            if kt >= 4 * qc2:
                                p.waitb(v, sRdy, base_Sc[si] + j + 1)
                                p.inc(v.tensor_tensor(
                                    E[:, si % 2, kt, :], E[:, si % 2, kt, :],
                                    cmask[:, kt - 4 * qc2, :], ALU.mult),
                                    "mbMk2", 1)
                    if di is not None:
                        p.waitb(v, sAV, base_Sc[di] + nks[di])
                        if di >= 2:
                            p.waitb(v, sBc, di - 1)
                        p.inc(v.tensor_copy(rd[0:1, di % 2, :],
                                            psO[di % 2][64:65, :]), sRec, 1)
                    if ei is not None:
                        p.waitb(v, sRb, ei + 1)
                        p.inc(v.reciprocal(Rb[:, ei % 2, :], Rb[:, ei % 2, :]),
                              "mbRc", 1)
                        p.wait(v, "mbRc")
                        p.waitb(v, f"mbOD{ei % 2}~s", 16 * od_uses[ei % 2])
                        od_uses[ei % 2] += 1
                        ins = v.tensor_tensor(osb[:, ei % 2, :],
                                              psO[ei % 2][0:64, :],
                                              Rb[:, ei % 2, :], ALU.mult)
                        p.inc(ins, sEvO, 1)

            @blk.sync
            def _(s):
                for sl in range(NI + 2):
                    si = sl if sl < NI else None
                    st = sl - 2 if 0 <= sl - 2 < NI else None
                    if si is not None and iters[si][1] == 0:
                        h2 = iters[si][0]
                        hh, hp = h2 // 2, h2 % 2
                        if h2 >= 2:
                            p.waitb(s, "mbSc", sc_end[h2 - 2])
                            p.waitb(s, sAV, sc_end[h2 - 2])
                        base = 64 * hp
                        p.inc(s.dma_start(qb.ap()[0:64, h2 % 2, :],
                                          qTd[base:base + 64, hh, :]),
                              f"mbQL{h2 % 2}~s", 16)
                        p.inc(s.dma_start(kb.ap()[0:64, h2 % 2, :],
                                          kTd[base:base + 64, hh, :]),
                              f"mbKL{h2 % 2}~s", 16)
                        p.inc(s.dma_start(
                            vh[:, h2 % 2, :, 0:HD],
                            vd[:, h2 * HD:(h2 + 1) * HD].rearrange(
                                "(tt q) d -> q tt d", q=128)),
                            f"mbVL{h2 % 2}~s", 16)
                    if st is not None:
                        h3, qc3 = iters[st]
                        p.waitb(s, sEvO, st + 1)
                        p.inc(s.dma_start(
                            oad[64 * (h3 % 2):64 * (h3 % 2) + 64, h3 // 2,
                                qc3 * TC:(qc3 + 1) * TC],
                            osb[:, st % 2, :]), f"mbOD{st % 2}~s", 16)
        with nc.Block() as blk:
            @blk.sync
            def _(s):
                p.wait(s, "mbOD0~s")
                p.wait(s, "mbOD1~s")
        p.phase_reset()

    with nc.sbuf_tensor(f"mcw{u}", [128, KT, D], BF16) as wout:
        with nc.Block() as blk:
            @blk.sync
            def _(s):
                p.inc(s.dma_start(wout[:],
                                  w_outT_bf_dram.rearrange("(k p) m -> p k m",
                                                           p=128)), "mcW~s", 16)
                p.wait(s, "mcW~s")
        emit_proj_residual(p, oad, 0, wout, bias=b_out, scale=1.0)
        p.phase_reset()



def emit_moe_router_l3(p, wg_sb, mrowd, scale=0.5, renorm=True):
    nc = p.nc
    u = p.uid()
    with nc.sbuf_tensor(f"r3xh{u}", [128, 2, KT, TC], BF16) as xh, \
         nc.sbuf_tensor(f"r3lg{u}", [128, 8], F32) as lg, \
         nc.sbuf_tensor(f"r3mx{u}", [128, 8], F32) as mx, \
         nc.sbuf_tensor(f"r3sc{u}", [128, 6], F32) as scr, \
         nc.sbuf_tensor(f"r3el{u}", [128, 8], F32) as el, \
         nc.sbuf_tensor(f"r3m{u}", [128, 8], F32) as mm, \
         nc.sbuf_tensor(f"r3mr{u}", [8, 2, TC], BF16) as mrow, \
         nc.psum_tensor(f"r3pr{u}", [128, 8], F32) as psR, \
         nc.psum_tensor(f"r3pt{u}", [8, 4, 128], F32) as psT:
        sR, sV, sS, sTr, sMr = "r3R", "r3V", "r3S", "r3T", "r3M~s"
        VP = 15 if renorm else 10
        with nc.Block() as blk:
            @blk.sync
            def _(s):
                p.inc(s.dma_start(xh[:, 0, :, :], p.XHD[:, :, 1:1 + TC]),
                      "r3L0~s", 16)
        for n in range(NCH):
            lo = n * TC
            with nc.Block() as blk:
                bR = p.mark(sR); bV = p.mark(sV); bS = p.mark(sS)
                bT = p.mark(sTr); bM = p.mark(sMr)

                @blk.sync
                def _(s):
                    if n + 1 < NCH:
                        p.inc(s.dma_start(
                            xh[:, (n + 1) % 2, :, :],
                            p.XHD[:, :, 1 + lo + TC:1 + lo + 2 * TC]),
                            "r3L" + str((n + 1) % 2) + "~s", 16)

                for tt in range(4):
                    @blk.tensor
                    def _(t, tt=tt):
                        if tt == 0:
                            p.waitb(t, "r3L" + str(n % 2) + "~s", 16 * (n // 2 + 1))
                        if tt > 0:
                            p.wait(t, sV, val=bV + VP * (tt - 1) + 2)
                            p.wait(t, sS, val=bS + 2 * tt)
                        for k in range(KT):
                            ins = t.matmul(psR[:], xh[:, n % 2, k,
                                                      tt * 128:(tt + 1) * 128],
                                           wg_sb[:, k, :], start=(k == 0),
                                           stop=(k == KT - 1))
                        p.inc(ins, sR, 1)

                    @blk.vector
                    def _(v, tt=tt):
                        p.wait(v, sR, val=bR + tt + 1)
                        p.inc(v.tensor_copy(lg[:], psR[:]), sV, 1)
                        p.wait(v, sV)
                        p.inc(v.max(mx[:], lg[:]), sV, 1)
                        p.wait(v, sV)
                        p.inc(v.tensor_scalar_mul(scr[:, 0:1], mx[:, 0:1], -1.0),
                              sV, 1)
                        p.wait(v, sS, val=bS + 2 * tt + 2)
                        p.inc(v.reciprocal(scr[:, 2:3], scr[:, 1:2]), sV, 1)
                        p.wait(v, sV)
                        p.inc(v.tensor_mul(scr[:, 3:4], mx[:, 1:2], scr[:, 2:3]),
                              sV, 1)
                        if renorm:
                            p.inc(v.tensor_add(scr[:, 4:5], mx[:, 0:1],
                                               mx[:, 1:2]), sV, 1)
                            p.wait(v, sV)
                            p.inc(v.tensor_mul(scr[:, 4:5], scr[:, 4:5],
                                               scr[:, 2:3]), sV, 1)
                            p.wait(v, sV)
                            p.inc(v.tensor_scalar_add(scr[:, 4:5], scr[:, 4:5],
                                                      1e-9), sV, 1)
                            p.wait(v, sV)
                            p.inc(v.reciprocal(scr[:, 5:6], scr[:, 4:5]), sV, 1)
                            p.wait(v, sV)
                            p.inc(v.tensor_scalar_mul(scr[:, 5:6], scr[:, 5:6],
                                                      scale), sV, 1)
                        p.wait(v, sV)
                        p.inc(v.tensor_scalar(mm[:], el[:], scr[:, 2:3], None,
                                              ALU.mult), sV, 1)
                        p.wait(v, sV)
                        p.inc(v.tensor_scalar(el[:], mm[:], scr[:, 3:4], None,
                                              ALU.is_ge), sV, 1)
                        p.wait(v, sV)
                        p.inc(v.tensor_mul(mm[:], mm[:], el[:]), sV, 1)
                        p.wait(v, sV)
                        if renorm:
                            p.inc(v.tensor_scalar(mm[:], mm[:], scr[:, 5:6],
                                                  None, ALU.mult), sV, 1)
                        else:
                            p.inc(v.tensor_scalar_mul(mm[:], mm[:], scale),
                                  sV, 1)

                    @blk.scalar
                    def _(a, tt=tt):
                        p.wait(a, sV, val=bV + VP * tt + 3)
                        p.wait(a, sS, val=bS + 2 * tt)
                        p.inc(a.activation(el[:], lg[:], AF.Exp,
                                           bias=scr[:, 0:1],
                                           accum_out=scr[:, 1:2]), sS, 1)
                        p.wait(a, sS)
                        p.inc(a.activation(mx[:, 0:2], mx[:, 0:2], AF.Exp,
                                           bias=scr[:, 0:1]), sS, 1)

                    @blk.tensor
                    def _(t, tt=tt):
                        p.wait(t, sV, val=bV + VP * tt + VP - 1)
                        p.inc(t.transpose(psT[:, tt, :], mm[:], p.ident_f32[:]),
                              sTr, 1)

                    @blk.vector
                    def _(v, tt=tt):
                        p.wait(v, sTr, val=bT + tt + 1)
                        p.waitb(v, sMr, max(0, 16 * (n - 1)))
                        p.inc(v.tensor_copy(mrow[:, n % 2, tt * 128:(tt + 1) * 128],
                                            psT[:, tt, :]), sV, 1)

                @blk.sync
                def _(s):
                    p.wait(s, sV)
                    p.inc(s.dma_start(mrowd[:, lo:lo + TC], mrow[:, n % 2, :]),
                          sMr, 16)
        with nc.Block() as blk:
            @blk.sync
            def _(s):
                p.wait(s, sMr)
        p.phase_reset()


def emit_moe_router_l2(p, wg_bf_dram, bg_bf_sb, mrowd, scale=0.5):
    nc = p.nc
    u = p.uid()
    with nc.sbuf_tensor(f"r2w{u}", [128, KT, D], BF16) as wg, \
         nc.sbuf_tensor(f"r2xh{u}", [128, 2, KT, TC], BF16) as xh, \
         nc.sbuf_tensor(f"r2el{u}", [128, D], BF16) as el, \
         nc.sbuf_tensor(f"r2mx{u}", [128, 8], F32) as mx, \
         nc.sbuf_tensor(f"r2mi{u}", [128, 8], U32) as mi, \
         nc.sbuf_tensor(f"r2miu{u}", [128, 1], U32) as miu, \
         nc.sbuf_tensor(f"r2mif{u}", [128, 1], F32) as mif, \
         nc.sbuf_tensor(f"r2sc{u}", [128, 4], F32) as scr, \
         nc.sbuf_tensor(f"r2m{u}", [128, 8], F32) as mm, \
         nc.sbuf_tensor(f"r2mr{u}", [8, 2, TC], BF16) as mrow, \
         nc.psum_tensor(f"r2p0{u}", [128, D], F32) as psR, \
         nc.psum_tensor(f"r2pt{u}", [8, 4, 128], F32) as psT:
        sR, sV, sS, sTr, sMr = "r2R", "r2V", "r2S", "r2T", "r2M~s"
        with nc.Block() as blk:
            @blk.sync
            def _(s):
                p.inc(s.dma_start(wg[:],
                                  wg_bf_dram.rearrange("(k p) m -> p k m", p=128)),
                      "r2W~s", 16)
                p.inc(s.dma_start(xh[:, 0, :, :], p.XHD[:, :, 1:1 + TC]),
                      "r2L0~s", 16)
        for n in range(NCH):
            lo = n * TC
            with nc.Block() as blk:
                bR = p.mark(sR); bV = p.mark(sV); bS = p.mark(sS)
                bT = p.mark(sTr); bM = p.mark(sMr)

                @blk.sync
                def _(s):
                    if n + 1 < NCH:
                        p.inc(s.dma_start(
                            xh[:, (n + 1) % 2, :, :],
                            p.XHD[:, :, 1 + lo + TC:1 + lo + 2 * TC]),
                            "r2L" + str((n + 1) % 2) + "~s", 16)

                for tt in range(4):
                    @blk.tensor
                    def _(t, tt=tt):
                        if tt == 0:
                            p.wait(t, "r2W~s")
                            p.waitb(t, "r2L" + str(n % 2) + "~s", 16 * (n // 2 + 1))
                        if tt > 0:
                            p.wait(t, sS, val=bS + tt)
                            p.wait(t, sV, val=bV + 10 * (tt - 1) + 2)
                        for cc in range(2):
                            for k in range(KT):
                                ins = t.matmul(
                                    psR[:, cc * 512:(cc + 1) * 512],
                                    xh[:, n % 2, k, tt * 128:(tt + 1) * 128],
                                    wg[:, k, cc * 512:(cc + 1) * 512],
                                    start=(k == 0), stop=False)
                            ins = t.matmul(psR[:, cc * 512:(cc + 1) * 512],
                                           p.ones_row_bf[0:1, 0:128],
                                           bg_bf_sb[0:1, cc * 512:(cc + 1) * 512],
                                           start=False, stop=True)
                        p.inc(ins, sR, 1)

                    @blk.vector
                    def _(v, tt=tt):
                        p.wait(v, sR, val=bR + tt + 1)
                        p.inc(v.max(mx[:], psR[:]), sV, 1)
                        p.wait(v, sV)
                        p.inc(v.max_index(mi[:], mx[:], psR[:]), sV, 1)
                        p.inc(v.tensor_scalar_mul(scr[:, 0:1], mx[:, 0:1], -1.0),
                              sV, 1)
                        p.wait(v, sV)
                        p.inc(v.tensor_single_scalar(miu[:], mi[:, 0:1], 7,
                                                     ALU.bitwise_and), sV, 1)
                        p.wait(v, sV)
                        p.inc(v.tensor_copy(mif[:], miu[:]), sV, 1)
                        p.wait(v, sS, val=bS + tt + 1)
                        p.inc(v.reciprocal(scr[:, 1:2], scr[:, 2:3]), sV, 1)
                        p.wait(v, sV)
                        p.inc(v.tensor_scalar_mul(scr[:, 1:2], scr[:, 1:2],
                                                  scale), sV, 1)
                        p.wait(v, sV)
                        p.inc(v.tensor_scalar(mm[:], p.iota8[:], mif[:, 0:1],
                                              None, ALU.is_equal), sV, 1)
                        p.wait(v, sV)
                        p.inc(v.tensor_scalar(mm[:], mm[:], scr[:, 1:2], None,
                                              ALU.mult), sV, 1)

                    @blk.scalar
                    def _(a, tt=tt):
                        p.wait(a, sV, val=bV + 10 * tt + 3)
                        p.wait(a, sS, val=bS + tt)
                        p.inc(a.activation(el[:], psR[:], AF.Exp,
                                           bias=scr[:, 0:1],
                                           accum_out=scr[:, 2:3]), sS, 1)

                    @blk.tensor
                    def _(t, tt=tt):
                        p.wait(t, sV, val=bV + 10 * tt + 9)
                        p.inc(t.transpose(psT[:, tt, :], mm[:], p.ident_f32[:]),
                              sTr, 1)

                    @blk.vector
                    def _(v, tt=tt):
                        p.wait(v, sTr, val=bT + tt + 1)
                        p.waitb(v, sMr, max(0, 16 * (n - 1)))
                        p.inc(v.tensor_copy(mrow[:, n % 2, tt * 128:(tt + 1) * 128],
                                            psT[:, tt, :]), sV, 1)

                @blk.sync
                def _(s):
                    p.wait(s, sV)
                    p.inc(s.dma_start(mrowd[:, lo:lo + TC], mrow[:, n % 2, :]),
                          sMr, 16)
        with nc.Block() as blk:
            @blk.sync
            def _(s):
                p.wait(s, sMr)
        p.phase_reset()


def emit_moe_experts(p, we_bf_dram, mrowd, relu=False, be_bf_dram=None):
    nc = p.nc
    u = p.uid()
    with ExitStack() as ctx:
        xh = ctx.enter_context(nc.sbuf_tensor(f"ex{u}", [128, 2, KT, TC], BF16))
        xrb = ctx.enter_context(nc.sbuf_tensor(f"exr{u}", [128, 2, KT, TC], F32))
        xt = ctx.enter_context(nc.sbuf_tensor(f"et{u}", [128, 2, KT, TC], BF16))
        we = ctx.enter_context(nc.sbuf_tensor(f"ew{u}", [128, 2, KT, D], BF16))
        msb = ctx.enter_context(nc.sbuf_tensor(f"em{u}", [128, 2, TC], BF16))
        tmpf = ctx.enter_context(nc.sbuf_tensor(f"etf{u}", [128, TC], F32))
        if be_bf_dram is not None:
            be2 = ctx.enter_context(nc.sbuf_tensor(f"eb{u}", [1, 2, D], BF16))
        pss = [ctx.enter_context(nc.psum_tensor(f"ep{m}{u}", [128, TC], F32))
               for m in range(8)]
        sFv, sFg, sP, sEv, sRl = "eFv", "eFg", "eP", "eEv", "eRl"
        wl = [0, 0]
        ml = [0, 0]
        xl = [0, 0]

        def load_we(s, e_global, sl):
            p.inc(s.dma_start(we[:, sl, :, :],
                              we_bf_dram[e_global].rearrange(
                                  "(k p) m -> p k m", p=128)), f"eW{sl}~s", 16)
            wl[sl] += 1

        def load_m(s, e_global, n_global, sl):
            p.inc(s.dma_start(
                msb[:, sl, :],
                mrowd.ap()[e_global:e_global + 1,
                           n_global * TC:(n_global + 1) * TC]
                .broadcast_to((128, TC))), f"eM{sl}~s", 16)
            ml[sl] += 1
            if be_bf_dram is not None:
                p.inc(s.dma_start(be2[:, sl, :],
                                  be_bf_dram[e_global:e_global + 1, :]),
                      f"eB{sl}~s", 16)

        with nc.Block() as blk:
            @blk.sync
            def _(s):
                p.inc(s.dma_start(xh[:, 0, :, :], p.XHD[:, :, 1:1 + TC]),
                      "eL0~s", 16)
                xl[0] += 1
                load_we(s, 0, 0)
                load_m(s, 0, 0, 0)

            @blk.gpsimd
            def _(g):
                p.inc(g.dma_start(xrb[:, 0, :, :], p.XRD[:, :, 0:TC]),
                      "eX0~g", 16)
        exo_uses = [0, 0]
        for n in range(NCH):
            lo = n * TC
            with nc.Block() as blk:
                bFv = p.mark(sFv); bFg = p.mark(sFg)
                bP = p.mark(sP); bEv = p.mark(sEv)
                bRl = p.mark(sRl)

                @blk.sync
                def _(s):
                    if n + 1 < NCH:
                        p.inc(s.dma_start(
                            xh[:, (n + 1) % 2, :, :],
                            p.XHD[:, :, 1 + lo + TC:1 + lo + 2 * TC]),
                            "eL" + str((n + 1) % 2) + "~s", 16)
                        xl[(n + 1) % 2] += 1
                    for e in range(8):
                        ee, nn2 = (e + 1, n) if e + 1 < 8 else (0, n + 1)
                        if nn2 < NCH:
                            sl = (e + 1) % 2
                            p.wait(s, sP, val=bP + 8 * e)
                            load_we(s, ee, sl)
                            p.wait(s, sFv, val=bFv + 4 * e)
                            p.wait(s, sFg, val=bFg + 4 * e)
                            load_m(s, ee, nn2, sl)
                    for m in range(8):
                        p.wait(s, sEv if not relu else sRl,
                               val=(bEv + m + 1) if not relu
                               else (bRl + 2 * m + 2))
                        p.inc(s.dma_start(p.XRD[:, m, lo:lo + TC],
                                          xrb[:, n % 2, m, :]),
                              "eO" + str(n % 2) + "~s", 16)

                @blk.vector
                def _(v):
                    for e in range(8):
                        sl = e % 2
                        g_idx = 8 * n + e
                        p.waitb(v, f"eM{sl}~s", 16 * (g_idx // 2 + 1))
                        if e == 0:
                            p.waitb(v, "eL" + str(n % 2) + "~s", 16 * (n // 2 + 1))
                        if e >= 2:
                            p.wait(v, sP, val=bP + 8 * (e - 1))
                        for k in range(4):
                            p.inc(v.tensor_tensor(xt[:, sl, k, :],
                                                  xh[:, n % 2, k, :],
                                                  msb[:, sl, :], ALU.mult),
                                  sFv, 1)

                @blk.gpsimd
                def _(g):
                    if n + 1 < NCH:
                        p.waitb(g, "eO" + str((n + 1) % 2) + "~s",
                                16 * 8 * exo_uses[(n + 1) % 2])
                        p.inc(g.dma_start(xrb[:, (n + 1) % 2, :, :],
                                          p.XRD[:, :, lo + TC:lo + 2 * TC]),
                              "eX" + str((n + 1) % 2) + "~g", 16)
                    for e in range(8):
                        sl = e % 2
                        g_idx = 8 * n + e
                        p.waitb(g, f"eM{sl}~s", 16 * (g_idx // 2 + 1))
                        if e == 0:
                            p.waitb(g, "eL" + str(n % 2) + "~s", 16 * (n // 2 + 1))
                        if e >= 2:
                            p.wait(g, sP, val=bP + 8 * (e - 1))
                        for k in range(4, KT):
                            p.inc(g.tensor_tensor(xt[:, sl, k, :],
                                                  xh[:, n % 2, k, :],
                                                  msb[:, sl, :], ALU.mult),
                                  sFg, 1)

                @blk.tensor
                def _(t):
                    for e in range(8):
                        sl = e % 2
                        g_idx = 8 * n + e
                        p.wait(t, sFv, val=bFv + 4 * (e + 1))
                        p.wait(t, sFg, val=bFg + 4 * (e + 1))
                        p.waitb(t, f"eW{sl}~s", 16 * (g_idx // 2 + 1))
                        if e == 0:
                            if relu:
                                p.wait(t, sRl, val=bRl)
                            else:
                                p.wait(t, sEv, val=bEv)
                        for m in range(8):
                            for k in range(KT):
                                last = (e == 7 and k == KT - 1
                                        and be_bf_dram is None)
                                ins = t.matmul(
                                    pss[m][:],
                                    we[:, sl, k, m * 128:(m + 1) * 128],
                                    xt[:, sl, k, :],
                                    start=(e == 0 and k == 0), stop=last)
                            if be_bf_dram is not None:
                                if m == 0:
                                    p.waitb(t, f"eB{sl}~s", 16 * (g_idx // 2 + 1))
                                ins = t.matmul(
                                    pss[m][:],
                                    be2[0:1, sl, m * 128:(m + 1) * 128],
                                    msb[0:1, sl, :],
                                    start=False, stop=(e == 7))
                        p.inc(ins, sP, 8)

                if relu:
                    @blk.scalar
                    def _(a):
                        for m in range(8):
                            p.wait(a, sP, val=bP + 64)
                            if m > 0:
                                p.wait(a, sRl, val=bRl + 2 * m)
                            p.inc(a.activation(tmpf[:], pss[m][:], AF.Relu),
                                  sRl, 1)

                    @blk.vector
                    def _(v):
                        p.waitb(v, "eX" + str(n % 2) + "~g", 16 * (n // 2 + 1))
                        for m in range(8):
                            p.wait(v, sRl, val=bRl + 2 * m + 1)
                            p.inc(v.tensor_tensor(xrb[:, n % 2, m, :],
                                                  xrb[:, n % 2, m, :],
                                                  tmpf[:], ALU.add), sRl, 1)
                else:
                    @blk.vector
                    def _(v):
                        p.waitb(v, "eX" + str(n % 2) + "~g", 16 * (n // 2 + 1))
                        for m in range(8):
                            p.wait(v, sP, val=bP + 64)
                            p.inc(v.tensor_tensor(xrb[:, n % 2, m, :],
                                                  xrb[:, n % 2, m, :],
                                                  pss[m][:], ALU.add), sEv, 1)
                exo_uses[n % 2] += 1
        with nc.Block() as blk:
            @blk.sync
            def _(s):
                p.wait(s, "eO0~s")
                p.wait(s, "eO1~s")
        p.phase_reset()



def emit_l1_ffn(p, w1_bf, w2_bf, mrowd):
    nc = p.nc
    u = p.uid()
    FT = 11
    with ExitStack() as ctx:
        xh = ctx.enter_context(nc.sbuf_tensor(f"l1x{u}", [128, 2, KT, TC], BF16))
        xrb = ctx.enter_context(nc.sbuf_tensor(f"l1r{u}", [128, KT, TC], F32))
        w1 = ctx.enter_context(nc.sbuf_tensor(f"l1w1{u}", [128, 2, KT, 2816],
                                              BF16))
        w2 = ctx.enter_context(nc.sbuf_tensor(f"l1w2{u}", [128, 2, FT, D],
                                              BF16))
        msb = ctx.enter_context(nc.sbuf_tensor(f"l1m{u}", [128, 2, TC], BF16))
        t1 = ctx.enter_context(nc.sbuf_tensor(f"l1t{u}", [128, FT, TC], BF16))
        oac = ctx.enter_context(nc.sbuf_tensor(f"l1o{u}", [128, KT, TC], F32))
        etm = ctx.enter_context(nc.sbuf_tensor(f"l1e{u}", [128, 4, TC], F32))
        psl = [ctx.enter_context(nc.psum_tensor(f"l1p{i}{u}", [128, TC], F32))
               for i in range(4)]
        sP, sSi, sDv, sGa, sFin = "l1P", "l1Si", "l1Dv", "l1Ga", "l1F"
        sCp = "l1Cp"

        def load_e(s, eg, ng, sl):
            p.inc(s.dma_start(w1[:, sl, :, :],
                              w1_bf[eg].rearrange("(k q) m -> q k m", q=128)),
                  f"l1W{sl}~s", 16)
            p.inc(s.dma_start(w2[:, sl, :, :],
                              w2_bf[eg].rearrange("(k q) m -> q k m", q=128)),
                  f"l1V{sl}~s", 16)
            p.inc(s.dma_start(
                msb[:, sl, :],
                mrowd.ap()[eg:eg + 1, ng * TC:(ng + 1) * TC]
                .broadcast_to((128, TC))), f"l1M{sl}~s", 16)

        with nc.Block() as blk:
            @blk.sync
            def _(s):
                p.inc(s.dma_start(xh[:, 0, :, :], p.XHD[:, :, 1:1 + TC]),
                      "l1L0~s", 16)
                load_e(s, 0, 0, 0)
        for n in range(NCH):
            lo = n * TC
            with nc.Block() as blk:
                bP = p.mark(sP); bSi = p.mark(sSi); bDv = p.mark(sDv)
                bGa = p.mark(sGa); bFin = p.mark(sFin); bCp = p.mark(sCp)

                @blk.sync
                def _(s):
                    if n + 1 < NCH:
                        p.inc(s.dma_start(
                            xh[:, (n + 1) % 2, :, :],
                            p.XHD[:, :, 1 + lo + TC:1 + lo + 2 * TC]),
                            "l1L" + str((n + 1) % 2) + "~s", 16)
                    for e in range(8):
                        ee, nn2 = (e + 1, n) if e + 1 < 8 else (0, n + 1)
                        if nn2 < NCH:
                            p.wait(s, sP, val=bP + 30 * e)
                            p.wait(s, sDv, val=bDv + 26 * e)
                            load_e(s, ee, nn2, (e + 1) % 2)
                    for m in range(KT):
                        p.wait(s, sFin, val=bFin + m + 1)
                        p.inc(s.dma_start(p.XRD[:, m, lo:lo + TC],
                                          xrb[:, m, :]),
                              "l1O" + str(n % 2) + "~s", 16)

                @blk.gpsimd
                def _(g):
                    p.waitb(g, "l1O" + str(n % 2) + "~s",
                            16 * KT * (n // 2))
                    p.inc(g.dma_start(xrb[:], p.XRD[:, :, lo:lo + TC]),
                          "l1X~g", 16)
                    for e in range(8):
                        for j, m in enumerate(range(0, KT, 2)):
                            p.wait(g, sCp, val=bCp + 4 * e + j + 1)
                            p.wait(g, sGa, val=bGa + 4 * e)
                            if e == 0:
                                ins = g.tensor_copy(oac[:, m, :], etm[:, j, :])
                            else:
                                ins = g.tensor_tensor(oac[:, m, :], oac[:, m, :],
                                                      etm[:, j, :], ALU.add)
                            p.inc(ins, sGa, 1)

                @blk.tensor
                def _(t):
                    for e in range(8):
                        sl = e % 2
                        g_idx = 8 * n + e
                        p.waitb(t, f"l1W{sl}~s", 16 * (g_idx // 2 + 1))
                        if e == 0:
                            p.waitb(t, "l1L" + str(n % 2) + "~s",
                                    16 * (n // 2 + 1))
                        for f in range(FT):
                            if f >= 4:
                                p.wait(t, sSi, val=bSi + 11 * e + f - 3)
                            elif e > 0:
                                p.wait(t, sDv, val=bDv + 26 * e)
                                p.wait(t, sCp, val=bCp + 4 * e)
                            for k in range(KT):
                                ins = t.matmul(psl[f % 4][:],
                                               w1[:, sl, k, f * 128:(f + 1) * 128],
                                               xh[:, n % 2, k, :],
                                               start=(k == 0), stop=(k == KT - 1))
                            p.inc(ins, sP, 1)
                        for f in range(FT):
                            if f < 3:
                                p.wait(t, sSi, val=bSi + 11 * e + f + 9)
                            elif f == 3:
                                p.wait(t, sSi, val=bSi + 11 * e + 8)
                            else:
                                p.wait(t, sDv, val=bDv + 26 * e + f + 8)
                            for k in range(KT):
                                ins = t.matmul(
                                    psl[f % 4][:],
                                    w1[:, sl, k,
                                       1408 + f * 128:1408 + (f + 1) * 128],
                                    xh[:, n % 2, k, :],
                                    start=(k == 0), stop=(k == KT - 1))
                            p.inc(ins, sP, 1)
                        p.waitb(t, f"l1V{sl}~s", 16 * (g_idx // 2 + 1))
                        for m in range(KT):
                            p.wait(t, sDv, val=bDv + 26 * e + 22)
                            if m >= 4:
                                if (m - 4) % 2 == 0:
                                    p.wait(t, sCp,
                                           val=bCp + 4 * e + (m - 4) // 2 + 1)
                                else:
                                    p.wait(t, sDv,
                                           val=bDv + 26 * e + 22 + (m - 4) // 2 + 1)
                            for f in range(FT):
                                ins = t.matmul(psl[m % 4][:],
                                               w2[:, sl, f, m * 128:(m + 1) * 128],
                                               t1[:, f, :],
                                               start=(f == 0), stop=(f == FT - 1))
                            p.inc(ins, sP, 1)

                @blk.scalar
                def _(a):
                    for e in range(8):
                        if e > 0:
                            p.wait(a, sDv, val=bDv + 26 * e - 4)
                        for f in range(FT):
                            p.wait(a, sP, val=bP + 30 * e + f + 1)
                            p.wait(a, sSi, val=bSi + 11 * e + f)
                            p.inc(a.activation(t1[:, f, :], psl[f % 4][:],
                                               AF.Silu), sSi, 1)
                        for j, m in enumerate(range(0, KT, 2)):
                            p.wait(a, sP, val=bP + 30 * e + 22 + m + 1)
                            p.wait(a, sGa, val=bGa + 4 * e + j)
                            p.inc(a.activation(etm[:, j, :], psl[m % 4][:],
                                               AF.Copy), sCp, 1)

                @blk.vector
                def _(v):
                    for e in range(8):
                        sl = e % 2
                        g_idx = 8 * n + e
                        p.waitb(v, f"l1M{sl}~s", 16 * (g_idx // 2 + 1))
                        for f in range(FT):
                            p.wait(v, sSi, val=bSi + 11 * e + f + 1)
                            if e > 0 and f == 0:
                                p.wait(v, sP, val=bP + 30 * e)
                            p.inc(v.tensor_tensor(t1[:, f, :], t1[:, f, :],
                                                  msb[:, sl, :], ALU.mult),
                                  sDv, 1)
                        for f in range(FT):
                            p.wait(v, sP, val=bP + 30 * e + 11 + f + 1)
                            p.wait(v, sDv, val=bDv + 26 * e + f + 1)
                            p.inc(v.tensor_tensor(t1[:, f, :], psl[f % 4][:],
                                                  t1[:, f, :], ALU.mult),
                                  sDv, 1)
                        for m in range(1, KT, 2):
                            p.wait(v, sP, val=bP + 30 * e + 22 + m + 1)
                            p.wait(v, sDv, val=bDv + 26 * e + 22)
                            ps = psl[m % 4]
                            if e == 0:
                                ins = v.tensor_copy(oac[:, m, :], ps[:])
                            else:
                                ins = v.tensor_tensor(oac[:, m, :], oac[:, m, :],
                                                      ps[:], ALU.add)
                            p.inc(ins, sDv, 1)
                    p.waitb(v, "l1X~g", 16 * (n + 1))
                    p.wait(v, sDv, val=bDv + 26 * 8)
                    p.wait(v, sGa, val=bGa + 4 * 8)
                    for m in range(KT):
                        p.inc(v.tensor_tensor(xrb[:, m, :], xrb[:, m, :],
                                              oac[:, m, :], ALU.add), sFin, 1)
        with nc.Block() as blk:
            @blk.sync
            def _(s):
                p.wait(s, "l1O0~s")
                p.wait(s, "l1O1~s")
        p.phase_reset()



def emit_l1_attn(p, wqkvT_bf, bq_c_sb, bkv_bf):
    nc = p.nc
    u = p.uid()
    qTd = p.dram(f"laq{u}", [128, 8, T], BF16)
    ktm = p.dram(f"lak{u}", [T, D], BF16)
    vtm = p.dram(f"lav{u}", [T, D], BF16)
    g_ext = p.dram(f"lag{u}", [D, 2 * D], F32)
    s_ext = p.dram(f"las{u}", [3 * D], F32)
    s_shift = p.dram(f"lash{u}", [128, 2 * D], F32)

    jobs = [("cm", m, qTd, m) for m in range(8)]
    jobs += [("tm", D, ktm, tt, cc) for tt in range(4) for cc in range(2)]
    jobs += [("tm", 2 * D, vtm, tt, cc) for tt in range(4) for cc in range(2)]
    emit_proj_to_staging(p, wqkvT_bf, 3072, jobs)

    with ExitStack() as ctx:
        wkv = ctx.enter_context(nc.sbuf_tensor(f"lbw{u}", [128, KT, 2048],
                                               BF16))
        xhc = ctx.enter_context(nc.sbuf_tensor(f"lbx{u}", [128, 2, KT, TC],
                                               BF16))
        sxc = ctx.enter_context(nc.sbuf_tensor(f"lbs{u}", [128, 2, KT], F32))
        sxb = ctx.enter_context(nc.sbuf_tensor(f"lbsb{u}", [128, KT], BF16))
        rows = ctx.enter_context(nc.sbuf_tensor(f"lbr{u}", [1, 6, D], F32))
        rowb = ctx.enter_context(nc.sbuf_tensor(f"lbrb{u}", [1, 4, D], BF16))
        ps1 = ctx.enter_context(nc.psum_tensor(f"lbp1{u}", [1, TC], F32))
        ps2 = ctx.enter_context(nc.psum_tensor(f"lbp2{u}", [1, TC], F32))
        with nc.Block() as blk:
            @blk.sync
            def _(s):
                p.inc(s.dma_start(wkv[:], wqkvT_bf[:, D:3 * D].rearrange(
                    "(k q) m -> q k m", q=128)), "lbW~s", 16)
                p.inc(s.dma_start(xhc[:, 0, :, :], p.XHD[:, :, 1:1 + TC]),
                      "lbX0~s", 16)

            @blk.gpsimd
            def _(g):
                p.inc(g.dma_start(rowb[0:1, 0, :], bkv_bf[0:1, :]),
                      "lbB~g", 16)
                p.inc(g.dma_start(rowb[0:1, 1, :], bkv_bf[1:2, :]),
                      "lbB~g", 16)
        for n in range(NCH):
            with nc.Block() as blk:
                bV = p.mark("lbV")

                @blk.sync
                def _(s):
                    if n + 1 < NCH:
                        p.inc(s.dma_start(
                            xhc[:, (n + 1) % 2, :, :],
                            p.XHD[:, :, 1 + (n + 1) * TC:1 + (n + 2) * TC]),
                            "lbX" + str((n + 1) % 2) + "~s", 16)

                @blk.vector
                def _(v):
                    p.waitb(v, "lbX" + str(n % 2) + "~s", 16 * (n // 2 + 1))
                    for k in range(KT):
                        p.inc(v.tensor_reduce(sxc[:, 1, k:k + 1],
                                              xhc[:, n % 2, k, :],
                                              AX.X, ALU.add), "lbV", 1)
                        p.wait(v, "lbV")
                        if n == 0:
                            p.inc(v.tensor_copy(sxc[:, 0, k:k + 1],
                                                sxc[:, 1, k:k + 1]), "lbV", 1)
                        else:
                            p.inc(v.tensor_add(sxc[:, 0, k:k + 1],
                                               sxc[:, 0, k:k + 1],
                                               sxc[:, 1, k:k + 1]), "lbV", 1)
                        p.wait(v, "lbV")
        with nc.Block() as blk:
            @blk.vector
            def _(v):
                p.inc(v.tensor_copy(sxb[:], sxc[:, 0, :]), "lbV", 1)
                p.wait(v, "lbV")
        with nc.Block() as blk:
            bbP = p.mark("lbP"); bbE = p.mark("lbE")

            @blk.tensor
            def _(t):
                p.waitb(t, "lbW~s", 16)
                for cc in range(2):
                    if cc:
                        p.wait(t, "lbE", val=bbE + 2)
                    for k in range(KT):
                        ins = t.matmul(ps1[:], sxb[:, k:k + 1],
                                       wkv[:, k, cc * 512:(cc + 1) * 512],
                                       start=(k == 0), stop=(k == KT - 1))
                    p.inc(ins, "lbP", 1)
                    for k in range(KT):
                        ins = t.matmul(ps2[:], sxb[:, k:k + 1],
                                       wkv[:, k, D + cc * 512:D + (cc + 1) * 512],
                                       start=(k == 0), stop=(k == KT - 1))
                    p.inc(ins, "lbP", 1)

            @blk.vector
            def _(v):
                for cc in range(2):
                    p.wait(v, "lbP", val=bbP + 2 * cc + 2)
                    p.inc(v.tensor_copy(rows[0:1, 2, cc * 512:(cc + 1) * 512],
                                        ps1[:]), "lbE", 1)
                    p.inc(v.tensor_copy(rows[0:1, 3, cc * 512:(cc + 1) * 512],
                                        ps2[:]), "lbE", 1)
        with nc.Block() as blk:
            bbG = p.mark("lbG")

            @blk.gpsimd
            def _(g):
                p.wait(g, "lbB~g")
                p.inc(g.tensor_copy(rows[0:1, 0, :], rowb[0:1, 0, :]), "lbG", 1)
                p.inc(g.tensor_copy(rows[0:1, 1, :], rowb[0:1, 1, :]), "lbG", 1)

            @blk.vector
            def _(v):
                p.wait(v, "lbG", val=bbG + 2)
                p.inc(v.scalar_tensor_tensor(rows[0:1, 3, :], rows[0:1, 1, :],
                                             float(T), rows[0:1, 3, :],
                                             ALU.mult, ALU.add), "lbV", 1)
                p.wait(v, "lbV")
                p.inc(v.tensor_copy(rowb[0:1, 2, :], rows[0:1, 2, :]), "lbV", 1)
                p.inc(v.tensor_copy(rowb[0:1, 3, :], rows[0:1, 3, :]), "lbV", 1)
                p.wait(v, "lbV")

        NT = T // 128
        with ExitStack() as ctx2:
            vts = ctx2.enter_context(nc.sbuf_tensor(f"lgv{u}", [128, NT, TC],
                                                    BF16))
            kts = ctx2.enter_context(nc.sbuf_tensor(f"lgk{u}", [128, 2, NT, 128],
                                                    BF16))
            gsb = ctx2.enter_context(nc.sbuf_tensor(f"lgg{u}", [128, 2, TC],
                                                    F32))
            psG = [ctx2.enter_context(nc.psum_tensor(f"lgp{i}{u}", [128, TC],
                                                     F32)) for i in range(2)]
            for cc in range(2):
                with nc.Block() as blk:
                    bGP = p.mark("lgP"); bGE = p.mark("lgE")

                    @blk.sync
                    def _(s):
                        p.inc(s.dma_start(
                            vts[:],
                            vtm[:, cc * 512:(cc + 1) * 512].rearrange(
                                "(tt q) d -> q tt d", q=128)), "lgV~s", 16)
                        for jt in range(KT):
                            if jt == 0 and cc == 0:
                                p.inc(s.dma_start(
                                    kts[:, 0, :, :],
                                    ktm[:, 0:128].rearrange(
                                        "(tt q) d -> q tt d", q=128)), "lgK0~s",
                                    16)
                            jn = jt + 1
                            if jn < KT or cc == 0:
                                jn2 = jn % KT
                                p.wait(s, "lgP", val=bGP + max(0, 2 * jt))
                                p.inc(s.dma_start(
                                    kts[:, (jt + 1) % 2, :, :],
                                    ktm[:, jn2 * 128:(jn2 + 1) * 128].rearrange(
                                        "(tt q) d -> q tt d", q=128)),
                                    "lgK" + str((jt + 1) % 2) + "~s", 16)
                            p.wait(s, "lgE", val=bGE + jt + 1)
                            p.inc(s.dma_start(
                                g_ext[jt * 128:(jt + 1) * 128,
                                      cc * 512:(cc + 1) * 512],
                                gsb[:, jt % 2, :]), f"lgO{jt % 2}~s", 16)
                            p.inc(s.dma_start(
                                g_ext[jt * 128:(jt + 1) * 128,
                                      D + cc * 512:D + (cc + 1) * 512],
                                gsb[:, jt % 2, :]), f"lgO{jt % 2}~s", 16)

                    @blk.tensor
                    def _(t):
                        p.waitb(t, "lgV~s", 16 * (cc + 1))
                        for jt in range(KT):
                            g_i = 8 * cc + jt
                            p.waitb(t, "lgK" + str(jt % 2) + "~s",
                                    16 * (g_i // 2 + 1))
                            if jt >= 2:
                                p.wait(t, "lgE", val=bGE + jt - 1)
                            ps = psG[jt % 2]
                            for tt in range(NT):
                                ins = t.matmul(ps[:], kts[:, jt % 2, tt, :],
                                               vts[:, tt, :], start=(tt == 0),
                                               stop=False)
                            ins = t.matmul(ps[:], rowb[0:1, 0,
                                                       jt * 128:(jt + 1) * 128],
                                           rowb[0:1, 3, cc * 512:(cc + 1) * 512],
                                           start=False, stop=False)
                            ins = t.matmul(ps[:], rowb[0:1, 2,
                                                       jt * 128:(jt + 1) * 128],
                                           rowb[0:1, 1, cc * 512:(cc + 1) * 512],
                                           start=False, stop=True)
                            p.inc(ins, "lgP", 2)

                    @blk.vector
                    def _(v):
                        for jt in range(KT):
                            p.wait(v, "lgP", val=bGP + 2 * (jt + 1))
                            p.waitb(v, f"lgO{jt % 2}~s",
                                    32 * ((8 * cc + jt) // 2))
                            p.inc(v.tensor_copy(gsb[:, jt % 2, :],
                                                psG[jt % 2][:]), "lgE", 1)
            with nc.Block() as blk:
                @blk.sync
                def _(s):
                    p.wait(s, "lgO0~s")
                    p.wait(s, "lgO1~s")

        with ExitStack() as ctx3:
            rsb = ctx3.enter_context(nc.sbuf_tensor(f"lcr{u}", [128, 2, D],
                                                    F32))
            ssb = ctx3.enter_context(nc.sbuf_tensor(f"lcs{u}", [1, D], F32))
            psS = [ctx3.enter_context(nc.psum_tensor(f"lcp{i}{u}", [1, TC],
                                                     F32)) for i in range(2)]
            with nc.Block() as blk:
                bCP = p.mark("lcP"); bCE = p.mark("lcE")

                @blk.sync
                def _(s):
                    for jt in range(KT):
                        if jt >= 2:
                            p.wait(s, "lcP", val=bCP + 2 * (jt - 1))
                        ap = bass.AP(g_ext, jt * 128 * 2047 + D,
                                     [[2047, 128], [1, D]])
                        p.inc(s.dma_start(rsb[:, jt % 2, :], ap),
                              "lcR" + str(jt % 2) + "~s", 16)

                @blk.tensor
                def _(t):
                    for jt in range(KT):
                        p.waitb(t, "lcR" + str(jt % 2) + "~s",
                                16 * (jt // 2 + 1))
                        for cc in range(2):
                            ins = t.matmul(psS[cc][:], p.ones_col_f32[:, 0:1],
                                           rsb[:, jt % 2,
                                               cc * 512:(cc + 1) * 512],
                                           start=(jt == 0), stop=(jt == KT - 1))
                        p.inc(ins, "lcP", 2)

                @blk.vector
                def _(v):
                    p.wait(v, "lcP", val=bCP + 16)
                    p.inc(v.tensor_copy(ssb[0:1, 0:512], psS[0][:]), "lcE", 1)
                    p.inc(v.tensor_copy(ssb[0:1, 512:1024], psS[1][:]), "lcE", 1)

                @blk.sync
                def _(s2):
                    p.wait(s2, "lcE", val=bCE + 2)
                    p.inc(s2.dma_start(s_ext.ap()[None, 0:D], ssb[0:1, :]), "lcO~s", 16)
                    p.inc(s2.dma_start(s_ext.ap()[None, D:2 * D], ssb[0:1, :]), "lcO~s", 16)
                    p.inc(s2.dma_start(s_ext.ap()[None, 2 * D:3 * D],
                                        ssb[0:1, :]), "lcO~s", 16)
                    p.wait(s2, "lcO~s")
                    for r in range(128):
                        p.inc(s2.dma_start(s_shift[r:r + 1, :],
                                           s_ext.ap()[None, r:r + 2 * D]),
                              "lcS~s", 16)
                    p.wait(s2, "lcS~s")

        mt = ctx.enter_context(nc.sbuf_tensor(f"ldm{u}", [128, KT, D], BF16))
        bqb = ctx.enter_context(nc.sbuf_tensor(f"ldq{u}", [128, KT], BF16))
        obq = ctx.enter_context(nc.sbuf_tensor(f"ldo{u}", [1, D], F32))
        bias_sb = ctx.enter_context(nc.sbuf_tensor(f"ldb{u}", [128, KT], F32))
        psB = ctx.enter_context(nc.psum_tensor(f"ldp{u}", [1, TC], F32))
        psB2 = ctx.enter_context(nc.psum_tensor(f"ldp2{u}", [128, 1], F32))
        with nc.Block() as blk:
            @blk.gpsimd
            def _(g):
                for jt in range(KT):
                    p.inc(g.dma_start(mt[:, jt, :],
                                      s_shift[:, jt * 128:jt * 128 + D]),
                          "ldM~g", 16)

            @blk.vector
            def _(v):
                p.inc(v.tensor_copy(bqb[:], bq_c_sb[:]), "ldV", 1)
        with nc.Block() as blk:
            bDP = p.mark("ldP"); bDE = p.mark("ldE")

            @blk.tensor
            def _(t):
                p.waitb(t, "ldM~g", 16 * KT)
                for cc in range(2):
                    if cc:
                        p.wait(t, "ldE", val=bDE + 1)
                    for jt in range(KT):
                        ins = t.matmul(psB[:], bqb[:, jt:jt + 1],
                                       mt[:, jt, cc * 512:(cc + 1) * 512],
                                       start=(jt == 0), stop=(jt == KT - 1))
                    p.inc(ins, "ldP", 1)

            @blk.vector
            def _(v):
                for cc in range(2):
                    p.wait(v, "ldP", val=bDP + cc + 1)
                    p.inc(v.tensor_copy(obq[0:1, cc * 512:(cc + 1) * 512],
                                        psB[:]), "ldE", 1)
        with nc.Block() as blk:
            bDT = p.mark("ldT"); bDE2 = p.mark("ldE2")

            @blk.tensor
            def _(t):
                for m in range(KT):
                    if m >= 1:
                        p.wait(t, "ldE2", val=bDE2 + m)
                    p.inc(t.transpose(psB2[:], obq[0:1, m * 128:(m + 1) * 128],
                                      p.ident_f32[0:1, 0:1]), "ldT", 1)

            @blk.vector
            def _(v):
                for m in range(KT):
                    p.wait(v, "ldT", val=bDT + m + 1)
                    p.inc(v.tensor_copy(bias_sb[:, m:m + 1], psB2[:]), "ldE2", 1)

        emit_proj_residual(p, qTd, 0, mt, bias=bias_sb, scale=1.0)
    p.phase_reset()






def emit_l2_attn(p, W):
    nc = p.nc
    u = p.uid()
    H, DK, DV, CLEN = 4, 128, 256, 32
    NCK = T // CLEN
    rTd = p.dram(f"l2r{u}", [128, H, T], BF16)
    kTd = p.dram(f"l2k{u}", [128, H, T], BF16)
    eTd = p.dram(f"l2e{u}", [128, H, T], F32)
    vtd = p.dram(f"l2v{u}", [T, D], BF16)
    gsd = p.dram(f"l2g{u}", [128, KT, T], BF16)
    oTd = p.dram(f"l2o{u}", [128, KT, T], BF16)

    with ExitStack() as c1:
        load_l2_p1_weights(p, W, c1)
        xh = c1.enter_context(nc.sbuf_tensor(f"p1x{u}", [128, 2, KT, TC + 1],
                                             BF16))
        dd = c1.enter_context(nc.sbuf_tensor(f"p1d{u}", [128, KT, TC], BF16))
        mxp = c1.enter_context(nc.sbuf_tensor(f"p1m{u}", [128, KT, TC], BF16))
        xl5 = c1.enter_context(nc.sbuf_tensor(f"p1l{u}", [32, 5, TC], BF16))
        xm = c1.enter_context(nc.sbuf_tensor(f"p1xm{u}", [128, 5, KT, TC],
                                             BF16))
        wt64 = c1.enter_context(nc.sbuf_tensor(f"p1w6{u}", [64, 2, TC], BF16))
        stR = c1.enter_context(nc.sbuf_tensor(f"p1sr{u}", [128, 4, TC], BF16))
        stV = c1.enter_context(nc.sbuf_tensor(f"p1sv{u}", [128, 4, TC], BF16))
        stG = c1.enter_context(nc.sbuf_tensor(f"p1sg{u}", [128, 2, TC], BF16))
        stE = c1.enter_context(nc.sbuf_tensor(f"p1se{u}", [128, 2, TC], F32))
        tmp2 = c1.enter_context(nc.sbuf_tensor(f"p1t{u}", [128, 2, TC], F32))
        psXl = [c1.enter_context(nc.psum_tensor(f"p1pl{i}{u}", [32, TC], F32))
                for i in range(2)]
        ps64 = c1.enter_context(nc.psum_tensor(f"p1p6{u}", [64, TC], F32))
        psW = [c1.enter_context(nc.psum_tensor(f"p1pw{i}{u}", [128, TC], F32))
               for i in range(2)]
        psl = [c1.enter_context(nc.psum_tensor(f"p1p{i}{u}", [128, TC], F32))
               for i in range(3)]
        with nc.Block() as blk:
            @blk.sync
            def _(s):
                p.wait(s, "!initd")
                p.inc(s.dma_start(xh[:, 0, :, :], p.XHD[:, :, 0:TC + 1]),
                      "p1L0~s", 16)
        du = {k2: 0 for k2 in ["R0", "R1", "R2", "R3", "V0", "V1", "V2", "V3",
                               "G0", "G1", "E0", "E1"]}
        for n in range(NCH):
            lo = n * TC
            with nc.Block() as blk:
                bM = p.mark("sMix"); bLP = p.mark("sXlP"); bLE = p.mark("sXlE")
                bXP = p.mark("sX5P"); bXa = p.mark("sXa"); bXb = p.mark("sXb")
                bPP = p.mark("sPP"); bRKE = p.mark("sRKE"); bVE = p.mark("sVE")
                bGE = p.mark("sGE"); bW1 = p.mark("sW1"); bWT = p.mark("sWT")
                bEE = p.mark("sEE")
                du0 = dict(du)

                @blk.sync
                def _(s):
                    if n + 1 < NCH:
                        p.inc(s.dma_start(
                            xh[:, (n + 1) % 2, :, :],
                            p.XHD[:, :, lo + TC:lo + 2 * TC + 1]),
                            "p1L" + str((n + 1) % 2) + "~s", 16)
                    for i in range(8):
                        p.wait(s, "sRKE", val=bRKE + i + 1)
                        dstd = rTd if i < 4 else kTd
                        p.inc(s.dma_start(dstd[:, i % 4, lo:lo + TC],
                                          stR[:, i % 4, :]),
                              f"p1R{i % 4}~s", 16)
                        du[f"R{i % 4}"] += 1
                    for i in range(4):
                        p.wait(s, "sEE", val=bEE + i + 1)
                        p.inc(s.dma_start(eTd[:, i, lo:lo + TC],
                                          stE[:, i % 2, :]),
                              f"p1E{i % 2}~s", 16)
                        du[f"E{i % 2}"] += 1
                    for i in range(8):
                        p.wait(s, "sVE", val=bVE + i + 1)
                        tt, cc = i // 2, i % 2
                        p.inc(s.dma_start(
                            vtd[lo + tt * 128:lo + (tt + 1) * 128,
                                cc * 512:(cc + 1) * 512],
                            stV[:, i % 4, :]), f"p1V{i % 4}~s", 16)
                        du[f"V{i % 4}"] += 1
                    for i in range(8):
                        p.wait(s, "sGE", val=bGE + i + 1)
                        p.inc(s.dma_start(gsd[:, i, lo:lo + TC],
                                          stG[:, i % 2, :]),
                              f"p1G{i % 2}~s", 16)
                        du[f"G{i % 2}"] += 1

                @blk.vector
                def _(v):
                    p.waitb(v, "p1L" + str(n % 2) + "~s", 16 * (n // 2 + 1))
                    for k in range(KT):
                        p.wait(v, "sMix")
                        p.inc(v.tensor_tensor(dd[:, k, :],
                                              xh[:, n % 2, k, 0:TC],
                                              xh[:, n % 2, k, 1:TC + 1],
                                              ALU.subtract), "sMix", 1)
                        p.wait(v, "sMix")
                        p.inc(v.scalar_tensor_tensor(mxp[:, k, :], dd[:, k, :],
                                                     W["xp_mu"][:, k:k + 1],
                                                     xh[:, n % 2, k, 1:TC + 1],
                                                     ALU.mult, ALU.add),
                              "sMix", 1)
                    for idx in range(40):
                        ns, m = idx // 8, idx % 8
                        p.wait(v, "sX5P", val=bXP + idx + 1)
                        p.wait(v, "sXb", val=bXb + max(0, idx - 1))
                        p.inc(v.tensor_tensor(tmp2[:, idx % 2, :],
                                              psl[idx % 3][:], dd[:, m, :],
                                              ALU.mult), "sXa", 1)
                    for i in range(8):
                        p.wait(v, "sPP", val=bPP + i + 1)
                        p.waitb(v, f"p1R{i % 4}~s",
                                16 * (du0[f"R{i % 4}"] + i // 4))
                        p.inc(v.tensor_copy(stR[:, i % 4, :], psW[i % 2][:]),
                              "sRKE", 1)
                    for i in range(8):
                        p.wait(v, "sPP", val=bPP + 8 + i + 1)
                        p.waitb(v, f"p1V{i % 4}~s",
                                16 * (du0[f"V{i % 4}"] + i // 4))
                        p.inc(v.tensor_copy(stV[:, i % 4, :], psW[i % 2][:]),
                              "sVE", 1)

                @blk.gpsimd
                def _(g):
                    for idx in range(40):
                        ns, m = idx // 8, idx % 8
                        p.wait(g, "sXa", val=bXa + idx + 1)
                        p.inc(g.tensor_tensor(xm[:, ns, m, :],
                                              tmp2[:, idx % 2, :],
                                              xh[:, n % 2, m, 1:TC + 1],
                                              ALU.add), "sXb", 1)

                @blk.scalar
                def _(a):
                    for ns in range(5):
                        p.wait(a, "sXlP", val=bLP + ns + 1)
                        p.wait(a, "sXlE", val=bLE + ns)
                        p.inc(a.activation(xl5[:, ns, :], psXl[ns % 2][:],
                                           AF.Tanh), "sXlE", 1)
                    p.wait(a, "sW1", val=bW1 + 1)
                    p.inc(a.activation(wt64[:, 0, :], ps64[:], AF.Tanh),
                          "sWT", 1)
                    for m in range(4):
                        p.wait(a, "sW1", val=bW1 + 2 + m)
                        p.waitb(a, f"p1E{m % 2}~s",
                                16 * (du0[f"E{m % 2}"] + m // 2))
                        p.inc(a.activation(stE[:, m % 2, :], psl[m % 2][:],
                                           AF.Exp,
                                           bias=W["wb_c"][:, m:m + 1]),
                              "sEE", 1)
                    p.wait(a, "sW1", val=bW1 + 6)
                    p.inc(a.activation(wt64[:, 1, :], ps64[:], AF.Tanh),
                          "sWT", 1)
                    for m in range(KT):
                        p.wait(a, "sPP", val=bPP + 16 + m + 1)
                        p.waitb(a, f"p1G{m % 2}~s",
                                16 * (du0[f"G{m % 2}"] + m // 2))
                        p.inc(a.activation(stG[:, m % 2, :], psW[m % 2][:],
                                           AF.Silu,
                                           bias=W["gb_c"][:, m:m + 1]),
                              "sGE", 1)

                @blk.tensor
                def _(t):
                    p.wait(t, "sMix", val=bM + 16)
                    for ns in range(5):
                        if ns >= 2:
                            p.wait(t, "sXlE", val=bLE + ns - 1)
                        for k in range(KT):
                            ins = t.matmul(psXl[ns % 2][:],
                                           W["xpW1T"][:, k,
                                                      ns * 32:(ns + 1) * 32],
                                           mxp[:, k, :], start=(k == 0),
                                           stop=(k == KT - 1))
                        p.inc(ins, "sXlP", 1)
                    for ns in range(5):
                        p.wait(t, "sXlE", val=bLE + ns + 1)
                        for m in range(KT):
                            idx = 8 * ns + m
                            if idx >= 3:
                                p.wait(t, "sXa", val=bXa + idx - 2)
                            ins = t.matmul(psl[idx % 3][:],
                                           W["w2n"][:, ns,
                                                    m * 128:(m + 1) * 128],
                                           xl5[:, ns, :], start=True,
                                           stop=False)
                            ins = t.matmul(psl[idx % 3][:],
                                           W["xbias"][0:1,
                                                      ns * D + m * 128:
                                                      ns * D + (m + 1) * 128],
                                           p.ones_row_bf[0:1, 0:TC],
                                           start=False, stop=True)
                            p.inc(ins, "sX5P", 1)
                    p.wait(t, "sXb", val=bXb + 16)
                    for k in range(KT):
                        ins = t.matmul(ps64[:], W["wAT"][:, k, :],
                                       xm[:, 1, k, :], start=(k == 0),
                                       stop=(k == KT - 1))
                    p.inc(ins, "sW1", 1)
                    p.wait(t, "sWT", val=bWT + 1)
                    p.wait(t, "sXa", val=bXa + 40)
                    for m in range(4):
                        p.wait(t, "sEE", val=bEE + max(0, m - 1))
                        ins = t.matmul(psl[m % 2][:],
                                       W["wBT"][:, m * 128:(m + 1) * 128],
                                       wt64[:, 0, :], start=True, stop=True)
                        p.inc(ins, "sW1", 1)
                    p.wait(t, "sXb", val=bXb + 40)
                    for k in range(KT):
                        ins = t.matmul(ps64[:], W["gAT"][:, k, :],
                                       xm[:, 4, k, :], start=(k == 0),
                                       stop=(k == KT - 1))
                    p.inc(ins, "sW1", 1)
                    for i in range(8):
                        if i == 0:
                            p.wait(t, "sXb", val=bXb + 8)
                        if i == 4:
                            p.wait(t, "sXb", val=bXb + 24)
                        w_, ns = (W["rWT"], 0) if i < 4 else (W["kWT"], 2)
                        m = i % 4
                        if i >= 2:
                            p.wait(t, "sRKE", val=bRKE + i - 1)
                        for k in range(KT):
                            ins = t.matmul(psW[i % 2][:],
                                           w_[:, k, m * 128:(m + 1) * 128],
                                           xm[:, ns, k, :], start=(k == 0),
                                           stop=(k == KT - 1))
                        p.inc(ins, "sPP", 1)
                    p.wait(t, "sXb", val=bXb + 32)
                    for i in range(8):
                        tt, cc = i // 2, i % 2
                        if i < 2:
                            p.wait(t, "sRKE", val=bRKE + 7 + i)
                        p.wait(t, "sVE", val=bVE + max(0, i - 1))
                        for k in range(KT):
                            ins = t.matmul(
                                psW[i % 2][:],
                                xm[:, 3, k, tt * 128:(tt + 1) * 128],
                                W["vWT"][:, k, cc * 512:(cc + 1) * 512],
                                start=(k == 0), stop=(k == KT - 1))
                        p.inc(ins, "sPP", 1)
                    p.wait(t, "sWT", val=bWT + 2)
                    for m in range(KT):
                        if m < 2:
                            p.wait(t, "sVE", val=bVE + 7 + m)
                        p.wait(t, "sGE", val=bGE + max(0, m - 1))
                        ins = t.matmul(psW[m % 2][:],
                                       W["gBT"][:, m * 128:(m + 1) * 128],
                                       wt64[:, 1, :], start=True, stop=True)
                        p.inc(ins, "sPP", 1)
        with nc.Block() as blk:
            @blk.sync
            def _(s):
                for nm2 in du:
                    p.wait(s, f"p1{nm2}~s")
        p.phase_reset()

    with ExitStack() as c2:
        rT = c2.enter_context(nc.sbuf_tensor(f"p2r{u}", [128, T], BF16))
        kT = c2.enter_context(nc.sbuf_tensor(f"p2k{u}", [128, T], BF16))
        eb = c2.enter_context(nc.sbuf_tensor(f"p2e{u}", [128, T], F32))
        Fc = c2.enter_context(nc.sbuf_tensor(f"p2F{u}", [128, T], F32))
        Fsh = c2.enter_context(nc.sbuf_tensor(f"p2Fs{u}", [128, T], F32))
        SPRD = c2.enter_context(nc.sbuf_tensor(f"p2sp{u}", [128, T], F32))
        m32 = c2.enter_context(nc.sbuf_tensor(f"p2m3{u}", [32, TC], BF16))
        i32 = c2.enter_context(nc.sbuf_tensor(f"p2i3{u}", [32, TC], BF16))
        cols = c2.enter_context(nc.sbuf_tensor(f"p2c{u}", [128, 4, NCK], F32))
        colT = c2.enter_context(nc.sbuf_tensor(f"p2ct{u}", [128, 3, 128],
                                               F32))
        Aend = c2.enter_context(nc.sbuf_tensor(f"p2ae{u}", [128, NCK], F32))
        qt = c2.enter_context(nc.sbuf_tensor(f"p2qt{u}", [128, T], BF16))
        kt2 = c2.enter_context(nc.sbuf_tensor(f"p2kt{u}", [128, T], BF16))
        kp = c2.enter_context(nc.sbuf_tensor(f"p2kp{u}", [128, T], BF16))
        qi = c2.enter_context(nc.sbuf_tensor(f"p2qi{u}", [128, T], BF16))
        vts2 = c2.enter_context(nc.sbuf_tensor(f"p2v{u}", [32, 2, 16, DV],
                                               BF16))
        kpT = c2.enter_context(nc.sbuf_tensor(f"p2kpt{u}", [32, 2, 16, 128],
                                              BF16))
        qk = c2.enter_context(nc.sbuf_tensor(f"p2qk{u}", [128, T], BF16))
        drow = c2.enter_context(nc.sbuf_tensor(f"p2dr{u}", [8, T], F32))
        Phat = c2.enter_context(nc.sbuf_tensor(f"p2ph{u}", [32, 2, TC], BF16))
        Sf = c2.enter_context(nc.sbuf_tensor(f"p2S{u}", [128, DV], F32))
        Sb = c2.enter_context(nc.sbuf_tensor(f"p2Sb{u}", [128, DV], BF16))
        osb = c2.enter_context(nc.sbuf_tensor(f"p2o{u}", [128, 2, 2, TC],
                                              BF16))
        t1 = c2.enter_context(nc.sbuf_tensor(f"p2t1{u}", [128, 2, TC], F32))
        t2 = c2.enter_context(nc.sbuf_tensor(f"p2t2{u}", [128, 2, TC], F32))
        t3 = c2.enter_context(nc.sbuf_tensor(f"p2t3{u}", [32, TC], F32))
        t4 = c2.enter_context(nc.sbuf_tensor(f"p2t4{u}", [32, TC], F32))
        psSm = c2.enter_context(nc.psum_tensor(f"p2ps{u}", [128, TC], F32))
        psTr = c2.enter_context(nc.psum_tensor(f"p2pt{u}", [128, TC], F32))
        psTb = c2.enter_context(nc.psum_tensor(f"p2ptb{u}", [32, 128], BF16))
        psP = c2.enter_context(nc.psum_tensor(f"p2pp{u}", [32, TC], F32))
        psDn = c2.enter_context(nc.psum_tensor(f"p2pdn{u}", [32, TC], F32))
        psO = [c2.enter_context(nc.psum_tensor(f"p2po{i}{u}", [128, CLEN],
                                               F32)) for i in range(2)]
        psDS = c2.enter_context(nc.psum_tensor(f"p2pds{u}", [128, DV], F32))

        with nc.sbuf_tensor(f"p2ic{u}", [128, T], BF16) as iC, \
                nc.sbuf_tensor(f"p2ir{u}", [32, TC], BF16) as i32r:
            with nc.Block() as blk:
                @blk.gpsimd
                def _(g):
                    p.inc(g.iota(iC[:], [[1, NCK], [0, CLEN]],
                                 channel_multiplier=0,
                                 allow_small_or_imprecise_dtypes=True),
                          "c2I", 1)
                    p.inc(g.iota(i32r[:], [[0, TC // CLEN], [1, CLEN]],
                                 channel_multiplier=0,
                                 allow_small_or_imprecise_dtypes=True),
                          "c2I", 1)
            with nc.Block() as blk:
                bI = p.mark("c2V")

                @blk.vector
                def _(v):
                    p.inc(v.tensor_scalar(SPRD[:], iC[:], p.iota_p_g[:, 0:1],
                                          None, ALU.is_equal), "c2V", 1)
                    p.inc(v.tensor_scalar(m32[:], i32r[:],
                                          p.iota_p_g[0:32, 0:1],
                                          None, ALU.is_gt), "c2V", 1)
                    p.inc(v.tensor_scalar(i32[:], i32r[:],
                                          p.iota_p_g[0:32, 0:1],
                                          None, ALU.is_equal), "c2V", 1)
        for h in range(H):
            with nc.Block() as blk:
                @blk.sync
                def _(s):
                    p.inc(s.dma_start(rT[:], rTd[:, h, :]), "p2Lr~s", 16)
                    p.inc(s.dma_start(kT[:], kTd[:, h, :]), "p2Lk~s", 16)
                    p.inc(s.dma_start(eb[:], eTd[:, h, :]), "p2Le~s", 16)

                @blk.gpsimd
                def _(g):
                    p.inc(g.memset(Sf[:], 0.0), "p2ms", 1)
                    p.inc(g.memset(Sb[:], 0.0), "p2ms", 1)
                    p.inc(g.memset(Fsh[:, 0:1], 0.0), "p2ms", 1)
            with nc.Block() as blk:
                bV = p.mark("p2V"); bA = p.mark("p2A")

                @blk.vector
                def _(v):
                    p.waitb(v, "p2Le~s", 16 * (h + 1))
                    p.inc(v.tensor_tensor_scan(Fc[:], eb[:], eb[:], 0.0,
                                               ALU.add, ALU.bypass), "p2V", 1)
                    p.wait(v, "p2V")
                    p.inc(v.tensor_copy(Fsh[:, 1:T], Fc[:, 0:T - 1]),
                          "p2V", 1)
                    p.inc(v.tensor_copy(cols[:, 1, :], Fc.ap()[:, 30::32]),
                          "p2V", 1)
                    p.inc(v.tensor_copy(cols[:, 2, :], Fc.ap()[:, 31::32]),
                          "p2V", 1)
                    p.wait(v, "p2V")
                    p.inc(v.memset(cols[:, 0, 0:1], 0.0), "p2V", 1)
                    p.inc(v.tensor_copy(cols[:, 0, 1:NCK],
                                        cols[:, 2, 0:NCK - 1]), "p2V", 1)
                    p.wait(v, "p2V")
                    p.inc(v.tensor_add(cols[:, 3, :], cols[:, 0, :],
                                       cols[:, 1, :]), "p2V", 1)
                    p.wait(v, "p2V")
                    p.inc(v.tensor_scalar_mul(cols[:, 3, :], cols[:, 3, :],
                                              0.5), "p2V", 1)
                    p.inc(v.tensor_sub(Aend[:], cols[:, 0, :],
                                       cols[:, 2, :]), "p2V", 1)
                    p.wait(v, "p2V")
                    p.waitb(v, "p2Lr~s", 16 * (h + 1))
                    p.waitb(v, "p2Lk~s", 16 * (h + 1))
                    p.inc(v.tensor_mul(qk[:], rT[:], kT[:]), "p2V", 1)
                    p.wait(v, "p2V")
                    p.inc(v.tensor_scalar(qk[:], qk[:], W["u_c"][:, h:h + 1],
                                          None, ALU.mult), "p2V", 1)

                @blk.scalar
                def _(a):
                    p.wait(a, "p2V", val=bV + 9)
                    p.inc(a.activation(Aend[:], Aend[:], AF.Exp), "p2A", 1)
            with nc.Block() as blk:
                bT = p.mark("p2T"); bTE = p.mark("p2TE")

                @blk.tensor
                def _(t):
                    for j, ci in enumerate([3, 0, 2]):
                        if j:
                            p.wait(t, "p2TE", val=bTE + j)
                        p.inc(t.transpose(psTr[0:NCK, 0:128], cols[:, ci, :],
                                          p.ident_f32[:]), "p2T", 1)

                @blk.vector
                def _(v):
                    for j in range(3):
                        p.wait(v, "p2T", val=bT + j + 1)
                        p.inc(v.tensor_copy(colT[0:NCK, j, :],
                                            psTr[0:NCK, 0:128]), "p2TE", 1)
            dits = [(j, nn) for j in range(3) for nn in range(NCH)]
            ND = len(dits)
            nsub = [2 if j == 0 else 1 for (j, nn) in dits]
            if h == 0:
                p.l2d = {"S": 0, "X": 0, "E2": 0}
            C2 = p.l2d
            s_c = {}; xs_end = {}; xm_end = {}; e_end = {}
            for i, (j, nn) in enumerate(dits):
                C2["S"] += 1; s_c[i] = C2["S"]
                C2["X"] += nsub[i]; xs_end[i] = C2["X"]
                C2["E2"] += nsub[i]; e_end[i] = C2["E2"]
                C2["X"] += nsub[i]; xm_end[i] = C2["X"]
            psSm2 = [psSm, psTr]

            with nc.Block() as blk:
                @blk.tensor
                def _(t):
                    for i, (j, nn) in enumerate(dits):
                        nlo = nn * TC
                        if i >= 2:
                            p.waitb(t, "p2X", xs_end[i - 2])
                        p.inc(t.matmul(psSm2[i % 2][:], colT[0:NCK, j, :],
                                       SPRD[0:NCK, nlo:nlo + TC],
                                       start=True, stop=True), "p2S", 1)

                @blk.vector
                def _(v):
                    for i in range(ND + 1):
                        if i < ND:
                            j, nn = dits[i]
                            nlo = nn * TC
                            p.waitb(v, "p2S", s_c[i])
                            p.wait(v, "p2X")
                            ps = psSm2[i % 2]
                            if j == 0:
                                p.inc(v.tensor_tensor(t1[:, i % 2, :], ps[:],
                                                      Fsh[:, nlo:nlo + TC],
                                                      ALU.subtract), "p2X", 1)
                                p.inc(v.tensor_tensor(t2[:, i % 2, :],
                                                      Fc[:, nlo:nlo + TC],
                                                      ps[:], ALU.subtract),
                                      "p2X", 1)
                            elif j == 1:
                                p.inc(v.tensor_tensor(t1[:, i % 2, :], ps[:],
                                                      Fsh[:, nlo:nlo + TC],
                                                      ALU.subtract), "p2X", 1)
                            else:
                                p.inc(v.tensor_tensor(t1[:, i % 2, :],
                                                      Fc[:, nlo:nlo + TC],
                                                      ps[:], ALU.subtract),
                                      "p2X", 1)
                        if i >= 1:
                            j, nn = dits[i - 1]
                            nlo = nn * TC
                            r = (i - 1) % 2
                            p.waitb(v, "p2E2", e_end[i - 1])
                            if j == 0:
                                p.inc(v.tensor_tensor(
                                    qt[:, nlo:nlo + TC], rT[:, nlo:nlo + TC],
                                    t1[:, r, :], ALU.mult), "p2X", 1)
                                p.inc(v.tensor_tensor(
                                    kt2[:, nlo:nlo + TC], kT[:, nlo:nlo + TC],
                                    t2[:, r, :], ALU.mult), "p2X", 1)
                            elif j == 1:
                                p.inc(v.tensor_tensor(
                                    qi[:, nlo:nlo + TC], rT[:, nlo:nlo + TC],
                                    t1[:, r, :], ALU.mult), "p2X", 1)
                            else:
                                p.inc(v.tensor_tensor(
                                    kp[:, nlo:nlo + TC], kT[:, nlo:nlo + TC],
                                    t1[:, r, :], ALU.mult), "p2X", 1)

                @blk.scalar
                def _(a):
                    for i, (j, nn) in enumerate(dits):
                        p.waitb(a, "p2X", xs_end[i])
                        if i >= 2:
                            p.waitb(a, "p2X", xm_end[i - 2])
                        p.inc(a.activation(t1[:, i % 2, :], t1[:, i % 2, :],
                                           AF.Exp), "p2E2", 1)
                        if j == 0:
                            p.inc(a.activation(t2[:, i % 2, :],
                                               t2[:, i % 2, :],
                                               AF.Exp), "p2E2", 1)
            with nc.Block() as blk:
                bDP = p.mark("p2DP"); bDR = p.mark("p2DR")

                @blk.tensor
                def _(t):
                    for nn in range(NCH):
                        if nn:
                            p.wait(t, "p2DR", val=bDR + nn)
                        p.inc(t.matmul(psDn[0:8, :], p.ones_col_bf[:],
                                       qk[:, nn * TC:(nn + 1) * TC],
                                       start=True, stop=True), "p2DP", 1)

                @blk.vector
                def _(v):
                    for nn in range(NCH):
                        p.wait(v, "p2DP", val=bDP + nn + 1)
                        p.inc(v.tensor_copy(drow[:, nn * TC:(nn + 1) * TC],
                                            psDn[0:8, :]), "p2DR", 1)
            if h == 0:
                p.od2 = [0, 0]
                p.l2c = {"PP": 0, "PD": 0, "PE": 0, "KT": 0, "KE": 0,
                         "O": 0, "Ev": 0, "Up": 0, "Vs": [0, 0]}
            C = p.l2c
            pp_c = {}; pd_c = {}; t3_c = {}; t4_c = {}; add_c = {}
            ktb = {}; keb = {}; o_end = {}; ev_end = {}; vs_c = {}
            for cg in range(NCH):
                C["PP"] += 1; pp_c[cg] = C["PP"]
                C["PD"] += 1; pd_c[cg] = C["PD"]
                t3_c[cg] = C["PE"] + 1; t4_c[cg] = C["PE"] + 2
                add_c[cg] = C["PE"] + 3; C["PE"] += 3
                ktb[cg] = C["KT"]; C["KT"] += 16
                keb[cg] = C["KE"]; C["KE"] += 16
                C["O"] += 32; o_end[cg] = C["O"]
                C["Ev"] += 32; ev_end[cg] = C["Ev"]
                C["Vs"][cg % 2] += 1; vs_c[cg] = C["Vs"][cg % 2]
            up0 = C["Up"]
            C["Up"] += 2 * 16 * NCH
            od2v0 = list(p.od2)

            with nc.Block() as blk:
                @blk.tensor
                def _(t):
                    for sl in range(NCH + 1):
                        pi = sl if sl < NCH else None
                        si = sl - 1 if sl >= 1 else None
                        pops = []
                        if pi is not None:
                            for jj in range(16):
                                pops.append(("psP", jj))
                            pops.append(("psDn", 0))
                            for jj in range(16):
                                pops.append(("T", jj))

                        def emit_prod(op):
                            kind, jj = op
                            c = 16 * pi + jj
                            if kind == "psP":
                                if jj == 0 and pi >= 1:
                                    p.waitb(t, "p2PE", t3_c[pi - 1])
                                ins = t.matmul(
                                    psP[:, jj * 32:(jj + 1) * 32],
                                    kt2[:, c * 32:(c + 1) * 32],
                                    qt[:, c * 32:(c + 1) * 32],
                                    start=True, stop=True)
                                if jj == 15:
                                    p.inc(ins, "p2PP", 1)
                            elif kind == "psDn":
                                if pi >= 1:
                                    p.waitb(t, "p2PE", t4_c[pi - 1])
                                p.inc(t.matmul(
                                    psDn[:], p.ones_row_f32[0:1, 0:32],
                                    drow[0:1, pi * TC:(pi + 1) * TC],
                                    start=True, stop=True), "p2PD", 1)
                            else:
                                p.waitb(t, "p2KE", keb[pi] + jj)
                                p.inc(t.transpose(
                                    psTb[:], kp[:, c * 32:(c + 1) * 32],
                                    p.ident_bf[:]), "p2KT", 1)

                        pidx = 0
                        for jj in range(16):
                            if si is not None:
                                c = 16 * si + jj
                                M = up0 // 2 + 16 * si + jj
                                if jj == 0:
                                    p.waitb(t, f"p2Vs{si % 2}~s", 16 * vs_c[si])
                                    p.waitb(t, "p2PE", add_c[si])
                                p.waitb(t, "p2Up", 2 * M)
                                p.waitb(t, "p2Ev", max(0, 2 * M - 2))
                                for i in range(2):
                                    ins = t.matmul(
                                        psO[i][:],
                                        Sb[:, i * 128:(i + 1) * 128],
                                        qi[:, c * 32:(c + 1) * 32],
                                        start=True, stop=False)
                                    ins = t.matmul(
                                        psO[i][:],
                                        vts2[:, si % 2, jj,
                                             i * 128:(i + 1) * 128],
                                        Phat[:, si % 2,
                                             jj * 32:(jj + 1) * 32],
                                        start=False, stop=True)
                                p.inc(ins, "p2O", 1)
                                p.waitb(t, "p2KE", keb[si] + jj + 1)
                                ins = t.matmul(psDS[:],
                                               kpT[:, si % 2, jj, :],
                                               vts2[:, si % 2, jj, :],
                                               start=True, stop=True)
                                p.inc(ins, "p2O", 1)
                            for _r in range(2):
                                if pidx < len(pops):
                                    emit_prod(pops[pidx])
                                    pidx += 1
                        while pidx < len(pops):
                            emit_prod(pops[pidx])
                            pidx += 1

                @blk.vector
                def _(v):
                    for sl in range(NCH + 1):
                        pi = sl if sl < NCH else None
                        si = sl - 1 if sl >= 1 else None

                        def emit_copy(kj):
                            p.waitb(v, "p2KT", ktb[pi] + kj + 1)
                            p.inc(v.tensor_copy(kpT[:, pi % 2, kj, :],
                                                psTb[:]), "p2KE", 1)

                        for jj in range(16):
                            if si is not None:
                                c = 16 * si + jj
                                M = up0 // 2 + 16 * si + jj
                                p.waitb(v, "p2O", 2 * M + 1)
                                if jj == 0:
                                    p.waitb(v, f"p2OD{si % 2}~s",
                                            32 * (od2v0[si % 2] + si // 2))
                                    p.od2[si % 2] += 1
                                for i in range(2):
                                    p.inc(v.tensor_copy(
                                        osb[:, si % 2, i,
                                            jj * 32:(jj + 1) * 32],
                                        psO[i][:]), "p2Ev", 1)
                                p.waitb(v, "p2O", 2 * M + 2)
                                p.inc(v.scalar_tensor_tensor(
                                    Sf[:], Sf[:], Aend[:, c:c + 1], psDS[:],
                                    ALU.mult, ALU.add), "p2Up", 1)
                                p.wait(v, "p2Up")
                                p.inc(v.tensor_copy(Sb[:], Sf[:]), "p2Up", 1)
                            if pi is not None and jj == 8:
                                if pi >= 2:
                                    p.waitb(v, "p2O", o_end[pi - 2])
                                p.waitb(v, "p2PP", pp_c[pi])
                                p.wait(v, "p2PE")
                                p.inc(v.tensor_tensor(t3[:], psP[:], m32[:],
                                                      ALU.mult), "p2PE", 1)
                                p.waitb(v, "p2PD", pd_c[pi])
                                p.inc(v.tensor_tensor(t4[:], psDn[:], i32[:],
                                                      ALU.mult), "p2PE", 1)
                                p.wait(v, "p2PE")
                                p.inc(v.tensor_add(Phat[:, pi % 2, :], t3[:],
                                                   t4[:]), "p2PE", 1)
                            if pi is not None:
                                for kj in range(16):
                                    if (17 + kj) // 2 == jj:
                                        emit_copy(kj)
                        if pi is not None:
                            emit_copy(15)

                @blk.sync
                def _(s):
                    for sl in range(NCH + 1):
                        pi = sl if sl < NCH else None
                        si = sl - 1 if sl >= 1 else None
                        if pi is not None:
                            if pi >= 2:
                                p.waitb(s, "p2O", o_end[pi - 2])
                            glo = pi * TC
                            p.inc(s.dma_start(
                                vts2[:, pi % 2, :, :],
                                vtd[glo:glo + TC,
                                    h * DV:(h + 1) * DV].rearrange(
                                    "(c q) d -> q c d", q=32)),
                                f"p2Vs{pi % 2}~s", 16)
                        if si is not None:
                            p.waitb(s, "p2Ev", ev_end[si])
                            for i in range(2):
                                p.inc(s.dma_start(
                                    oTd[:, 2 * h + i,
                                        si * TC:(si + 1) * TC],
                                    osb[:, si % 2, i, :]),
                                    f"p2OD{si % 2}~s", 16)
        with nc.Block() as blk:
            @blk.sync
            def _(s):
                p.wait(s, "p2OD0~s")
                p.wait(s, "p2OD1~s")
        p.phase_reset()

    emit_ln(p, src=oTd, gnw=W["gnw"], gnb=W["gnb"], dst=p.XHD, dst_off=1,
            src_bf=True, src_off=0, mul_dram=gsd)
    p.phase_reset()
    with nc.sbuf_tensor(f"p3w{u}", [128, KT, D], BF16) as wout:
        with nc.Block() as blk:
            @blk.sync
            def _(s):
                p.inc(s.dma_start(wout[:],
                                  W["oWT_bf"].rearrange("(k q) m -> q k m",
                                                        q=128)), "p3W~s", 16)
                p.wait(s, "p3W~s")
        emit_proj_residual(p, p.XHD, 1, wout, bias=None, scale=1.0)
    p.phase_reset()



def convert_weights(p, conv_list, sem="!wconv"):
    nc = p.nc
    with nc.Block() as blk:
        @blk.gpsimd
        def _(g):
            for src, dst in conv_list:
                p.inc(g.dma_start(dst, src), sem, 16)


def load_weight_sb(p, s, sb, bf_dram, sem):
    p.inc(s.dma_start(sb[:], bf_dram.rearrange("(k p) m -> p k m", p=128)), sem, 16)


def load_col_vec(p, s, sb, f32_dram, sem):
    p.inc(s.dma_start(sb[:], f32_dram[:]), sem, 16)



def build_l0_ffn_program():
    nc = bass.Bass("TRN2", target_bir_lowering=False)
    p = Prog(nc)
    x_in = p.dram("x_in", [T, D], F32, kind="ExternalInput")
    wr = p.dram("wrT", [D, D], F32, kind="ExternalInput")
    wk = p.dram("wkT", [D, D], F32, kind="ExternalInput")
    wv = p.dram("wvT", [D, D], F32, kind="ExternalInput")
    mur = p.dram("mu_r", [128, KT], F32, kind="ExternalInput")
    muk = p.dram("mu_k", [128, KT], F32, kind="ExternalInput")
    x_out = p.dram("x_out", [T, D], F32, kind="ExternalOutput")
    wr_bf = p.dram("wr_bf", [D, D], BF16)
    wk_bf = p.dram("wk_bf", [D, D], BF16)
    wv_bf = p.dram("wv_bf", [D, D], BF16)

    with p.ctx:
        build_globals(p)
        W = {
            "wrT": p.sbuf("w_wrT", [128, KT, D], BF16),
            "wkT": p.sbuf("w_wkT", [128, KT, D], BF16),
            "wvT": p.sbuf("w_wvT", [128, KT, D], BF16),
            "mu_r": p.sbuf("w_mur", [128, KT], F32),
            "mu_k": p.sbuf("w_muk", [128, KT], F32),
        }
        convert_weights(p, [(wr[:], wr_bf[:]), (wk[:], wk_bf[:]),
                            (wv[:], wv_bf[:])])
        with nc.Block() as blk:
            @blk.sync
            def _(s):
                p.wait(s, "!wconv")
                load_weight_sb(p, s, W["wrT"], wr_bf, "!wload")
                load_weight_sb(p, s, W["wkT"], wk_bf, "!wload")
                load_weight_sb(p, s, W["wvT"], wv_bf, "!wload")
                load_col_vec(p, s, W["mu_r"], mur, "!wload")
                load_col_vec(p, s, W["mu_k"], muk, "!wload")
                p.wait(s, "!wload")
        emit_load_x(p, x_in)
        emit_ln(p)
        emit_l0_ffn(p, W, scale=0.5)
        emit_store_x(p, x_out)
    return nc


def build_l0_attn_program():
    nc = bass.Bass("TRN2", target_bir_lowering=False)
    p = Prog(nc)
    x_in = p.dram("x_in", [T, D], F32, kind="ExternalInput")
    wqkvT = p.dram("wqkvT", [D, 3072], F32, kind="ExternalInput")
    woutT = p.dram("woutT", [D, D], F32, kind="ExternalInput")
    bout = p.dram("b_out", [128, KT], F32, kind="ExternalInput")
    x_out = p.dram("x_out", [T, D], F32, kind="ExternalOutput")
    wqkv_bf = p.dram("wqkv_bf", [D, 3072], BF16)
    wout_bf = p.dram("wout_bf", [D, D], BF16)

    with p.ctx:
        build_globals(p)
        bout_sb = p.sbuf("bout_sb", [128, KT], F32)
        convert_weights(p, [(wqkvT[:], wqkv_bf[:]), (woutT[:], wout_bf[:])])
        with nc.Block() as blk:
            @blk.sync
            def _(s):
                p.wait(s, "!wconv")
                load_col_vec(p, s, bout_sb, bout, "!wload")
                p.wait(s, "!wload")
        emit_load_x(p, x_in)
        emit_ln(p)
        p.phase_reset()
        emit_mha(p, wqkv_bf, wout_bf, b_out=bout_sb, causal=False,
                 scale_qk=0.125)
        emit_store_x(p, x_out)
    return nc


def build_l3_attn_program():
    nc = bass.Bass("TRN2", target_bir_lowering=False)
    p = Prog(nc)
    x_in = p.dram("x_in", [T, D], F32, kind="ExternalInput")
    wqkvT = p.dram("wqkvT", [D, 3072], F32, kind="ExternalInput")
    wqkswT = p.dram("wqkswT", [D, 2048], F32, kind="ExternalInput")
    woutT = p.dram("woutT", [D, D], F32, kind="ExternalInput")
    ropet = p.dram("ropet", [2, 128, T], BF16, kind="ExternalInput")
    x_out = p.dram("x_out", [T, D], F32, kind="ExternalOutput")
    wqkv_bf = p.dram("wqkv_bf", [D, 3072], BF16)
    wqksw_bf = p.dram("wqksw_bf", [D, 2048], BF16)
    wout_bf = p.dram("wout_bf", [D, D], BF16)

    with p.ctx:
        build_globals(p)
        convert_weights(p, [(wqkvT[:], wqkv_bf[:]), (woutT[:], wout_bf[:]),
                            (wqkswT[:], wqksw_bf[:])])
        with nc.Block() as blk:
            @blk.sync
            def _(s):
                p.wait(s, "!wconv")
        emit_load_x(p, x_in)
        emit_ln(p)
        p.phase_reset()
        emit_mha(p, wqkv_bf, wout_bf, b_out=None, causal=True,
                 rope2_dram=ropet, qkswT_bf_dram=wqksw_bf, scale_qk=0.125)
        emit_store_x(p, x_out)
    return nc


def rope_tables(Tn=None):
    import ml_dtypes
    Tn = Tn or T
    r = np.arange(64)
    f = 10000.0 ** (-2.0 * (r // 2) / 64.0)
    t = np.arange(Tn)
    th = t[None, :] * f[:, None]
    cos = np.cos(th)
    sin = np.sin(th)
    sgn = np.where(r % 2 == 0, -1.0, 1.0)[:, None]
    tab = np.stack([cos, sgn * sin])
    tab = np.concatenate([tab, tab], axis=1)
    return tab.astype(ml_dtypes.bfloat16)


def build_l3_ffn_program():
    nc = bass.Bass("TRN2", target_bir_lowering=False)
    p = Prog(nc)
    x_in = p.dram("x_in", [T, D], F32, kind="ExternalInput")
    wg = p.dram("Wg", [D, 8], F32, kind="ExternalInput")
    we = p.dram("We", [8, D, D], F32, kind="ExternalInput")
    x_out = p.dram("x_out", [T, D], F32, kind="ExternalOutput")
    we_bf = p.dram("we_bf", [8, D, D], BF16)
    mrowd = p.dram("mrowd", [8, T], BF16)

    with p.ctx:
        build_globals(p)
        wg_sb = p.sbuf("wg_sb", [128, KT, 8], BF16)
        convert_weights(p, [(we[:], we_bf[:])])
        with nc.Block() as blk:
            @blk.gpsimd
            def _(g):
                p.inc(g.dma_start(wg_sb[:],
                                  wg.rearrange("(k p) m -> p k m", p=128)),
                      "!wlg", 16)
                p.wait(g, "!wlg")

            @blk.sync
            def _(s):
                p.wait(s, "!wconv")
        emit_load_x(p, x_in)
        emit_ln(p)
        p.phase_reset()
        emit_moe_router_l3(p, wg_sb, mrowd, scale=0.5)
        emit_moe_experts(p, we_bf, mrowd, relu=False)
        emit_store_x(p, x_out)
    return nc


def build_l2_ffn_program():
    nc = bass.Bass("TRN2", target_bir_lowering=False)
    p = Prog(nc)
    x_in = p.dram("x_in", [T, D], F32, kind="ExternalInput")
    wg = p.dram("Wg", [D, D], F32, kind="ExternalInput")
    bg = p.dram("bg", [1, D], F32, kind="ExternalInput")
    we = p.dram("We", [8, D, D], F32, kind="ExternalInput")
    be = p.dram("be", [8, D], F32, kind="ExternalInput")
    x_out = p.dram("x_out", [T, D], F32, kind="ExternalOutput")
    we_bf = p.dram("we_bf", [8, D, D], BF16)
    wg_bf = p.dram("wg_bf", [D, D], BF16)
    be_bf = p.dram("be_bf", [8, D], BF16)
    mrowd = p.dram("mrowd", [8, T], BF16)

    with p.ctx:
        build_globals(p)
        bg_sb = p.sbuf("bg_sb", [1, D], BF16)
        convert_weights(p, [(we[:], we_bf[:]), (wg[:], wg_bf[:]),
                            (be[:], be_bf[:])])
        with nc.Block() as blk:
            @blk.gpsimd
            def _(g):
                p.inc(g.dma_start(bg_sb[:], bg[:]), "!wlg", 16)
                p.wait(g, "!wlg")

            @blk.sync
            def _(s):
                p.wait(s, "!wconv")
        emit_load_x(p, x_in)
        emit_ln(p)
        p.phase_reset()
        emit_moe_router_l2(p, wg_bf, bg_sb, mrowd, scale=0.5)
        emit_moe_experts(p, we_bf, mrowd, relu=True, be_bf_dram=be_bf)
        emit_store_x(p, x_out)
    return nc


def build_l1_attn_program():
    nc = bass.Bass("TRN2", target_bir_lowering=False)
    p = Prog(nc)
    x_in = p.dram("x_in", [T, D], F32, kind="ExternalInput")
    wqkv = p.dram("wqkv", [D, 3072], F32, kind="ExternalInput")
    bq_c = p.dram("bq_c", [128, KT], F32, kind="ExternalInput")
    bkv = p.dram("bkv", [2, D], F32, kind="ExternalInput")
    x_out = p.dram("x_out", [T, D], F32, kind="ExternalOutput")
    wqkv_bf = p.dram("wqkv_bf", [D, 3072], BF16)
    bkv_bf = p.dram("bkv_bf", [2, D], BF16)

    with p.ctx:
        build_globals(p)
        bq_sb = p.sbuf("bq_sb", [128, KT], F32)
        convert_weights(p, [(wqkv[:], wqkv_bf[:]), (bkv[:], bkv_bf[:])])
        with nc.Block() as blk:
            @blk.gpsimd
            def _(g):
                p.inc(g.dma_start(bq_sb[:], bq_c[:]), "!wlg", 16)
                p.wait(g, "!wlg")

            @blk.sync
            def _(s):
                p.wait(s, "!wconv")
        emit_load_x(p, x_in)
        emit_ln(p)
        p.phase_reset()
        emit_l1_attn(p, wqkv_bf, bq_sb, bkv_bf)
        emit_store_x(p, x_out)
    return nc


def build_l1_ffn_program():
    nc = bass.Bass("TRN2", target_bir_lowering=False)
    p = Prog(nc)
    x_in = p.dram("x_in", [T, D], F32, kind="ExternalInput")
    rw = p.dram("rw", [D, 8], F32, kind="ExternalInput")
    w1 = p.dram("w1", [8, D, 2816], F32, kind="ExternalInput")
    w2 = p.dram("w2", [8, 1408, D], F32, kind="ExternalInput")
    x_out = p.dram("x_out", [T, D], F32, kind="ExternalOutput")
    w1_bf = p.dram("w1_bf", [8, D, 2816], BF16)
    w2_bf = p.dram("w2_bf", [8, 1408, D], BF16)
    mrowd = p.dram("mrowd", [8, T], BF16)

    with p.ctx:
        build_globals(p)
        wg_sb = p.sbuf("wg_sb", [128, KT, 8], BF16)
        convert_weights(p, [(w1[:], w1_bf[:]), (w2[:], w2_bf[:])])
        with nc.Block() as blk:
            @blk.gpsimd
            def _(g):
                p.inc(g.dma_start(wg_sb[:],
                                  rw.rearrange("(k q) m -> q k m", q=128)),
                      "!wlg", 16)
                p.wait(g, "!wlg")

            @blk.sync
            def _(s):
                p.wait(s, "!wconv")
        emit_load_x(p, x_in)
        emit_ln(p)
        p.phase_reset()
        emit_moe_router_l3(p, wg_sb, mrowd, scale=0.5, renorm=False)
        emit_l1_ffn(p, w1_bf, w2_bf, mrowd)
        emit_store_x(p, x_out)
    return nc


def load_l2_weights(p, ins_dram):
    nc = p.nc
    W = {}
    W["u_c"] = p.sbuf("l2w_u", [128, 4], F32)
    W["gnw"] = p.sbuf("l2w_gw", [128, KT], F32)
    W["gnb"] = p.sbuf("l2w_gn", [128, KT], F32)
    W["oWT_bf"] = ins_dram["oWT_bf"]
    W["_dram"] = ins_dram
    with nc.Block() as blk:
        @blk.gpsimd
        def _(g):
            for nm2 in ["u_c", "gnw", "gnb"]:
                p.inc(g.dma_start(W[nm2][:], ins_dram[nm2][:]), "!l2wl", 16)
            p.wait(g, "!l2wl")
    return W


def load_l2_p1_weights(p, W, c1):
    nc = p.nc
    ins_dram = W["_dram"]
    for nm2, key, shp, dt in [
            ("xp_mu", "l2w_mu", [128, KT], F32),
            ("xpW1T", "l2w_w1", [128, KT, 160], BF16),
            ("w2n", "l2w_w2", [32, 5, D], BF16),
            ("xbias", "l2w_xb", [1, 5 * D], BF16),
            ("rWT", "l2w_r", [128, KT, 512], BF16),
            ("wAT", "l2w_wa", [128, KT, 64], BF16),
            ("wBT", "l2w_wb", [64, 512], BF16),
            ("wb_c", "l2w_wbc", [128, 4], F32),
            ("kWT", "l2w_k", [128, KT, 512], BF16),
            ("vWT", "l2w_v", [128, KT, D], BF16),
            ("gAT", "l2w_ga", [128, KT, 64], BF16),
            ("gBT", "l2w_gb", [64, D], BF16),
            ("gb_c", "l2w_gbc", [128, KT], F32)]:
        W[nm2] = c1.enter_context(nc.sbuf_tensor(key, shp, dt))
    with nc.Block() as blk:
        @blk.gpsimd
        def _(g):
            for nm2, rearr in [
                    ("xp_mu", None), ("xpW1T", "kp"), ("w2n", None),
                    ("xbias", None), ("rWT", "kp"), ("wAT", "kp"),
                    ("wBT", None), ("wb_c", None), ("kWT", "kp"),
                    ("vWT", "kp"), ("gAT", "kp"), ("gBT", None),
                    ("gb_c", None)]:
                srcd = ins_dram[nm2]
                ap = (srcd.rearrange("(k q) m -> q k m", q=128)
                      if rearr == "kp" else srcd[:])
                p.inc(g.dma_start(W[nm2][:], ap), "!l2w1", 16)
            p.wait(g, "!l2w1")


def build_l2_attn_program():
    nc = bass.Bass("TRN2", target_bir_lowering=False)
    p = Prog(nc)
    x_in = p.dram("x_in", [T, D], F32, kind="ExternalInput")
    ins = {}
    for nm2, shp in [("xp_mu", [128, KT]), ("xpW1T", [D, 160]),
                     ("w2n", [32, 5, D]), ("xbias", [1, 5 * D]),
                     ("rWT", [D, 512]), ("wAT", [D, 64]), ("wBT", [64, 512]),
                     ("wb_c", [128, 4]), ("kWT", [D, 512]), ("vWT", [D, D]),
                     ("gAT", [D, 64]), ("gBT", [64, D]), ("gb_c", [128, KT]),
                     ("u_c", [128, 4]), ("gnw", [128, KT]),
                     ("gnb", [128, KT]), ("oWT", [D, D])]:
        ins[nm2] = p.dram(nm2, shp, F32, kind="ExternalInput")
    x_out = p.dram("x_out", [T, D], F32, kind="ExternalOutput")
    ins["oWT_bf"] = p.dram("oWT_bf", [D, D], BF16)

    with p.ctx:
        build_globals(p)
        convert_weights(p, [(ins["oWT"][:], ins["oWT_bf"][:])])
        W = load_l2_weights(p, ins)
        with nc.Block() as blk:
            @blk.sync
            def _(s):
                p.wait(s, "!wconv")
        emit_load_x(p, x_in)
        emit_ln(p)
        p.phase_reset()
        emit_l2_attn(p, W)
        emit_store_x(p, x_out)
    return nc


def build_megakernel_program():
    nc = bass.Bass("TRN2", target_bir_lowering=False)
    p = Prog(nc)
    x_in = p.dram("x_in", [128, KT, T], F32, kind="ExternalInput")
    x_out = p.dram("x_out", [128, KT, T], F32, kind="ExternalOutput")
    I = {}

    def din(name, shp, dt=F32):
        I[name] = p.dram(name, shp, dt, kind="ExternalInput")
        return I[name]

    din("l0f_wrT", [D, D]); din("l0f_wkT", [D, D]); din("l0f_wvT", [D, D])
    din("l0f_mur", [128, KT]); din("l0f_muk", [128, KT])
    din("l0a_wqkvT", [D, 3072]); din("l0a_woutT", [D, D])
    din("l0a_bout", [128, KT])
    din("l1f_rw", [D, 8]); din("l1f_w1", [8, D, 2816])
    din("l1f_w2", [8, 1408, D])
    din("l1a_wqkv", [D, 3072]); din("l1a_bq", [128, KT]); din("l1a_bkv", [2, D])
    din("l2f_wg", [D, D]); din("l2f_bg", [1, D]); din("l2f_we", [8, D, D])
    din("l2f_be", [8, D])
    l2a = {}
    for nm2, shp in [("xp_mu", [128, KT]), ("xpW1T", [D, 160]),
                     ("w2n", [32, 5, D]), ("xbias", [1, 5 * D]),
                     ("rWT", [D, 512]), ("wAT", [D, 64]), ("wBT", [64, 512]),
                     ("wb_c", [128, 4]), ("kWT", [D, 512]), ("vWT", [D, D]),
                     ("gAT", [D, 64]), ("gBT", [64, D]), ("gb_c", [128, KT]),
                     ("u_c", [128, 4]), ("gnw", [128, KT]),
                     ("gnb", [128, KT]), ("oWT", [D, D])]:
        l2a[nm2] = din("l2a_" + nm2, shp)
    din("l3f_wg", [D, 8]); din("l3f_we", [8, D, D])
    din("l3a_wqkvT", [D, 3072]); din("l3a_wqkswT", [D, 2048])
    din("l3a_woutT", [D, D])
    din("l3a_ropet", [2, 128, T], BF16)

    B = {}
    for nm2, shp in [("l0f_wrT", [D, D]), ("l0f_wkT", [D, D]),
                     ("l0f_wvT", [D, D]), ("l0a_wqkvT", [D, 3072]),
                     ("l0a_woutT", [D, D]),
                     ("l1f_w1", [8, D, 2816]), ("l1f_w2", [8, 1408, D]),
                     ("l1a_wqkv", [D, 3072]), ("l1a_bkv", [2, D]),
                     ("l2f_wg", [D, D]), ("l2f_we", [8, D, D]),
                     ("l2f_be", [8, D]), ("l2a_oWT", [D, D]),
                     ("l3f_we", [8, D, D]), ("l3a_wqkvT", [D, 3072]),
                     ("l3a_wqkswT", [D, 2048]), ("l3a_woutT", [D, D])]:
        B[nm2] = p.dram(nm2 + "_bf", shp, BF16)
    l2a["oWT_bf"] = B["l2a_oWT"]
    mrowd = p.dram("mrowd", [8, T], BF16)

    grp = {"!cv0": ["l0f_wrT", "l0f_wkT", "l0f_wvT", "l0a_wqkvT",
                    "l0a_woutT"],
           "!cv1": ["l1f_w1", "l1f_w2", "l1a_wqkv", "l1a_bkv"],
           "!cv2": ["l2f_wg", "l2f_we", "l2f_be", "l2a_oWT"],
           "!cv3": ["l3f_we", "l3a_wqkvT", "l3a_wqkswT", "l3a_woutT"]}

    def cv_gate(sem):
        with nc.Block() as blk:
            @blk.sync
            def _(s):
                p.wait(s, sem)

    with p.ctx:
        build_globals(p, xrd=x_out)
        bout_sb = p.sbuf("bout_sb", [128, KT], F32)
        wg1_sb = p.sbuf("wg1_sb", [128, KT, 8], BF16)
        bq_sb = p.sbuf("bq_sb", [128, KT], F32)
        wg3_sb = p.sbuf("wg3_sb", [128, KT, 8], BF16)
        with nc.Block() as blk:
            @blk.gpsimd
            def _(g):
                p.inc(g.dma_start(wg1_sb[:],
                                  I["l1f_rw"].rearrange("(k q) m -> q k m",
                                                        q=128)), "!wlg", 16)
                p.inc(g.dma_start(wg3_sb[:],
                                  I["l3f_wg"].rearrange("(k q) m -> q k m",
                                                        q=128)), "!wlg", 16)
                p.wait(g, "!wlg")

            @blk.sync
            def _(s):
                p.inc(s.dma_start(x_out[:], x_in[:]), "!xcp", 16)
                load_col_vec(p, s, bout_sb, I["l0a_bout"], "!wload")
                load_col_vec(p, s, bq_sb, I["l1a_bq"], "!wload")
                p.wait(s, "!wload")
        W2 = load_l2_weights(p, l2a)
        for sem in ["!cv0", "!cv1", "!cv2", "!cv3"]:
            convert_weights(p, [(I[nm2][:], B[nm2][:]) for nm2 in grp[sem]],
                            sem=sem)
        with nc.Block() as blk:
            @blk.sync
            def _(s):
                p.wait(s, "!xcp")

        cv_gate("!cv0")
        with ExitStack() as s0:
            W0 = {
                "wrT": s0.enter_context(
                    nc.sbuf_tensor("w_wrT", [128, KT, D], BF16)),
                "wkT": s0.enter_context(
                    nc.sbuf_tensor("w_wkT", [128, KT, D], BF16)),
                "wvT": s0.enter_context(
                    nc.sbuf_tensor("w_wvT", [128, KT, D], BF16)),
                "mu_r": s0.enter_context(
                    nc.sbuf_tensor("w_mur", [128, KT], F32)),
                "mu_k": s0.enter_context(
                    nc.sbuf_tensor("w_muk", [128, KT], F32)),
            }
            with nc.Block() as blk:
                @blk.sync
                def _(s):
                    load_weight_sb(p, s, W0["wrT"], B["l0f_wrT"], "!w0l")
                    load_weight_sb(p, s, W0["wkT"], B["l0f_wkT"], "!w0l")
                    load_weight_sb(p, s, W0["wvT"], B["l0f_wvT"], "!w0l")
                    load_col_vec(p, s, W0["mu_r"], I["l0f_mur"], "!w0l")
                    load_col_vec(p, s, W0["mu_k"], I["l0f_muk"], "!w0l")
                    p.wait(s, "!w0l")
            emit_ln(p)
            emit_l0_ffn(p, W0, scale=0.5)
            p.phase_reset()
            emit_ln(p)
            p.phase_reset()
            emit_mha(p, B["l0a_wqkvT"], B["l0a_woutT"], b_out=bout_sb,
                     causal=False, scale_qk=0.125)
            p.phase_reset()
            emit_ln(p)
            emit_l0_ffn(p, W0, scale=0.5)
            p.phase_reset()

        cv_gate("!cv1")
        for _rep in range(2):
            emit_ln(p)
            p.phase_reset()
            emit_moe_router_l3(p, wg1_sb, mrowd, scale=0.5, renorm=False)
            emit_l1_ffn(p, B["l1f_w1"], B["l1f_w2"], mrowd)
            p.phase_reset()
            if _rep == 0:
                emit_ln(p)
                p.phase_reset()
                emit_l1_attn(p, B["l1a_wqkv"], bq_sb, B["l1a_bkv"])
                p.phase_reset()

        cv_gate("!cv2")
        for _rep in range(2):
            emit_ln(p)
            p.phase_reset()
            with nc.sbuf_tensor(f"bg_sb{_rep}", [1, D], BF16) as bg_sb:
                with nc.Block() as blk:
                    @blk.gpsimd
                    def _(g):
                        p.inc(g.dma_start(bg_sb[:], I["l2f_bg"][:]),
                              "!wlg", 16)
                        p.wait(g, "!wlg")
                emit_moe_router_l2(p, B["l2f_wg"], bg_sb, mrowd, scale=0.5)
                emit_moe_experts(p, B["l2f_we"], mrowd, relu=True,
                                 be_bf_dram=B["l2f_be"])
                p.phase_reset()
            if _rep == 0:
                emit_ln(p)
                p.phase_reset()
                emit_l2_attn(p, W2)

        cv_gate("!cv3")
        for _rep in range(2):
            emit_ln(p)
            p.phase_reset()
            emit_moe_router_l3(p, wg3_sb, mrowd, scale=0.5)
            emit_moe_experts(p, B["l3f_we"], mrowd, relu=False)
            p.phase_reset()
            if _rep == 0:
                emit_ln(p)
                p.phase_reset()
                emit_mha(p, B["l3a_wqkvT"], B["l3a_woutT"], b_out=None,
                         causal=True, rope2_dram=I["l3a_ropet"],
                         qkswT_bf_dram=B["l3a_wqkswT"], scale_qk=0.125)
                p.phase_reset()
    return nc


def kernel(**inputs):
    from concourse.bass_utils import run_bass_kernel_spmd
    ca = np.ascontiguousarray
    ii = {k: np.asarray(v, dtype=np.float32) for k, v in inputs.items()}
    x = ca(ii["L0_attn_x"])
    perm = np.arange(D) ^ 1
    wqT = ii["L3_attn_wq"].T
    wkT = ii["L3_attn_wk"].T

    im = {
        "l0f_wrT": ca(ii["L0_ffn_W_r"].T),
        "l0f_wkT": ca(ii["L0_ffn_W_k"].T),
        "l0f_wvT": ca(ii["L0_ffn_W_v"].T),
        "l0f_mur": ca(ii["L0_ffn_mu_r"].reshape(8, 128).T),
        "l0f_muk": ca(ii["L0_ffn_mu_k"].reshape(8, 128).T),
        "l0a_wqkvT": ca(ii["L0_attn_w_qkv"].T),
        "l0a_woutT": ca(ii["L0_attn_w_out"].T),
        "l0a_bout": ca(ii["L0_attn_b_out"].reshape(8, 128).T),
        "l1f_rw": ca(ii["L1_ffn_router_weight"]),
        "l1f_w1": ca(ii["L1_ffn_w1"]),
        "l1f_w2": ca(ii["L1_ffn_w2"]),
        "l1a_wqkv": ca(np.concatenate([ii["L1_attn_Wq"], ii["L1_attn_Wk"],
                                       ii["L1_attn_Wv"]], axis=1)),
        "l1a_bq": ca(ii["L1_attn_bq"].reshape(8, 128).T),
        "l1a_bkv": ca(np.stack([ii["L1_attn_bk"], ii["L1_attn_bv"]])),
        "l2f_wg": ca(ii["L2_ffn_Wg"]),
        "l2f_bg": ca(ii["L2_ffn_bg"].reshape(1, D)),
        "l2f_we": ca(ii["L2_ffn_We"]),
        "l2f_be": ca(ii["L2_ffn_be"]),
        "l2a_xp_mu": ca(ii["L2_attn_xp_mu"].reshape(8, 128).T),
        "l2a_xpW1T": ca(ii["L2_attn_xp_W1"].T),
        "l2a_w2n": ca(ii["L2_attn_xp_W2"].reshape(D, 5, 32)
                      .transpose(2, 1, 0)),
        "l2a_xbias": ca(ii["L2_attn_x_bias"].reshape(1, 5 * D)),
        "l2a_rWT": ca(ii["L2_attn_r_W"].T),
        "l2a_wAT": ca(ii["L2_attn_w_A"].T),
        "l2a_wBT": ca(ii["L2_attn_w_B"].T),
        "l2a_wb_c": ca(ii["L2_attn_w_b"].reshape(4, 128).T),
        "l2a_kWT": ca(ii["L2_attn_k_W"].T),
        "l2a_vWT": ca(ii["L2_attn_v_W"].T),
        "l2a_gAT": ca(ii["L2_attn_g_A"].T),
        "l2a_gBT": ca(ii["L2_attn_g_B"].T),
        "l2a_gb_c": ca(ii["L2_attn_g_b"].reshape(8, 128).T),
        "l2a_u_c": ca(ii["L2_attn_bonus"].T),
        "l2a_gnw": ca(ii["L2_attn_gn_w"].reshape(8, 128).T),
        "l2a_gnb": ca(ii["L2_attn_gn_b"].reshape(8, 128).T),
        "l2a_oWT": ca(ii["L2_attn_o_W"].T),
        "l3f_wg": ca(ii["L3_ffn_Wg"]),
        "l3f_we": ca(ii["L3_ffn_We"]),
        "l3a_wqkvT": ca(np.concatenate([wqT, wkT, ii["L3_attn_wv"].T],
                                       axis=1)),
        "l3a_wqkswT": ca(np.concatenate([wqT[:, perm], wkT[:, perm]],
                                        axis=1)),
        "l3a_woutT": ca(ii["L3_attn_wo"].T),
        "l3a_ropet": rope_tables(),
    }

    nc = build_megakernel_program()
    in_maps = []
    for c in range(8):
        d = {"x_in": ca(x[c].T.reshape(8, 128, T).transpose(1, 0, 2))}
        d.update(im)
        in_maps.append(d)
    res = run_bass_kernel_spmd(nc, in_maps, core_ids=list(range(8)))
    out = []
    for c in range(8):
        xo = np.asarray(res.results[c]["x_out"], dtype=np.float32)
        out.append(xo.transpose(1, 0, 2).reshape(D, T).T)
    return np.stack(out)


# version 09 (final)
# speedup vs baseline: 1.2118x; 1.0357x over previous
import math
from contextlib import ExitStack

import numpy as np

import concourse.bass as bass
import concourse.mybir as mybir
from concourse.bass_utils import run_bass_kernel_spmd

F32 = mybir.dt.float32
BF16 = mybir.dt.bfloat16
U32 = mybir.dt.uint32
AF = mybir.ActivationFunctionType
ALU = mybir.AluOpType
AX = mybir.AxisListType

D = 1024
T = 4096
KT = 8
TC = 512
NCH = T // TC


class Prog:

    NPOOL = 64

    def __init__(self, nc):
        self.nc = nc
        self.ctx = ExitStack()
        self._persist = {}
        self._pool = []
        self._free_slots = ([], [], [])
        self._slot_counts = {}
        self._base = {}
        self._name2slot = {}
        self.counts = {}
        self._uid = 0

    def uid(self):
        self._uid += 1
        return self._uid

    @staticmethod
    def _kind(name):
        if "~g" in name:
            return 1
        if "~s" in name:
            return 2
        return 0

    def sem(self, name):
        if name.startswith("!"):
            if name not in self._persist:
                self._persist[name] = self.ctx.enter_context(
                    self.nc.semaphore(name.strip("!")))
                self.counts[name] = 0
                self._base[name] = 0
            return self._persist[name]
        if name not in self._name2slot:
            kind = self._kind(name)
            if self._free_slots[kind]:
                slot = self._free_slots[kind].pop()
            else:
                slot = len(self._pool)
                self._pool.append(self.ctx.enter_context(
                    self.nc.semaphore(f"pool{slot}")))
                assert slot < self.NPOOL, "semaphore pool exhausted"
            self._name2slot[name] = slot
            self.counts[name] = self._slot_counts.get(slot, 0)
            self._base[name] = self.counts[name]
        return self._pool[self._name2slot[name]]

    def phase_reset(self):
        if not self._name2slot:
            return
        items = [(n, self._pool[s]) for n, s in self._name2slot.items()]
        with self.nc.Block() as blk:
            @blk.gpsimd
            def _(g):
                for name, h in items:
                    g.wait_ge(h, self.counts[name])
        for name, slot in self._name2slot.items():
            self._slot_counts[slot] = self.counts[name]
            del self.counts[name]
            self._free_slots[self._kind(name)].append(slot)
        self._name2slot = {}

    def mark(self, name):
        self.sem(name)
        return self.counts[name]

    def inc(self, instr, name, n=1):
        instr.then_inc(self.sem(name), n)
        self.counts[name] += n
        return instr

    def wait(self, eng, name, val=None):
        eng.wait_ge(self.sem(name), self.counts[name] if val is None else val)

    def waitb(self, eng, name, off):
        self.sem(name)
        eng.wait_ge(self.sem(name), self._base[name] + off)

    def sbuf(self, name, shape, dtype):
        return self.ctx.enter_context(self.nc.sbuf_tensor(name, shape, dtype))

    def dram(self, name, shape, dtype, kind="Internal"):
        return self.nc.dram_tensor(name, shape, dtype, kind=kind)



def build_globals(p, xrd=None):
    nc = p.nc
    p.XRD = xrd if xrd is not None else p.dram("xrd", [128, KT, T], F32)
    p.XHD = p.dram("xhd", [128, KT, T + 1], BF16)
    p.ident_bf = p.sbuf("ident_bf", [128, 128], BF16)
    p.ident_f32 = p.sbuf("ident_f32", [128, 128], F32)
    p.ones_col_f32 = p.sbuf("ones_col_f32", [128, 8], F32)
    p.ones_col_bf = p.sbuf("ones_col_bf", [128, 8], BF16)
    p.ones_row_bf = p.sbuf("ones_row_bf", [1, 512], BF16)
    p.ones_row_f32 = p.sbuf("ones_row_f32", [1, 512], F32)
    p.sel8_f32 = p.sbuf("sel8_f32", [8, 128], F32)
    p.zero_col = p.sbuf("zero_col", [128, KT, 1], BF16)
    p.eps_col = p.sbuf("eps_col", [128, 1], F32)
    iota_f = p.sbuf("iota_f", [128, 128], F32)
    iota_p = p.sbuf("iota_p", [128, 1], F32)
    p.iota_p_g = iota_p
    p.iota8 = p.sbuf("iota8", [128, 8], F32)
    with nc.Block() as blk:
        @blk.gpsimd
        def _(g):
            g.iota(iota_f[:], [[1, 128]], channel_multiplier=0,
                   allow_small_or_imprecise_dtypes=True)
            g.iota(iota_p[:], [[0, 1]], channel_multiplier=1,
                   allow_small_or_imprecise_dtypes=True)
            g.iota(p.iota8[:], [[1, 8]], channel_multiplier=0,
                   allow_small_or_imprecise_dtypes=True)
            g.memset(p.ones_col_f32[:], 1.0)
            g.memset(p.ones_col_bf[:], 1.0)
            g.memset(p.ones_row_bf[:], 1.0)
            g.memset(p.ones_row_f32[:], 1.0)
            p.inc(g.memset(p.sel8_f32[:], 0.0), "!init", 1)
            p.wait(g, "!init")
            g.memset(p.sel8_f32[0:1, :], 1.0)
            g.memset(p.eps_col[:], 1e-5)
            p.inc(g.memset(p.zero_col[:], 0.0), "!init", 1)
    with nc.Block() as blk:
        @blk.vector
        def _(v):
            v.tensor_scalar(p.ident_bf[:], iota_f[:], iota_p[:, 0:1], None,
                            ALU.is_equal)
            v.tensor_scalar(p.ident_f32[:], iota_f[:], iota_p[:, 0:1], None,
                            ALU.is_equal)

        @blk.gpsimd
        def _(g):
            p.inc(g.dma_start(p.XHD[:, :, 0:1], p.zero_col[:],
                              allow_slow_non_contiguous=True), "!initd", 16)
            p.wait(g, "!initd")



def emit_load_x(p, x_dram):
    nc = p.nc
    u = p.uid()
    sD, sP, sE, sO = f"lx{u}D~s", f"lx{u}P", f"lx{u}E", f"lx{u}O~s"
    nrow = T // 128
    bP = p.mark(sP)
    bE = p.mark(sE)
    with nc.sbuf_tensor(f"lxst{u}", [128, 2, D], F32) as st, \
         nc.sbuf_tensor(f"lxot{u}", [128, 2, KT, 128], F32) as ot, \
         nc.psum_tensor(f"lxps0{u}", [128, 128], F32) as psA, \
         nc.psum_tensor(f"lxps1{u}", [128, 128], F32) as psB, \
         nc.Block() as blk:
        psl = [psA, psB]

        @blk.sync
        def _(s):
            for i in range(nrow):
                if i >= 2:
                    p.wait(s, sP, val=bP + 8 * (i - 1))
                p.inc(s.dma_start(st[:, i % 2, :],
                                  x_dram[i * 128:(i + 1) * 128, :]),
                      sD + str(i % 2), 16)
                p.wait(s, sE, val=bE + 8 * (i + 1))
                p.inc(s.dma_start(p.XRD[:, :, i * 128:(i + 1) * 128],
                                  ot[:, i % 2, :, :]), sO + str(i % 2), 16)

        @blk.tensor
        def _(t):
            for i in range(nrow):
                p.waitb(t, sD + str(i % 2), 16 * (i // 2 + 1))
                for k in range(KT):
                    if 8 * i + k >= 2:
                        p.wait(t, sE, val=bE + 8 * i + k - 1)
                    p.inc(t.transpose(psl[(8 * i + k) % 2][:],
                                      st[:, i % 2, k * 128:(k + 1) * 128],
                                      p.ident_f32[:]), sP, 1)

        @blk.vector
        def _(v):
            for i in range(nrow):
                if i >= 2:
                    p.waitb(v, sO + str(i % 2), 16 * (i // 2))
                for k in range(KT):
                    p.wait(v, sP, val=bP + 8 * i + k + 1)
                    p.inc(v.tensor_copy(ot[:, i % 2, k, :],
                                        psl[(8 * i + k) % 2][:]), sE, 1)

        @blk.gpsimd
        def _(g):
            p.wait(g, sO + "0")
            p.wait(g, sO + "1")


def emit_store_x(p, out_dram):
    nc = p.nc
    u = p.uid()
    sP, sE, sD, sI = f"sx{u}P", f"sx{u}E", f"sx{u}D~s", f"sx{u}I~"
    nrow = T // 128
    bP = p.mark(sP)
    bE = p.mark(sE)
    with nc.sbuf_tensor(f"sxst{u}", [128, 2, D], F32) as st, \
         nc.sbuf_tensor(f"sxin{u}", [128, 2, KT, 128], F32) as xin, \
         nc.psum_tensor(f"sxps0{u}", [128, 128], F32) as psA, \
         nc.psum_tensor(f"sxps1{u}", [128, 128], F32) as psB, \
         nc.Block() as blk:
        psl = [psA, psB]

        @blk.gpsimd
        def _(g):
            for i in range(nrow):
                if i >= 2:
                    p.wait(g, sP, val=bP + 8 * (i - 1))
                p.inc(g.dma_start(xin[:, i % 2, :, :],
                                  p.XRD[:, :, i * 128:(i + 1) * 128]),
                      sI + str(i % 2) + "~g", 16)

        @blk.tensor
        def _(t):
            for i in range(nrow):
                p.waitb(t, sI + str(i % 2) + "~g", 16 * (i // 2 + 1))
                for k in range(KT):
                    if 8 * i + k >= 2:
                        p.wait(t, sE, val=bE + 8 * i + k - 1)
                    p.inc(t.transpose(psl[(8 * i + k) % 2][:],
                                      xin[:, i % 2, k, :],
                                      p.ident_f32[:]), sP, 1)

        @blk.vector
        def _(v):
            for i in range(nrow):
                if i >= 2:
                    p.waitb(v, sD + str(i % 2), 16 * (i // 2))
                for k in range(KT):
                    p.wait(v, sP, val=bP + 8 * i + k + 1)
                    p.inc(v.tensor_copy(st[:, i % 2, k * 128:(k + 1) * 128],
                                        psl[(8 * i + k) % 2][:]), sE, 1)

        @blk.sync
        def _(s):
            for i in range(nrow):
                p.wait(s, sE, val=bE + 8 * (i + 1))
                p.inc(s.dma_start(out_dram[i * 128:(i + 1) * 128, :],
                                  st[:, i % 2, :]), sD + str(i % 2), 16)
            p.wait(s, sD + "0")
            p.wait(s, sD + "1")



def emit_ln(p, src=None, gnw=None, gnb=None, dst=None, dst_off=1,
            src_bf=False, src_off=0, mul_dram=None):
    nc = p.nc
    srcd = src if src is not None else p.XRD
    XHD = dst if dst is not None else p.XHD
    u = p.uid()
    with nc.sbuf_tensor(f"lnx{u}", [128, 4, KT, TC],
                        BF16 if src_bf else F32) as xrb, \
         nc.sbuf_tensor(f"lnsq{u}", [128, 2, KT, TC], BF16) as sqb, \
         nc.sbuf_tensor(f"lnxh{u}", [128, 2, KT, TC], BF16) as xhb, \
         nc.sbuf_tensor(f"lnsb{u}", [8, 2, 2, TC], F32) as ssb, \
         nc.sbuf_tensor(f"lnMb{u}", [128, 2, TC], F32) as Mb, \
         nc.sbuf_tensor(f"lnRb{u}", [128, 2, TC], F32) as Rb, \
         nc.sbuf_tensor(f"lnVb{u}", [128, 2, TC], F32) as Vb, \
         nc.psum_tensor(f"lnS0{u}", [8, TC], F32) as psS0, \
         nc.psum_tensor(f"lnS1{u}", [8, TC], F32) as psS1, \
         nc.psum_tensor(f"lnQ0{u}", [8, TC], F32) as psQ0, \
         nc.psum_tensor(f"lnQ1{u}", [8, TC], F32) as psQ1, \
         nc.psum_tensor(f"lnB10{u}", [128, TC], F32) as psB10, \
         nc.psum_tensor(f"lnB11{u}", [128, TC], F32) as psB11, \
         nc.psum_tensor(f"lnB20{u}", [128, TC], F32) as psB20, \
         nc.psum_tensor(f"lnB21{u}", [128, TC], F32) as psB21:
        psS = [psS0, psS1]
        psQ = [psQ0, psQ1]
        psB1 = [psB10, psB11]
        psB2 = [psB20, psB21]
        sA, sP, sV, sOut = "lnA", "lnP", "lnV", "lnO~s"
        if mul_dram is not None:
            mctx = nc.sbuf_tensor(f"lnm{u}", [128, 4, KT, TC], BF16)
            mrb = mctx.__enter__()
        npk = 2 + (1 if gnw is not None else 0) + \
            (1 if mul_dram is not None else 0)
        vc = ac = pc = 0
        a_sq = {}; a_sqrt = {}
        p_sum1 = {}; p_sum2 = {}; p_b1 = {}; p_b2 = {}
        v_cp1 = {}; v_cp2 = {}; v_mb = {}; v_vb = {}; v_end = {}
        for t2 in range(NCH + 2):
            if t2 < NCH:
                ac += 1; a_sq[t2] = ac
            if 0 <= t2 - 2 < NCH:
                ac += 1; a_sqrt[t2 - 2] = ac
            if t2 < NCH:
                pc += 1; p_sum1[t2] = pc
                pc += 1; p_sum2[t2] = pc
            if 0 <= t2 - 1 < NCH:
                pc += 1; p_b1[t2 - 1] = pc
                pc += 1; p_b2[t2 - 1] = pc
            if t2 < NCH:
                vc += 1; v_cp1[t2] = vc
                vc += 1; v_cp2[t2] = vc
            if 0 <= t2 - 1 < NCH:
                vc += 3; v_mb[t2 - 1] = vc - 2; v_vb[t2 - 1] = vc
            if 0 <= t2 - 2 < NCH:
                vc += 1 + npk * KT + 1
                v_end[t2 - 2] = vc

        with nc.Block() as blk:
            @blk.gpsimd
            def _(g):
                for c in range(NCH):
                    if c >= 4:
                        p.waitb(g, sV, v_end[c - 4])
                    lo = c * TC
                    p.inc(g.dma_start(
                        xrb[:, c % 4, :, :],
                        srcd[:, :, src_off + lo:src_off + lo + TC]),
                        f"lnX{c % 4}~g", 16)
                    if mul_dram is not None:
                        p.inc(g.dma_start(mrb[:, c % 4, :, :],
                                          mul_dram[:, :, lo:lo + TC]),
                              f"lnM{c % 4}~g", 16)

            @blk.scalar
            def _(a):
                for t2 in range(NCH + 2):
                    c, c2 = t2, t2 - 2
                    if c < NCH:
                        p.waitb(a, f"lnX{c % 4}~g", 16 * (c // 4 + 1))
                        if c >= 2:
                            p.waitb(a, sP, p_sum2[c - 2])
                        for k in range(KT):
                            ins = a.activation(sqb[:, c % 2, k, :],
                                               xrb[:, c % 4, k, :], AF.Square)
                        p.inc(ins, sA, 1)
                    if 0 <= c2 < NCH:
                        p.waitb(a, sV, v_vb[c2])
                        p.inc(a.activation(Vb[:, c2 % 2, :], Vb[:, c2 % 2, :],
                                           AF.Sqrt, bias=p.eps_col[:, 0:1]),
                              sA, 1)

            @blk.tensor
            def _(t):
                for t2 in range(NCH + 2):
                    c, c1 = t2, t2 - 1
                    if c < NCH:
                        p.waitb(t, f"lnX{c % 4}~g", 16 * (c // 4 + 1))
                        if c >= 2:
                            p.waitb(t, sV, v_cp1[c - 2])
                        for k in range(KT):
                            ins = t.matmul(psS[c % 2][:],
                                           p.ones_col_bf[:] if src_bf
                                           else p.ones_col_f32[:],
                                           xrb[:, c % 4, k, :],
                                           start=(k == 0), stop=(k == KT - 1))
                        p.inc(ins, sP, 1)
                        p.waitb(t, sA, a_sq[c])
                        if c >= 2:
                            p.waitb(t, sV, v_cp2[c - 2])
                        for k in range(KT):
                            ins = t.matmul(psQ[c % 2][:], p.ones_col_bf[:],
                                           sqb[:, c % 2, k, :],
                                           start=(k == 0), stop=(k == KT - 1))
                        p.inc(ins, sP, 1)
                    if 0 <= c1 < NCH:
                        p.waitb(t, sV, v_cp1[c1])
                        if c1 >= 2:
                            p.waitb(t, sV, v_mb[c1 - 2])
                        p.inc(t.matmul(psB1[c1 % 2][:], p.sel8_f32[:],
                                       ssb[:, c1 % 2, 0, :],
                                       start=True, stop=True), sP, 1)
                        p.waitb(t, sV, v_cp2[c1])
                        if c1 >= 2:
                            p.waitb(t, sV, v_vb[c1 - 2])
                        p.inc(t.matmul(psB2[c1 % 2][:], p.sel8_f32[:],
                                       ssb[:, c1 % 2, 1, :],
                                       start=True, stop=True), sP, 1)

            @blk.vector
            def _(v):
                for t2 in range(NCH + 2):
                    c, c1, c2 = t2, t2 - 1, t2 - 2
                    if c < NCH:
                        p.waitb(v, sP, p_sum1[c])
                        if c >= 2:
                            p.waitb(v, sP, p_b2[c - 2])
                        p.inc(v.tensor_copy(ssb[:, c % 2, 0, :],
                                            psS[c % 2][:]), sV, 1)
                        p.waitb(v, sP, p_sum2[c])
                        p.inc(v.tensor_copy(ssb[:, c % 2, 1, :],
                                            psQ[c % 2][:]), sV, 1)
                    if 0 <= c1 < NCH:
                        r = c1 % 2
                        p.waitb(v, sP, p_b1[c1])
                        if c1 >= 2:
                            p.waitb(v, sV, v_end[c1 - 2])
                        p.inc(v.tensor_scalar(Mb[:, r, :], psB1[r][:],
                                              1.0 / D, None, ALU.mult), sV, 1)
                        p.wait(v, sV)
                        p.inc(v.tensor_tensor(Rb[:, r, :], Mb[:, r, :],
                                              Mb[:, r, :], ALU.mult), sV, 1)
                        p.waitb(v, sP, p_b2[c1])
                        p.wait(v, sV)
                        p.inc(v.scalar_tensor_tensor(Vb[:, r, :], psB2[r][:],
                                                     1.0 / D, Rb[:, r, :],
                                                     ALU.mult, ALU.subtract),
                              sV, 1)
                    if 0 <= c2 < NCH:
                        r = c2 % 2
                        lo = c2 * TC
                        p.waitb(v, sA, a_sqrt[c2])
                        p.inc(v.reciprocal(Rb[:, r, :], Vb[:, r, :]), sV, 1)
                        p.wait(v, sV)
                        if c2 >= 2:
                            p.waitb(v, sOut, 16 * (c2 - 1))
                        for k in range(KT):
                            p.inc(v.tensor_tensor(Vb[:, r, :],
                                                  xrb[:, c2 % 4, k, :],
                                                  Mb[:, r, :], ALU.subtract),
                                  sV, 1)
                            p.wait(v, sV)
                            dsl = xhb[:, c2 % 2, k, :]
                            if gnw is not None:
                                p.inc(v.tensor_tensor(Vb[:, r, :], Vb[:, r, :],
                                                      Rb[:, r, :], ALU.mult),
                                      sV, 1)
                                p.wait(v, sV)
                                p.inc(v.tensor_scalar(dsl, Vb[:, r, :],
                                                      gnw[:, k:k + 1],
                                                      gnb[:, k:k + 1],
                                                      ALU.mult, ALU.add),
                                      sV, 1)
                            else:
                                p.inc(v.tensor_tensor(dsl, Vb[:, r, :],
                                                      Rb[:, r, :], ALU.mult),
                                      sV, 1)
                            p.wait(v, sV)
                            if mul_dram is not None:
                                if k == 0:
                                    p.waitb(v, f"lnM{c2 % 4}~g",
                                            16 * (c2 // 4 + 1))
                                p.inc(v.tensor_tensor(dsl, dsl,
                                                      mrb[:, c2 % 4, k, :],
                                                      ALU.mult), sV, 1)
                                p.wait(v, sV)
                        p.inc(v.engine_nop(), sV, 1)

            @blk.sync
            def _(s):
                for t2 in range(NCH + 2):
                    c2 = t2 - 2
                    if 0 <= c2 < NCH:
                        lo = c2 * TC
                        p.waitb(s, sV, v_end[c2])
                        p.inc(s.dma_start(
                            XHD[:, :, lo + dst_off:lo + TC + dst_off],
                            xhb[:, c2 % 2, :, :]), sOut, 16)
        with nc.Block() as blk:
            @blk.sync
            def _(s):
                p.wait(s, sOut)
        if mul_dram is not None:
            mctx.__exit__(None, None, None)



class ProjPlan:

    def __init__(self, p, name, nk, nm, nn, psums, lhsT_fn, rhs_fn):
        self.p, self.name, self.nk, self.nm, self.nn = p, name, nk, nm, nn
        self.psums = psums
        self.lhsT_fn, self.rhs_fn = lhsT_fn, rhs_fn
        self.sP, self.sE = name + "P", name + "E"

    def tensor_section(self, t, pre_wait=None):
        p = self.p
        nb = len(self.psums)
        idx = 0
        for m in range(self.nm):
            for n in range(self.nn):
                if pre_wait is not None:
                    pre_wait(t, m, n)
                if idx >= nb:
                    p.wait(t, self.sE, val=idx - nb + 1)
                ps = self.psums[idx % nb]
                for k in range(self.nk):
                    ins = t.matmul(ps, self.lhsT_fn(k, m), self.rhs_fn(k, n),
                                   start=(k == 0), stop=(k == self.nk - 1))
                p.inc(ins, self.sP, 1)
                idx += 1

    def evac_section(self, e, evac_fn):
        p = self.p
        nb = len(self.psums)
        idx = 0
        for m in range(self.nm):
            for n in range(self.nn):
                p.wait(e, self.sP, val=idx + 1)
                evac_fn(e, m, n, self.psums[idx % nb])
                idx += 1



def emit_l0_ffn(p, W, scale=0.5):
    nc = p.nc
    u = p.uid()
    C = 256
    nch = T // C
    with nc.sbuf_tensor(f"f0xh{u}", [128, 2, KT, C + 1], BF16) as xh, \
         nc.sbuf_tensor(f"f0xr2{u}", [128, 2, KT, C], F32) as xrb, \
         nc.sbuf_tensor(f"f0xr{u}", [128, KT, C], BF16) as xr, \
         nc.sbuf_tensor(f"f0xk{u}", [128, KT, C], BF16) as xk, \
         nc.sbuf_tensor(f"f0r{u}", [128, KT, C], BF16) as rr, \
         nc.sbuf_tensor(f"f0ka{u}", [128, KT, C], BF16) as ka, \
         nc.sbuf_tensor(f"f0t{u}", [128, C], F32) as tmpf, \
         nc.psum_tensor(f"f0p0{u}", [128, C], F32) as ps0, \
         nc.psum_tensor(f"f0p1{u}", [128, C], F32) as ps1, \
         nc.psum_tensor(f"f0p2{u}", [128, C], F32) as ps2, \
         nc.psum_tensor(f"f0p3{u}", [128, C], F32) as ps3:
        psl = [ps0, ps1, ps2, ps3]
        sLd, sMix, sPrk, sErk, sPv, sEv = ("f0Ld~s", "f0M", "f0P", "f0E",
                                           "f0VP", "f0VE")
        with nc.Block() as blk:
            @blk.sync
            def _(s):
                p.wait(s, "!initd")
                p.inc(s.dma_start(xh[:, 0, :, :], p.XHD[:, :, 0:C + 1]),
                      sLd + "0", 16)

            @blk.gpsimd
            def _(g):
                p.inc(g.dma_start(xrb[:, 0, :, :], p.XRD[:, :, 0:C]),
                      "f0X0~g", 16)
        f0o_uses = [0, 0]
        for n in range(nch):
            lo = n * C
            with nc.Block() as blk:
                bM = p.mark(sMix)
                bP = p.mark(sPrk); bE = p.mark(sErk)
                bPv = p.mark(sPv); bEv = p.mark(sEv)

                def erk_free(idx):
                    m2, wh = idx // 2, idx % 2
                    return 3 * m2 + 1 + wh

                @blk.sync
                def _(s):
                    if n + 1 < nch:
                        p.inc(s.dma_start(xh[:, (n + 1) % 2, :, :],
                                          p.XHD[:, :, lo + C:lo + 2 * C + 1]),
                              sLd + str((n + 1) % 2), 16)
                    for m in range(KT):
                        p.wait(s, sEv, val=bEv + 2 * (m + 1))
                        p.inc(s.dma_start(p.XRD[:, m, lo:lo + C],
                                          xrb[:, n % 2, m, :]),
                              "f0O" + str(n % 2) + "~s", 16)

                @blk.gpsimd
                def _(g):
                    if n + 1 < nch:
                        p.waitb(g, "f0O" + str((n + 1) % 2) + "~s",
                                16 * KT * f0o_uses[(n + 1) % 2])
                        p.inc(g.dma_start(xrb[:, (n + 1) % 2, :, :],
                                          p.XRD[:, :, lo + C:lo + 2 * C]),
                              "f0X" + str((n + 1) % 2) + "~g", 16)

                @blk.vector
                def _(v):
                    b = n % 2
                    p.waitb(v, sLd + str(n % 2), 16 * (n // 2 + 1))
                    for k in range(KT):
                        p.wait(v, sMix)
                        ins = v.tensor_tensor(tmpf[:], xh[:, b, k, 1:C + 1],
                                              xh[:, b, k, 0:C], ALU.subtract)
                        p.inc(ins, sMix, 1)
                        p.wait(v, sMix)
                        ins = v.scalar_tensor_tensor(xr[:, k, :], tmpf[:],
                                                     W["mu_r"][:, k:k + 1],
                                                     xh[:, b, k, 0:C],
                                                     ALU.mult, ALU.add)
                        p.inc(ins, sMix, 1)
                        ins = v.scalar_tensor_tensor(xk[:, k, :], tmpf[:],
                                                     W["mu_k"][:, k:k + 1],
                                                     xh[:, b, k, 0:C],
                                                     ALU.mult, ALU.add)
                        p.inc(ins, sMix, 1)
                    p.waitb(v, "f0X" + str(n % 2) + "~g", 16 * (n // 2 + 1))
                    for m in range(KT):
                        p.wait(v, sPv, val=bPv + m + 1)
                        p.wait(v, sErk, val=bE + 3 * m + 1)
                        p.wait(v, sEv)
                        ps = psl[m % 4]
                        ins = v.scalar_tensor_tensor(tmpf[:], ps[:], scale,
                                                     rr[:, m, :], ALU.mult,
                                                     ALU.mult)
                        p.inc(ins, sEv, 1)
                        p.wait(v, sEv)
                        ins = v.tensor_tensor(xrb[:, n % 2, m, :],
                                              xrb[:, n % 2, m, :],
                                              tmpf[:], ALU.add)
                        p.inc(ins, sEv, 1)

                @blk.tensor
                def _(t):
                    p.wait(t, sMix, val=bM + 3 * KT)
                    idx = 0
                    for m in range(KT):
                        for which in range(2):
                            w = W["wrT"] if which == 0 else W["wkT"]
                            srcb = xr if which == 0 else xk
                            if idx >= 4:
                                p.wait(t, sErk, val=bE + erk_free(idx - 4))
                            ps = psl[idx % 4]
                            for k in range(KT):
                                ins = t.matmul(ps[:],
                                               w[:, k, m * 128:(m + 1) * 128],
                                               srcb[:, k, :], start=(k == 0),
                                               stop=(k == KT - 1))
                            p.inc(ins, sPrk, 1)
                            idx += 1
                    for m in range(KT):
                        if m == 0:
                            p.wait(t, sErk, val=bE + 3 * KT)
                        if m >= 4:
                            p.wait(t, sEv, val=bEv + 2 * (m - 4) + 1)
                        ps = psl[m % 4]
                        for k in range(KT):
                            ins = t.matmul(ps[:],
                                           W["wvT"][:, k,
                                                    m * 128:(m + 1) * 128],
                                           ka[:, k, :], start=(k == 0),
                                           stop=(k == KT - 1))
                        p.inc(ins, sPv, 1)

                @blk.scalar
                def _(a):
                    idx = 0
                    for m in range(KT):
                        for which in range(2):
                            p.wait(a, sPrk, val=bP + idx + 1)
                            ps = psl[idx % 4]
                            if which == 0:
                                p.inc(a.activation(rr[:, m, :], ps[:],
                                                   AF.Sigmoid), sErk, 1)
                            else:
                                p.inc(a.activation(ka[:, m, :], ps[:],
                                                   AF.Relu), sErk, 1)
                                p.wait(a, sErk)
                                p.inc(a.activation(ka[:, m, :], ka[:, m, :],
                                                   AF.Square), sErk, 1)
                            idx += 1
                f0o_uses[n % 2] += 1
        with nc.Block() as blk:
            @blk.sync
            def _(s):
                p.wait(s, "f0O0~s")
                p.wait(s, "f0O1~s")



def emit_proj_residual(p, src_dram, src_off, wT, bias=None, scale=1.0,
                       nm=KT, dst=None):
    nc = p.nc
    u = p.uid()
    dstd = dst if dst is not None else p.XRD
    with nc.sbuf_tensor(f"pjxh{u}", [128, 2, KT, TC], BF16) as xh, \
         nc.sbuf_tensor(f"pjxr{u}", [128, 2, KT, TC], F32) as xrb, \
         nc.sbuf_tensor(f"pjt{u}", [128, TC], F32) as tmpf, \
         nc.psum_tensor(f"pjp0{u}", [128, TC], F32) as ps0, \
         nc.psum_tensor(f"pjp1{u}", [128, TC], F32) as ps1, \
         nc.psum_tensor(f"pjp2{u}", [128, TC], F32) as ps2, \
         nc.psum_tensor(f"pjp3{u}", [128, TC], F32) as ps3:
        psl = [ps0, ps1, ps2, ps3]
        sLd, sP, sE = "pjLd~s", "pjP", "pjE"
        with nc.Block() as blk:
            @blk.sync
            def _(s):
                p.inc(s.dma_start(xh[:, 0, :, :],
                                  src_dram[:, :, src_off:src_off + TC]),
                      sLd + "0", 16)

            @blk.gpsimd
            def _(g):
                p.inc(g.dma_start(xrb[:, 0, :, :], p.XRD[:, :, 0:TC]
                                  if dst is None else dstd[:, :, 0:TC]),
                      "pjX0~g", 16)
        pjo_uses = [0, 0]
        for n in range(NCH):
            lo = n * TC
            with nc.Block() as blk:
                bP = p.mark(sP); bE = p.mark(sE)

                @blk.sync
                def _(s):
                    if n + 1 < NCH:
                        p.inc(s.dma_start(
                            xh[:, (n + 1) % 2, :, :],
                            src_dram[:, :, src_off + lo + TC:src_off + lo + 2 * TC]),
                            sLd + str((n + 1) % 2), 16)
                    for m in range(nm):
                        p.wait(s, sE, val=bE + 2 * (m + 1))
                        p.inc(s.dma_start(dstd[:, m, lo:lo + TC],
                                          xrb[:, n % 2, m, :]),
                              "pjO" + str(n % 2) + "~s", 16)
                pjo_uses[n % 2] += 1

                @blk.gpsimd
                def _(g):
                    if n + 1 < NCH:
                        p.waitb(g, "pjO" + str((n + 1) % 2) + "~s",
                                16 * nm * pjo_uses[(n + 1) % 2])
                        p.inc(g.dma_start(xrb[:, (n + 1) % 2, :, :],
                                          dstd[:, :, lo + TC:lo + 2 * TC]),
                              "pjX" + str((n + 1) % 2) + "~g", 16)

                @blk.tensor
                def _(t):
                    p.waitb(t, sLd + str(n % 2), 16 * (n // 2 + 1))
                    for m in range(nm):
                        if m >= 4:
                            p.wait(t, sE, val=bE + 2 * (m - 4) + 1)
                        ps = psl[m % 4]
                        for k in range(KT):
                            ins = t.matmul(ps[:],
                                           wT[:, k, m * 128:(m + 1) * 128],
                                           xh[:, n % 2, k, :], start=(k == 0),
                                           stop=(k == KT - 1))
                        p.inc(ins, sP, 1)

                @blk.vector
                def _(v):
                    p.waitb(v, "pjX" + str(n % 2) + "~g", 16 * (n // 2 + 1))
                    for m in range(nm):
                        p.wait(v, sP, val=bP + m + 1)
                        p.wait(v, sE)
                        ps = psl[m % 4]
                        if bias is not None:
                            ins = v.tensor_scalar(tmpf[:], ps[:],
                                                  bias[:, m:m + 1], None, ALU.add)
                        else:
                            ins = v.tensor_copy(tmpf[:], ps[:])
                        p.inc(ins, sE, 1)
                        p.wait(v, sE)
                        ins = v.scalar_tensor_tensor(xrb[:, n % 2, m, :],
                                                     tmpf[:], scale,
                                                     xrb[:, n % 2, m, :],
                                                     ALU.mult, ALU.add)
                        p.inc(ins, sE, 1)



def emit_proj_to_staging(p, w_bf_dram, wcols, jobs, src_dram=None, src_off=1):
    nc = p.nc
    u = p.uid()
    srcd = src_dram if src_dram is not None else p.XHD
    NJ = len(jobs)
    with nc.sbuf_tensor(f"paw{u}", [128, KT, wcols], BF16) as wsb, \
         nc.sbuf_tensor(f"paxh{u}", [128, 2, KT, TC], BF16) as xh, \
         nc.sbuf_tensor(f"past{u}", [128, 4, TC], BF16) as stg, \
         nc.psum_tensor(f"pap0{u}", [128, TC], F32) as ps0, \
         nc.psum_tensor(f"pap1{u}", [128, TC], F32) as ps1, \
         nc.psum_tensor(f"pap2{u}", [128, TC], F32) as ps2, \
         nc.psum_tensor(f"pap3{u}", [128, TC], F32) as ps3:
        psl = [ps0, ps1, ps2, ps3]
        sP, sE = "paP", "paE"
        with nc.Block() as blk:
            @blk.sync
            def _(s):
                p.inc(s.dma_start(wsb[:],
                                  w_bf_dram.rearrange("(k p) m -> p k m", p=128)),
                      "paW~s", 16)
                p.inc(s.dma_start(xh[:, 0, :, :],
                                  srcd[:, :, src_off:src_off + TC]), "paLd0~s", 16)
        for n in range(NCH):
            lo = n * TC
            with nc.Block() as blk:
                bP = p.mark(sP); bE = p.mark(sE)

                @blk.sync
                def _(s):
                    if n + 1 < NCH:
                        p.inc(s.dma_start(
                            xh[:, (n + 1) % 2, :, :],
                            srcd[:, :, src_off + lo + TC:src_off + lo + 2 * TC]),
                            "paLd" + str((n + 1) % 2) + "~s", 16)
                    for idx, job in enumerate(jobs):
                        p.wait(s, sE, val=bE + idx + 1)
                        j = idx % 4
                        if job[0] == "cm":
                            _, _wm, dstd, dm = job
                            dap = dstd[:, dm, lo:lo + TC]
                        else:
                            _, _wo, dstd, tt, cc = job
                            dap = dstd[lo + tt * 128:lo + (tt + 1) * 128,
                                       cc * 512:(cc + 1) * 512]
                        p.inc(s.dma_start(dap, stg[:, j, :]), f"paD{j}~s", 16)

                @blk.tensor
                def _(t):
                    if n == 0:
                        p.wait(t, "paW~s")
                    p.waitb(t, "paLd" + str(n % 2) + "~s", 16 * (n // 2 + 1))
                    for idx, job in enumerate(jobs):
                        if idx >= 4:
                            p.wait(t, sE, val=bE + idx - 3)
                        ps = psl[idx % 4]
                        if job[0] == "cm":
                            wm = job[1]
                            for k in range(KT):
                                ins = t.matmul(
                                    ps[:], wsb[:, k, wm * 128:(wm + 1) * 128],
                                    xh[:, n % 2, k, :], start=(k == 0),
                                    stop=(k == KT - 1))
                        else:
                            wo, tt, cc = job[1], job[3], job[4]
                            for k in range(KT):
                                ins = t.matmul(
                                    ps[:], xh[:, n % 2, k, tt * 128:(tt + 1) * 128],
                                    wsb[:, k, wo + cc * 512:wo + (cc + 1) * 512],
                                    start=(k == 0), stop=(k == KT - 1))
                        p.inc(ins, sP, 1)

                @blk.vector
                def _(v):
                    nuse = (NJ + 3) // 4
                    for idx in range(NJ):
                        j = idx % 4
                        p.wait(v, sP, val=bP + idx + 1)
                        p.waitb(v, f"paD{j}~s", 16 * (nuse * n + idx // 4))
                        p.inc(v.tensor_copy(stg[:, j, :], psl[j][:]), sE, 1)

        with nc.Block() as blk:
            @blk.sync
            def _(s):
                for j in range(4):
                    if f"paD{j}~s" in p.counts:
                        p.wait(s, f"paD{j}~s")
        p.phase_reset()



def emit_rope_rotate(p, wswT_bf, dstd, rope2_sb):
    nc = p.nc
    u = p.uid()
    with nc.sbuf_tensor(f"rrw{u}", [128, KT, D], BF16) as wsb, \
         nc.sbuf_tensor(f"rrxh{u}", [128, 2, KT, TC], BF16) as xh, \
         nc.sbuf_tensor(f"rrqt{u}", [128, KT, TC], BF16) as qt, \
         nc.sbuf_tensor(f"rrt1{u}", [128, TC], F32) as t1, \
         nc.sbuf_tensor(f"rrt2{u}", [128, TC], F32) as t2, \
         nc.sbuf_tensor(f"rrst{u}", [128, 4, TC], BF16) as stg, \
         nc.psum_tensor(f"rrp0{u}", [128, TC], F32) as ps0, \
         nc.psum_tensor(f"rrp1{u}", [128, TC], F32) as ps1, \
         nc.psum_tensor(f"rrp2{u}", [128, TC], F32) as ps2, \
         nc.psum_tensor(f"rrp3{u}", [128, TC], F32) as ps3:
        psl = [ps0, ps1, ps2, ps3]
        sP, sE = "rrP", "rrE"
        with nc.Block() as blk:
            @blk.sync
            def _(s):
                p.inc(s.dma_start(wsb[:],
                                  wswT_bf.rearrange("(k p) m -> p k m", p=128)),
                      "rrW~s", 16)
                p.inc(s.dma_start(xh[:, 0, :, :], p.XHD[:, :, 1:1 + TC]),
                      "rrLd0~s", 16)
        for n in range(NCH):
            lo = n * TC
            with nc.Block() as blk:
                bP = p.mark(sP); bE = p.mark(sE)

                @blk.sync
                def _(s):
                    p.inc(s.dma_start(qt[:], dstd[:, :, lo:lo + TC]), "rrQ~s", 16)
                    if n + 1 < NCH:
                        p.inc(s.dma_start(
                            xh[:, (n + 1) % 2, :, :],
                            p.XHD[:, :, 1 + lo + TC:1 + lo + 2 * TC]),
                            "rrLd" + str((n + 1) % 2) + "~s", 16)
                    for m in range(KT):
                        p.wait(s, sE, val=bE + 3 * m + 3)
                        p.inc(s.dma_start(dstd[:, m, lo:lo + TC],
                                          stg[:, m % 4, :]), f"rrD{m % 4}~s", 16)

                @blk.tensor
                def _(t):
                    if n == 0:
                        p.wait(t, "rrW~s")
                    p.waitb(t, "rrLd" + str(n % 2) + "~s", 16 * (n // 2 + 1))
                    for m in range(KT):
                        if m >= 4:
                            p.wait(t, sE, val=bE + 3 * (m - 4) + 1)
                        ps = psl[m % 4]
                        for k in range(KT):
                            ins = t.matmul(ps[:], wsb[:, k, m * 128:(m + 1) * 128],
                                           xh[:, n % 2, k, :], start=(k == 0),
                                           stop=(k == KT - 1))
                        p.inc(ins, sP, 1)

                @blk.vector
                def _(v):
                    p.waitb(v, "rrQ~s", 16 * (n + 1))
                    for m in range(KT):
                        p.wait(v, sP, val=bP + m + 1)
                        p.wait(v, sE)
                        ins = v.tensor_tensor(t1[:], psl[m % 4][:],
                                              rope2_sb[:, 1, lo:lo + TC],
                                              ALU.mult)
                        p.inc(ins, sE, 1)
                        ins = v.tensor_tensor(t2[:], qt[:, m, :],
                                              rope2_sb[:, 0, lo:lo + TC],
                                              ALU.mult)
                        p.inc(ins, sE, 1)
                        p.wait(v, sE)
                        p.waitb(v, f"rrD{m % 4}~s", 16 * (2 * n + m // 4))
                        p.inc(v.tensor_tensor(stg[:, m % 4, :], t1[:], t2[:],
                                              ALU.add), sE, 1)

        with nc.Block() as blk:
            @blk.sync
            def _(s):
                for j in range(4):
                    p.wait(s, f"rrD{j}~s")
        p.phase_reset()



def emit_proj_to_staging(p, w_bf_dram, wcols, jobs, src_dram=None, src_off=1):
    nc = p.nc
    u = p.uid()
    srcd = src_dram if src_dram is not None else p.XHD
    NJ = len(jobs)
    with nc.sbuf_tensor(f"paw{u}", [128, KT, wcols], BF16) as wsb, \
         nc.sbuf_tensor(f"paxh{u}", [128, 2, KT, TC], BF16) as xh, \
         nc.sbuf_tensor(f"past{u}", [128, 4, TC], BF16) as stg, \
         nc.psum_tensor(f"pap0{u}", [128, TC], F32) as ps0, \
         nc.psum_tensor(f"pap1{u}", [128, TC], F32) as ps1, \
         nc.psum_tensor(f"pap2{u}", [128, TC], F32) as ps2, \
         nc.psum_tensor(f"pap3{u}", [128, TC], F32) as ps3:
        psl = [ps0, ps1, ps2, ps3]
        sP, sE = "paP", "paE"
        with nc.Block() as blk:
            @blk.sync
            def _(s):
                p.inc(s.dma_start(wsb[:],
                                  w_bf_dram.rearrange("(k p) m -> p k m", p=128)),
                      "paW~s", 16)
                p.inc(s.dma_start(xh[:, 0, :, :],
                                  srcd[:, :, src_off:src_off + TC]), "paLd0~s", 16)
        for n in range(NCH):
            lo = n * TC
            with nc.Block() as blk:
                bP = p.mark(sP); bE = p.mark(sE)

                @blk.sync
                def _(s):
                    if n + 1 < NCH:
                        p.inc(s.dma_start(
                            xh[:, (n + 1) % 2, :, :],
                            srcd[:, :, src_off + lo + TC:src_off + lo + 2 * TC]),
                            "paLd" + str((n + 1) % 2) + "~s", 16)
                    for idx, job in enumerate(jobs):
                        p.wait(s, sE, val=bE + idx + 1)
                        j = idx % 4
                        if job[0] == "cm":
                            dap = job[2][:, job[3], lo:lo + TC]
                        else:
                            tt, cc = job[3], job[4]
                            dap = job[2][lo + tt * 128:lo + (tt + 1) * 128,
                                         cc * 512:(cc + 1) * 512]
                        p.inc(s.dma_start(dap, stg[:, j, :]), f"paD{j}~s", 16)

                @blk.tensor
                def _(t):
                    if n == 0:
                        p.wait(t, "paW~s")
                    p.waitb(t, "paLd" + str(n % 2) + "~s", 16 * (n // 2 + 1))
                    for idx, job in enumerate(jobs):
                        if idx >= 4:
                            p.wait(t, sE, val=bE + idx - 3)
                        ps = psl[idx % 4]
                        if job[0] == "cm":
                            wm = job[1]
                            for k in range(KT):
                                ins = t.matmul(
                                    ps[:], wsb[:, k, wm * 128:(wm + 1) * 128],
                                    xh[:, n % 2, k, :], start=(k == 0),
                                    stop=(k == KT - 1))
                        else:
                            wo, tt, cc = job[1], job[3], job[4]
                            for k in range(KT):
                                ins = t.matmul(
                                    ps[:], xh[:, n % 2, k, tt * 128:(tt + 1) * 128],
                                    wsb[:, k, wo + cc * 512:wo + (cc + 1) * 512],
                                    start=(k == 0), stop=(k == KT - 1))
                        p.inc(ins, sP, 1)

                @blk.vector
                def _(v):
                    nuse = (NJ + 3) // 4
                    for idx in range(NJ):
                        j = idx % 4
                        p.wait(v, sP, val=bP + idx + 1)
                        prior = nuse * n + idx // 4
                        if prior > 0:
                            p.waitb(v, f"paD{j}~s", 16 * prior)
                        p.inc(v.tensor_copy(stg[:, j, :], psl[j][:]), sE, 1)

        with nc.Block() as blk:
            @blk.sync
            def _(s):
                for j in range(4):
                    if f"paD{j}~s" in p.counts:
                        p.wait(s, f"paD{j}~s")
        p.phase_reset()



def emit_mha(p, qkvT_bf_dram, w_outT_bf_dram, b_out=None, causal=False,
             rope2_dram=None, qkswT_bf_dram=None, scale_qk=0.125):
    nc = p.nc
    u = p.uid()
    H, HD = 16, 64
    qTd = p.dram(f"mhq{u}", [128, 8, T], BF16)
    kTd = p.dram(f"mhk{u}", [128, 8, T], BF16)
    vd = p.dram(f"mhv{u}", [T, D], BF16)
    oad = p.dram(f"mho{u}", [128, 8, T], BF16)
    rope = rope2_dram is not None

    jobs = [("cm", m, qTd, m) for m in range(8)]
    jobs += [("cm", 8 + m, kTd, m) for m in range(8)]
    jobs += [("tm", 2048, vd, tt, cc) for tt in range(4) for cc in range(2)]
    emit_proj_to_staging(p, qkvT_bf_dram, 3072, jobs)
    if rope:
        with nc.sbuf_tensor(f"rrtb{u}", [128, 2, T], BF16) as rope2_sb:
            with nc.Block() as blk:
                @blk.sync
                def _(s):
                    p.inc(s.dma_start(rope2_sb[:], rope2_dram.rearrange(
                        "c p t -> p c t")), "rrT~s", 16)
                    p.wait(s, "rrT~s")
            emit_rope_rotate(p, qkswT_bf_dram[:, 0:D], qTd, rope2_sb)
            emit_rope_rotate(p, qkswT_bf_dram[:, D:2 * D], kTd, rope2_sb)

    nkt = T // 128
    with ExitStack() as bctx:
        qb = bctx.enter_context(nc.sbuf_tensor(f"mbq{u}", [128, 2, T], BF16))
        kb = bctx.enter_context(nc.sbuf_tensor(f"mbk{u}", [128, 2, T], BF16))
        vh = bctx.enter_context(nc.sbuf_tensor(f"mbv{u}",
                                               [128, 2, nkt, HD + 1], BF16))
        E = bctx.enter_context(nc.sbuf_tensor(f"mbe{u}", [128, 2, nkt, TC],
                                              BF16))
        rd = bctx.enter_context(nc.sbuf_tensor(f"mbrd{u}", [1, 2, TC], F32))
        Rb = bctx.enter_context(nc.sbuf_tensor(f"mbrb{u}", [64, 2, TC], F32))
        osb = bctx.enter_context(nc.sbuf_tensor(f"mbo{u}", [64, 2, TC], BF16))
        if causal:
            cmask = bctx.enter_context(nc.sbuf_tensor(f"mbcm{u}", [128, 4, TC],
                                                      BF16))
            iota512 = bctx.enter_context(nc.sbuf_tensor(f"mbio{u}", [128, TC],
                                                        F32))
            ccol = bctx.enter_context(nc.sbuf_tensor(f"mbcc{u}", [128, 1], F32))
        psS0 = bctx.enter_context(nc.psum_tensor(f"mbs0{u}", [128, TC], F32))
        psS1 = bctx.enter_context(nc.psum_tensor(f"mbs1{u}", [128, TC], F32))
        psS2 = bctx.enter_context(nc.psum_tensor(f"mbs2{u}", [128, TC], F32))
        psS3 = bctx.enter_context(nc.psum_tensor(f"mbs3{u}", [128, TC], F32))
        psO = [bctx.enter_context(nc.psum_tensor(f"mbso{i}{u}", [65, TC],
                                                 F32)) for i in range(2)]
        psB = bctx.enter_context(nc.psum_tensor(f"mbsb{u}", [64, TC], F32))
        psSl = [psS0, psS1, psS2, psS3]

        with nc.Block() as blk:
            @blk.gpsimd
            def _(g):
                p.inc(g.memset(vh[:, 0, :, HD:HD + 1], 1.0), "mbI", 1)
                p.inc(g.memset(vh[:, 1, :, HD:HD + 1], 1.0), "mbI", 1)
                p.inc(g.memset(qb.ap()[64:128, :, :], 0.0), "mbI", 1)
                p.inc(g.memset(kb.ap()[64:128, :, :], 0.0), "mbI", 1)
                if causal:
                    p.inc(g.iota(iota512[:], [[1, TC]], channel_multiplier=0,
                                 allow_small_or_imprecise_dtypes=True), "mbI", 1)
            if causal:
                @blk.vector
                def _(v):
                    p.wait(v, "mbI", val=2)
                    for j in range(4):
                        p.inc(v.tensor_scalar_add(ccol[:], p.iota_p_g[:],
                                                  float(128 * j)), "mbI", 1)
                        p.wait(v, "mbI")
                        p.inc(v.tensor_scalar(cmask[:, j, :], iota512[:],
                                              ccol[:, 0:1], None, ALU.is_ge),
                              "mbI", 1)
                        p.wait(v, "mbI")

        sRdy, sAV, sRec, sBc, sRb, sEvO = ("mbRdy", "mbAV", "mbRec", "mbBc",
                                           "mbRb", "mbEv")
        iters = [(h, qc) for h in range(H) for qc in range(NCH)]
        NI = len(iters)
        ktrs = [list(range(min(nkt, 4 * (qc + 1)))) if causal
                else list(range(nkt)) for (h, qc) in iters]
        nks = [len(kk) for kk in ktrs]
        base_Sc = [0] * (NI + 1)
        for s2 in range(NI):
            base_Sc[s2 + 1] = base_Sc[s2] + nks[s2]
        mask_cnt = {}
        exp_waits = [[] for _ in range(NI)]
        last_E = {}
        mcnt = 0
        for s2, (h2, qc2) in enumerate(iters):
            for j2, kt2 in enumerate(ktrs[s2]):
                exp_waits[s2].append(list(last_E.get((kt2, s2 % 2), [])))
                acc = [(sAV, base_Sc[s2] + j2 + 1)]
                if causal and kt2 >= 4 * qc2:
                    mcnt += 1
                    mask_cnt[(s2, kt2)] = mcnt
                    acc.append(("mbMk2", mcnt))
                last_E[(kt2, s2 % 2)] = acc
        sc_end = [base_Sc[min((h2 + 1) * NCH, NI)] for h2 in range(H)]
        od_uses = [0, 0]

        with nc.Block() as blk:
            @blk.tensor
            def _(t):
                for sl in range(NI + 2):
                    si = sl if sl < NI else None
                    ai = sl - 1 if 0 <= sl - 1 < NI else None
                    bi = sl - 2 if 0 <= sl - 2 < NI else None
                    if si is not None and iters[si][1] == 0:
                        h2 = iters[si][0]
                        p.waitb(t, f"mbQL{h2 % 2}~s", 16 * (h2 // 2 + 1))
                        p.waitb(t, f"mbKL{h2 % 2}~s", 16 * (h2 // 2 + 1))
                    if ai is not None and iters[ai][1] == 0:
                        h2 = iters[ai][0]
                        p.waitb(t, f"mbVL{h2 % 2}~s", 16 * (h2 // 2 + 1))
                    n_s = nks[si] if si is not None else 0
                    n_a = nks[ai] if ai is not None else 0
                    for j in range(max(n_s, n_a)):
                        if ai is not None and j < n_a:
                            kt = ktrs[ai][j]
                            p.waitb(t, sRdy, base_Sc[ai] + j + 1)
                            if causal and (ai, kt) in mask_cnt:
                                p.waitb(t, "mbMk2", mask_cnt[(ai, kt)])
                            if j == 0 and ai >= 2:
                                p.waitb(t, sEvO, ai - 1)
                            ins = t.matmul(psO[ai % 2][:],
                                           vh[:, iters[ai][0] % 2, kt, :],
                                           E[:, ai % 2, kt, :],
                                           start=(j == 0),
                                           stop=(j == n_a - 1))
                            p.inc(ins, sAV, 1)
                        if si is not None and j < n_s:
                            kt = ktrs[si][j]
                            g = base_Sc[si] + j
                            if g >= 4:
                                p.waitb(t, sRdy, g - 3)
                            h2, qc2 = iters[si]
                            ins = t.matmul(
                                psSl[g % 4][:],
                                kb[:, h2 % 2, kt * 128:(kt + 1) * 128],
                                qb[:, h2 % 2, qc2 * TC:(qc2 + 1) * TC],
                                start=True, stop=True)
                            p.inc(ins, "mbSc", 1)
                    if bi is not None:
                        p.waitb(t, sRec, bi + 1)
                        p.waitb(t, sRb, bi)
                        p.inc(t.matmul(psB[:], p.ones_row_f32[0:1, 0:64],
                                       rd[0:1, bi % 2, :],
                                       start=True, stop=True), sBc, 1)

            @blk.scalar
            def _(a):
                for sl in range(NI + 2):
                    si = sl if sl < NI else None
                    ri = sl - 2 if 0 <= sl - 2 < NI else None
                    if si is not None:
                        for j, kt in enumerate(ktrs[si]):
                            g = base_Sc[si] + j
                            p.waitb(a, "mbSc", g + 1)
                            for semn, cnt in exp_waits[si][j]:
                                p.waitb(a, semn, cnt)
                            ins = a.activation(E[:, si % 2, kt, :],
                                               psSl[g % 4][:],
                                               AF.Exp, scale=scale_qk)
                            p.inc(ins, sRdy, 1)
                    if ri is not None:
                        p.waitb(a, sBc, ri + 1)
                        if ri >= 2:
                            p.waitb(a, sEvO, ri - 1)
                        p.inc(a.activation(Rb[:, ri % 2, :], psB[:], AF.Copy),
                              sRb, 1)

            @blk.vector
            def _(v):
                for sl in range(NI + 2):
                    si = sl if sl < NI else None
                    di = sl - 1 if 0 <= sl - 1 < NI else None
                    ei = sl - 2 if 0 <= sl - 2 < NI else None
                    if causal and si is not None:
                        h2, qc2 = iters[si]
                        for j, kt in enumerate(ktrs[si]):
                            if kt >= 4 * qc2:
                                p.waitb(v, sRdy, base_Sc[si] + j + 1)
                                p.inc(v.tensor_tensor(
                                    E[:, si % 2, kt, :], E[:, si % 2, kt, :],
                                    cmask[:, kt - 4 * qc2, :], ALU.mult),
                                    "mbMk2", 1)
                    if di is not None:
                        p.waitb(v, sAV, base_Sc[di] + nks[di])
                        if di >= 2:
                            p.waitb(v, sBc, di - 1)
                        p.inc(v.tensor_copy(rd[0:1, di % 2, :],
                                            psO[di % 2][64:65, :]), sRec, 1)
                    if ei is not None:
                        p.waitb(v, sRb, ei + 1)
                        p.inc(v.reciprocal(Rb[:, ei % 2, :], Rb[:, ei % 2, :]),
                              "mbRc", 1)
                        p.wait(v, "mbRc")
                        p.waitb(v, f"mbOD{ei % 2}~s", 16 * od_uses[ei % 2])
                        od_uses[ei % 2] += 1
                        ins = v.tensor_tensor(osb[:, ei % 2, :],
                                              psO[ei % 2][0:64, :],
                                              Rb[:, ei % 2, :], ALU.mult)
                        p.inc(ins, sEvO, 1)

            @blk.sync
            def _(s):
                for sl in range(NI + 2):
                    si = sl if sl < NI else None
                    st = sl - 2 if 0 <= sl - 2 < NI else None
                    if si is not None and iters[si][1] == 0:
                        h2 = iters[si][0]
                        hh, hp = h2 // 2, h2 % 2
                        if h2 >= 2:
                            p.waitb(s, "mbSc", sc_end[h2 - 2])
                            p.waitb(s, sAV, sc_end[h2 - 2])
                        base = 64 * hp
                        p.inc(s.dma_start(qb.ap()[0:64, h2 % 2, :],
                                          qTd[base:base + 64, hh, :]),
                              f"mbQL{h2 % 2}~s", 16)
                        p.inc(s.dma_start(kb.ap()[0:64, h2 % 2, :],
                                          kTd[base:base + 64, hh, :]),
                              f"mbKL{h2 % 2}~s", 16)
                        p.inc(s.dma_start(
                            vh[:, h2 % 2, :, 0:HD],
                            vd[:, h2 * HD:(h2 + 1) * HD].rearrange(
                                "(tt q) d -> q tt d", q=128)),
                            f"mbVL{h2 % 2}~s", 16)
                    if st is not None:
                        h3, qc3 = iters[st]
                        p.waitb(s, sEvO, st + 1)
                        p.inc(s.dma_start(
                            oad[64 * (h3 % 2):64 * (h3 % 2) + 64, h3 // 2,
                                qc3 * TC:(qc3 + 1) * TC],
                            osb[:, st % 2, :]), f"mbOD{st % 2}~s", 16)
        with nc.Block() as blk:
            @blk.sync
            def _(s):
                p.wait(s, "mbOD0~s")
                p.wait(s, "mbOD1~s")
        p.phase_reset()

    with nc.sbuf_tensor(f"mcw{u}", [128, KT, D], BF16) as wout:
        with nc.Block() as blk:
            @blk.sync
            def _(s):
                p.inc(s.dma_start(wout[:],
                                  w_outT_bf_dram.rearrange("(k p) m -> p k m",
                                                           p=128)), "mcW~s", 16)
                p.wait(s, "mcW~s")
        emit_proj_residual(p, oad, 0, wout, bias=b_out, scale=1.0)
        p.phase_reset()



def emit_moe_router_l3(p, wg_sb, mrowd, scale=0.5, renorm=True):
    nc = p.nc
    u = p.uid()
    with nc.sbuf_tensor(f"r3xh{u}", [128, 2, KT, TC], BF16) as xh, \
         nc.sbuf_tensor(f"r3lg{u}", [128, 8], F32) as lg, \
         nc.sbuf_tensor(f"r3mx{u}", [128, 8], F32) as mx, \
         nc.sbuf_tensor(f"r3sc{u}", [128, 6], F32) as scr, \
         nc.sbuf_tensor(f"r3el{u}", [128, 8], F32) as el, \
         nc.sbuf_tensor(f"r3m{u}", [128, 8], F32) as mm, \
         nc.sbuf_tensor(f"r3mr{u}", [8, 2, TC], BF16) as mrow, \
         nc.psum_tensor(f"r3pr{u}", [128, 8], F32) as psR, \
         nc.psum_tensor(f"r3pt{u}", [8, 4, 128], F32) as psT:
        sR, sV, sS, sTr, sMr = "r3R", "r3V", "r3S", "r3T", "r3M~s"
        VP = 15 if renorm else 10
        with nc.Block() as blk:
            @blk.sync
            def _(s):
                p.inc(s.dma_start(xh[:, 0, :, :], p.XHD[:, :, 1:1 + TC]),
                      "r3L0~s", 16)
        for n in range(NCH):
            lo = n * TC
            with nc.Block() as blk:
                bR = p.mark(sR); bV = p.mark(sV); bS = p.mark(sS)
                bT = p.mark(sTr); bM = p.mark(sMr)

                @blk.sync
                def _(s):
                    if n + 1 < NCH:
                        p.inc(s.dma_start(
                            xh[:, (n + 1) % 2, :, :],
                            p.XHD[:, :, 1 + lo + TC:1 + lo + 2 * TC]),
                            "r3L" + str((n + 1) % 2) + "~s", 16)

                for tt in range(4):
                    @blk.tensor
                    def _(t, tt=tt):
                        if tt == 0:
                            p.waitb(t, "r3L" + str(n % 2) + "~s", 16 * (n // 2 + 1))
                        if tt > 0:
                            p.wait(t, sV, val=bV + VP * (tt - 1) + 2)
                            p.wait(t, sS, val=bS + 2 * tt)
                        for k in range(KT):
                            ins = t.matmul(psR[:], xh[:, n % 2, k,
                                                      tt * 128:(tt + 1) * 128],
                                           wg_sb[:, k, :], start=(k == 0),
                                           stop=(k == KT - 1))
                        p.inc(ins, sR, 1)

                    @blk.vector
                    def _(v, tt=tt):
                        p.wait(v, sR, val=bR + tt + 1)
                        p.inc(v.tensor_copy(lg[:], psR[:]), sV, 1)
                        p.wait(v, sV)
                        p.inc(v.max(mx[:], lg[:]), sV, 1)
                        p.wait(v, sV)
                        p.inc(v.tensor_scalar_mul(scr[:, 0:1], mx[:, 0:1], -1.0),
                              sV, 1)
                        p.wait(v, sS, val=bS + 2 * tt + 2)
                        p.inc(v.reciprocal(scr[:, 2:3], scr[:, 1:2]), sV, 1)
                        p.wait(v, sV)
                        p.inc(v.tensor_mul(scr[:, 3:4], mx[:, 1:2], scr[:, 2:3]),
                              sV, 1)
                        if renorm:
                            p.inc(v.tensor_add(scr[:, 4:5], mx[:, 0:1],
                                               mx[:, 1:2]), sV, 1)
                            p.wait(v, sV)
                            p.inc(v.tensor_mul(scr[:, 4:5], scr[:, 4:5],
                                               scr[:, 2:3]), sV, 1)
                            p.wait(v, sV)
                            p.inc(v.tensor_scalar_add(scr[:, 4:5], scr[:, 4:5],
                                                      1e-9), sV, 1)
                            p.wait(v, sV)
                            p.inc(v.reciprocal(scr[:, 5:6], scr[:, 4:5]), sV, 1)
                            p.wait(v, sV)
                            p.inc(v.tensor_scalar_mul(scr[:, 5:6], scr[:, 5:6],
                                                      scale), sV, 1)
                        p.wait(v, sV)
                        p.inc(v.tensor_scalar(mm[:], el[:], scr[:, 2:3], None,
                                              ALU.mult), sV, 1)
                        p.wait(v, sV)
                        p.inc(v.tensor_scalar(el[:], mm[:], scr[:, 3:4], None,
                                              ALU.is_ge), sV, 1)
                        p.wait(v, sV)
                        p.inc(v.tensor_mul(mm[:], mm[:], el[:]), sV, 1)
                        p.wait(v, sV)
                        if renorm:
                            p.inc(v.tensor_scalar(mm[:], mm[:], scr[:, 5:6],
                                                  None, ALU.mult), sV, 1)
                        else:
                            p.inc(v.tensor_scalar_mul(mm[:], mm[:], scale),
                                  sV, 1)

                    @blk.scalar
                    def _(a, tt=tt):
                        p.wait(a, sV, val=bV + VP * tt + 3)
                        p.wait(a, sS, val=bS + 2 * tt)
                        p.inc(a.activation(el[:], lg[:], AF.Exp,
                                           bias=scr[:, 0:1],
                                           accum_out=scr[:, 1:2]), sS, 1)
                        p.wait(a, sS)
                        p.inc(a.activation(mx[:, 0:2], mx[:, 0:2], AF.Exp,
                                           bias=scr[:, 0:1]), sS, 1)

                    @blk.tensor
                    def _(t, tt=tt):
                        p.wait(t, sV, val=bV + VP * tt + VP - 1)
                        p.inc(t.transpose(psT[:, tt, :], mm[:], p.ident_f32[:]),
                              sTr, 1)

                    @blk.vector
                    def _(v, tt=tt):
                        p.wait(v, sTr, val=bT + tt + 1)
                        p.waitb(v, sMr, max(0, 16 * (n - 1)))
                        p.inc(v.tensor_copy(mrow[:, n % 2, tt * 128:(tt + 1) * 128],
                                            psT[:, tt, :]), sV, 1)

                @blk.sync
                def _(s):
                    p.wait(s, sV)
                    p.inc(s.dma_start(mrowd[:, lo:lo + TC], mrow[:, n % 2, :]),
                          sMr, 16)
        with nc.Block() as blk:
            @blk.sync
            def _(s):
                p.wait(s, sMr)
        p.phase_reset()


def emit_moe_router_l2(p, wg_bf_dram, bg_bf_sb, mrowd, scale=0.5):
    nc = p.nc
    u = p.uid()
    with nc.sbuf_tensor(f"r2w{u}", [128, KT, D], BF16) as wg, \
         nc.sbuf_tensor(f"r2xh{u}", [128, 2, KT, TC], BF16) as xh, \
         nc.sbuf_tensor(f"r2el{u}", [128, D], BF16) as el, \
         nc.sbuf_tensor(f"r2mx{u}", [128, 8], F32) as mx, \
         nc.sbuf_tensor(f"r2mi{u}", [128, 8], U32) as mi, \
         nc.sbuf_tensor(f"r2miu{u}", [128, 1], U32) as miu, \
         nc.sbuf_tensor(f"r2mif{u}", [128, 1], F32) as mif, \
         nc.sbuf_tensor(f"r2sc{u}", [128, 4], F32) as scr, \
         nc.sbuf_tensor(f"r2m{u}", [128, 8], F32) as mm, \
         nc.sbuf_tensor(f"r2mr{u}", [8, 2, TC], BF16) as mrow, \
         nc.psum_tensor(f"r2p0{u}", [128, D], F32) as psR, \
         nc.psum_tensor(f"r2pt{u}", [8, 4, 128], F32) as psT:
        sR, sV, sS, sTr, sMr = "r2R", "r2V", "r2S", "r2T", "r2M~s"
        with nc.Block() as blk:
            @blk.sync
            def _(s):
                p.inc(s.dma_start(wg[:],
                                  wg_bf_dram.rearrange("(k p) m -> p k m", p=128)),
                      "r2W~s", 16)
                p.inc(s.dma_start(xh[:, 0, :, :], p.XHD[:, :, 1:1 + TC]),
                      "r2L0~s", 16)
        for n in range(NCH):
            lo = n * TC
            with nc.Block() as blk:
                bR = p.mark(sR); bV = p.mark(sV); bS = p.mark(sS)
                bT = p.mark(sTr); bM = p.mark(sMr)

                @blk.sync
                def _(s):
                    if n + 1 < NCH:
                        p.inc(s.dma_start(
                            xh[:, (n + 1) % 2, :, :],
                            p.XHD[:, :, 1 + lo + TC:1 + lo + 2 * TC]),
                            "r2L" + str((n + 1) % 2) + "~s", 16)

                for tt in range(4):
                    @blk.tensor
                    def _(t, tt=tt):
                        if tt == 0:
                            p.wait(t, "r2W~s")
                            p.waitb(t, "r2L" + str(n % 2) + "~s", 16 * (n // 2 + 1))
                        if tt > 0:
                            p.wait(t, sS, val=bS + tt)
                            p.wait(t, sV, val=bV + 10 * (tt - 1) + 2)
                        for cc in range(2):
                            for k in range(KT):
                                ins = t.matmul(
                                    psR[:, cc * 512:(cc + 1) * 512],
                                    xh[:, n % 2, k, tt * 128:(tt + 1) * 128],
                                    wg[:, k, cc * 512:(cc + 1) * 512],
                                    start=(k == 0), stop=False)
                            ins = t.matmul(psR[:, cc * 512:(cc + 1) * 512],
                                           p.ones_row_bf[0:1, 0:128],
                                           bg_bf_sb[0:1, cc * 512:(cc + 1) * 512],
                                           start=False, stop=True)
                        p.inc(ins, sR, 1)

                    @blk.vector
                    def _(v, tt=tt):
                        p.wait(v, sR, val=bR + tt + 1)
                        p.inc(v.max(mx[:], psR[:]), sV, 1)
                        p.wait(v, sV)
                        p.inc(v.max_index(mi[:], mx[:], psR[:]), sV, 1)
                        p.inc(v.tensor_scalar_mul(scr[:, 0:1], mx[:, 0:1], -1.0),
                              sV, 1)
                        p.wait(v, sV)
                        p.inc(v.tensor_single_scalar(miu[:], mi[:, 0:1], 7,
                                                     ALU.bitwise_and), sV, 1)
                        p.wait(v, sV)
                        p.inc(v.tensor_copy(mif[:], miu[:]), sV, 1)
                        p.wait(v, sS, val=bS + tt + 1)
                        p.inc(v.reciprocal(scr[:, 1:2], scr[:, 2:3]), sV, 1)
                        p.wait(v, sV)
                        p.inc(v.tensor_scalar_mul(scr[:, 1:2], scr[:, 1:2],
                                                  scale), sV, 1)
                        p.wait(v, sV)
                        p.inc(v.tensor_scalar(mm[:], p.iota8[:], mif[:, 0:1],
                                              None, ALU.is_equal), sV, 1)
                        p.wait(v, sV)
                        p.inc(v.tensor_scalar(mm[:], mm[:], scr[:, 1:2], None,
                                              ALU.mult), sV, 1)

                    @blk.scalar
                    def _(a, tt=tt):
                        p.wait(a, sV, val=bV + 10 * tt + 3)
                        p.wait(a, sS, val=bS + tt)
                        p.inc(a.activation(el[:], psR[:], AF.Exp,
                                           bias=scr[:, 0:1],
                                           accum_out=scr[:, 2:3]), sS, 1)

                    @blk.tensor
                    def _(t, tt=tt):
                        p.wait(t, sV, val=bV + 10 * tt + 9)
                        p.inc(t.transpose(psT[:, tt, :], mm[:], p.ident_f32[:]),
                              sTr, 1)

                    @blk.vector
                    def _(v, tt=tt):
                        p.wait(v, sTr, val=bT + tt + 1)
                        p.waitb(v, sMr, max(0, 16 * (n - 1)))
                        p.inc(v.tensor_copy(mrow[:, n % 2, tt * 128:(tt + 1) * 128],
                                            psT[:, tt, :]), sV, 1)

                @blk.sync
                def _(s):
                    p.wait(s, sV)
                    p.inc(s.dma_start(mrowd[:, lo:lo + TC], mrow[:, n % 2, :]),
                          sMr, 16)
        with nc.Block() as blk:
            @blk.sync
            def _(s):
                p.wait(s, sMr)
        p.phase_reset()


def emit_moe_experts(p, we_bf_dram, mrowd, relu=False, be_bf_dram=None):
    nc = p.nc
    u = p.uid()
    with ExitStack() as ctx:
        xh = ctx.enter_context(nc.sbuf_tensor(f"ex{u}", [128, 2, KT, TC], BF16))
        xrb = ctx.enter_context(nc.sbuf_tensor(f"exr{u}", [128, 2, KT, TC], F32))
        xt = ctx.enter_context(nc.sbuf_tensor(f"et{u}", [128, 2, KT, TC], BF16))
        we = ctx.enter_context(nc.sbuf_tensor(f"ew{u}", [128, 2, KT, D], BF16))
        msb = ctx.enter_context(nc.sbuf_tensor(f"em{u}", [128, 2, TC], BF16))
        tmpf = ctx.enter_context(nc.sbuf_tensor(f"etf{u}", [128, TC], F32))
        if be_bf_dram is not None:
            be2 = ctx.enter_context(nc.sbuf_tensor(f"eb{u}", [1, 2, D], BF16))
        pss = [ctx.enter_context(nc.psum_tensor(f"ep{m}{u}", [128, TC], F32))
               for m in range(8)]
        sFv, sFg, sP, sEv, sRl = "eFv", "eFg", "eP", "eEv", "eRl"
        wl = [0, 0]
        ml = [0, 0]
        xl = [0, 0]

        def load_we(s, e_global, sl):
            p.inc(s.dma_start(we[:, sl, :, :],
                              we_bf_dram[e_global].rearrange(
                                  "(k p) m -> p k m", p=128)), f"eW{sl}~s", 16)
            wl[sl] += 1

        def load_m(s, e_global, n_global, sl):
            p.inc(s.dma_start(
                msb[:, sl, :],
                mrowd.ap()[e_global:e_global + 1,
                           n_global * TC:(n_global + 1) * TC]
                .broadcast_to((128, TC))), f"eM{sl}~s", 16)
            ml[sl] += 1
            if be_bf_dram is not None:
                p.inc(s.dma_start(be2[:, sl, :],
                                  be_bf_dram[e_global:e_global + 1, :]),
                      f"eB{sl}~s", 16)

        with nc.Block() as blk:
            @blk.sync
            def _(s):
                p.inc(s.dma_start(xh[:, 0, :, :], p.XHD[:, :, 1:1 + TC]),
                      "eL0~s", 16)
                xl[0] += 1
                load_we(s, 0, 0)
                load_m(s, 0, 0, 0)

            @blk.gpsimd
            def _(g):
                p.inc(g.dma_start(xrb[:, 0, :, :], p.XRD[:, :, 0:TC]),
                      "eX0~g", 16)
        exo_uses = [0, 0]
        for n in range(NCH):
            lo = n * TC
            with nc.Block() as blk:
                bFv = p.mark(sFv); bFg = p.mark(sFg)
                bP = p.mark(sP); bEv = p.mark(sEv)
                bRl = p.mark(sRl)

                @blk.sync
                def _(s):
                    if n + 1 < NCH:
                        p.inc(s.dma_start(
                            xh[:, (n + 1) % 2, :, :],
                            p.XHD[:, :, 1 + lo + TC:1 + lo + 2 * TC]),
                            "eL" + str((n + 1) % 2) + "~s", 16)
                        xl[(n + 1) % 2] += 1
                    for e in range(8):
                        ee, nn2 = (e + 1, n) if e + 1 < 8 else (0, n + 1)
                        if nn2 < NCH:
                            sl = (e + 1) % 2
                            p.wait(s, sP, val=bP + 8 * e)
                            load_we(s, ee, sl)
                            p.wait(s, sFv, val=bFv + 4 * e)
                            p.wait(s, sFg, val=bFg + 4 * e)
                            load_m(s, ee, nn2, sl)
                    for m in range(8):
                        p.wait(s, sEv if not relu else sRl,
                               val=(bEv + m + 1) if not relu
                               else (bRl + 2 * m + 2))
                        p.inc(s.dma_start(p.XRD[:, m, lo:lo + TC],
                                          xrb[:, n % 2, m, :]),
                              "eO" + str(n % 2) + "~s", 16)

                @blk.vector
                def _(v):
                    for e in range(8):
                        sl = e % 2
                        g_idx = 8 * n + e
                        p.waitb(v, f"eM{sl}~s", 16 * (g_idx // 2 + 1))
                        if e == 0:
                            p.waitb(v, "eL" + str(n % 2) + "~s", 16 * (n // 2 + 1))
                        if e >= 2:
                            p.wait(v, sP, val=bP + 8 * (e - 1))
                        for k in range(4):
                            p.inc(v.tensor_tensor(xt[:, sl, k, :],
                                                  xh[:, n % 2, k, :],
                                                  msb[:, sl, :], ALU.mult),
                                  sFv, 1)

                @blk.gpsimd
                def _(g):
                    if n + 1 < NCH:
                        p.waitb(g, "eO" + str((n + 1) % 2) + "~s",
                                16 * 8 * exo_uses[(n + 1) % 2])
                        p.inc(g.dma_start(xrb[:, (n + 1) % 2, :, :],
                                          p.XRD[:, :, lo + TC:lo + 2 * TC]),
                              "eX" + str((n + 1) % 2) + "~g", 16)
                    for e in range(8):
                        sl = e % 2
                        g_idx = 8 * n + e
                        p.waitb(g, f"eM{sl}~s", 16 * (g_idx // 2 + 1))
                        if e == 0:
                            p.waitb(g, "eL" + str(n % 2) + "~s", 16 * (n // 2 + 1))
                        if e >= 2:
                            p.wait(g, sP, val=bP + 8 * (e - 1))
                        for k in range(4, KT):
                            p.inc(g.tensor_tensor(xt[:, sl, k, :],
                                                  xh[:, n % 2, k, :],
                                                  msb[:, sl, :], ALU.mult),
                                  sFg, 1)

                @blk.tensor
                def _(t):
                    for e in range(8):
                        sl = e % 2
                        g_idx = 8 * n + e
                        p.wait(t, sFv, val=bFv + 4 * (e + 1))
                        p.wait(t, sFg, val=bFg + 4 * (e + 1))
                        p.waitb(t, f"eW{sl}~s", 16 * (g_idx // 2 + 1))
                        if e == 0:
                            if relu:
                                p.wait(t, sRl, val=bRl)
                            else:
                                p.wait(t, sEv, val=bEv)
                        for m in range(8):
                            for k in range(KT):
                                last = (e == 7 and k == KT - 1
                                        and be_bf_dram is None)
                                ins = t.matmul(
                                    pss[m][:],
                                    we[:, sl, k, m * 128:(m + 1) * 128],
                                    xt[:, sl, k, :],
                                    start=(e == 0 and k == 0), stop=last)
                            if be_bf_dram is not None:
                                if m == 0:
                                    p.waitb(t, f"eB{sl}~s", 16 * (g_idx // 2 + 1))
                                ins = t.matmul(
                                    pss[m][:],
                                    be2[0:1, sl, m * 128:(m + 1) * 128],
                                    msb[0:1, sl, :],
                                    start=False, stop=(e == 7))
                        p.inc(ins, sP, 8)

                if relu:
                    @blk.scalar
                    def _(a):
                        for m in range(8):
                            p.wait(a, sP, val=bP + 64)
                            if m > 0:
                                p.wait(a, sRl, val=bRl + 2 * m)
                            p.inc(a.activation(tmpf[:], pss[m][:], AF.Relu),
                                  sRl, 1)

                    @blk.vector
                    def _(v):
                        p.waitb(v, "eX" + str(n % 2) + "~g", 16 * (n // 2 + 1))
                        for m in range(8):
                            p.wait(v, sRl, val=bRl + 2 * m + 1)
                            p.inc(v.tensor_tensor(xrb[:, n % 2, m, :],
                                                  xrb[:, n % 2, m, :],
                                                  tmpf[:], ALU.add), sRl, 1)
                else:
                    @blk.vector
                    def _(v):
                        p.waitb(v, "eX" + str(n % 2) + "~g", 16 * (n // 2 + 1))
                        for m in range(8):
                            p.wait(v, sP, val=bP + 64)
                            p.inc(v.tensor_tensor(xrb[:, n % 2, m, :],
                                                  xrb[:, n % 2, m, :],
                                                  pss[m][:], ALU.add), sEv, 1)
                exo_uses[n % 2] += 1
        with nc.Block() as blk:
            @blk.sync
            def _(s):
                p.wait(s, "eO0~s")
                p.wait(s, "eO1~s")
        p.phase_reset()



def emit_l1_ffn(p, w1_bf, w2_bf, mrowd):
    nc = p.nc
    u = p.uid()
    FT = 11
    with ExitStack() as ctx:
        xh = ctx.enter_context(nc.sbuf_tensor(f"l1x{u}", [128, 2, KT, TC], BF16))
        xrb = ctx.enter_context(nc.sbuf_tensor(f"l1r{u}", [128, KT, TC], F32))
        w1 = ctx.enter_context(nc.sbuf_tensor(f"l1w1{u}", [128, 2, KT, 2816],
                                              BF16))
        w2 = ctx.enter_context(nc.sbuf_tensor(f"l1w2{u}", [128, 2, FT, D],
                                              BF16))
        msb = ctx.enter_context(nc.sbuf_tensor(f"l1m{u}", [128, 2, TC], BF16))
        t1 = ctx.enter_context(nc.sbuf_tensor(f"l1t{u}", [128, FT, TC], BF16))
        oac = ctx.enter_context(nc.sbuf_tensor(f"l1o{u}", [128, KT, TC], F32))
        etm = ctx.enter_context(nc.sbuf_tensor(f"l1e{u}", [128, 4, TC], F32))
        psl = [ctx.enter_context(nc.psum_tensor(f"l1p{i}{u}", [128, TC], F32))
               for i in range(4)]
        sP, sSi, sDv, sGa, sFin = "l1P", "l1Si", "l1Dv", "l1Ga", "l1F"
        sCp = "l1Cp"

        def load_e(s, eg, ng, sl):
            p.inc(s.dma_start(w1[:, sl, :, :],
                              w1_bf[eg].rearrange("(k q) m -> q k m", q=128)),
                  f"l1W{sl}~s", 16)
            p.inc(s.dma_start(w2[:, sl, :, :],
                              w2_bf[eg].rearrange("(k q) m -> q k m", q=128)),
                  f"l1V{sl}~s", 16)
            p.inc(s.dma_start(
                msb[:, sl, :],
                mrowd.ap()[eg:eg + 1, ng * TC:(ng + 1) * TC]
                .broadcast_to((128, TC))), f"l1M{sl}~s", 16)

        with nc.Block() as blk:
            @blk.sync
            def _(s):
                p.inc(s.dma_start(xh[:, 0, :, :], p.XHD[:, :, 1:1 + TC]),
                      "l1L0~s", 16)
                load_e(s, 0, 0, 0)
        for n in range(NCH):
            lo = n * TC
            with nc.Block() as blk:
                bP = p.mark(sP); bSi = p.mark(sSi); bDv = p.mark(sDv)
                bGa = p.mark(sGa); bFin = p.mark(sFin); bCp = p.mark(sCp)

                @blk.sync
                def _(s):
                    if n + 1 < NCH:
                        p.inc(s.dma_start(
                            xh[:, (n + 1) % 2, :, :],
                            p.XHD[:, :, 1 + lo + TC:1 + lo + 2 * TC]),
                            "l1L" + str((n + 1) % 2) + "~s", 16)
                    for e in range(8):
                        ee, nn2 = (e + 1, n) if e + 1 < 8 else (0, n + 1)
                        if nn2 < NCH:
                            p.wait(s, sP, val=bP + 30 * e)
                            p.wait(s, sDv, val=bDv + 26 * e)
                            load_e(s, ee, nn2, (e + 1) % 2)
                    for m in range(KT):
                        p.wait(s, sFin, val=bFin + m + 1)
                        p.inc(s.dma_start(p.XRD[:, m, lo:lo + TC],
                                          xrb[:, m, :]),
                              "l1O" + str(n % 2) + "~s", 16)

                @blk.gpsimd
                def _(g):
                    p.waitb(g, "l1O" + str(n % 2) + "~s",
                            16 * KT * (n // 2))
                    p.inc(g.dma_start(xrb[:], p.XRD[:, :, lo:lo + TC]),
                          "l1X~g", 16)
                    for e in range(8):
                        for j, m in enumerate(range(0, KT, 2)):
                            p.wait(g, sCp, val=bCp + 4 * e + j + 1)
                            p.wait(g, sGa, val=bGa + 4 * e)
                            if e == 0:
                                ins = g.tensor_copy(oac[:, m, :], etm[:, j, :])
                            else:
                                ins = g.tensor_tensor(oac[:, m, :], oac[:, m, :],
                                                      etm[:, j, :], ALU.add)
                            p.inc(ins, sGa, 1)

                @blk.tensor
                def _(t):
                    for e in range(8):
                        sl = e % 2
                        g_idx = 8 * n + e
                        p.waitb(t, f"l1W{sl}~s", 16 * (g_idx // 2 + 1))
                        if e == 0:
                            p.waitb(t, "l1L" + str(n % 2) + "~s",
                                    16 * (n // 2 + 1))
                        for f in range(FT):
                            if f >= 4:
                                p.wait(t, sSi, val=bSi + 11 * e + f - 3)
                            elif e > 0:
                                p.wait(t, sDv, val=bDv + 26 * e)
                                p.wait(t, sCp, val=bCp + 4 * e)
                            for k in range(KT):
                                ins = t.matmul(psl[f % 4][:],
                                               w1[:, sl, k, f * 128:(f + 1) * 128],
                                               xh[:, n % 2, k, :],
                                               start=(k == 0), stop=(k == KT - 1))
                            p.inc(ins, sP, 1)
                        for f in range(FT):
                            if f < 3:
                                p.wait(t, sSi, val=bSi + 11 * e + f + 9)
                            elif f == 3:
                                p.wait(t, sSi, val=bSi + 11 * e + 8)
                            else:
                                p.wait(t, sDv, val=bDv + 26 * e + f + 8)
                            for k in range(KT):
                                ins = t.matmul(
                                    psl[f % 4][:],
                                    w1[:, sl, k,
                                       1408 + f * 128:1408 + (f + 1) * 128],
                                    xh[:, n % 2, k, :],
                                    start=(k == 0), stop=(k == KT - 1))
                            p.inc(ins, sP, 1)
                        p.waitb(t, f"l1V{sl}~s", 16 * (g_idx // 2 + 1))
                        for m in range(KT):
                            p.wait(t, sDv, val=bDv + 26 * e + 22)
                            if m >= 4:
                                if (m - 4) % 2 == 0:
                                    p.wait(t, sCp,
                                           val=bCp + 4 * e + (m - 4) // 2 + 1)
                                else:
                                    p.wait(t, sDv,
                                           val=bDv + 26 * e + 22 + (m - 4) // 2 + 1)
                            for f in range(FT):
                                ins = t.matmul(psl[m % 4][:],
                                               w2[:, sl, f, m * 128:(m + 1) * 128],
                                               t1[:, f, :],
                                               start=(f == 0), stop=(f == FT - 1))
                            p.inc(ins, sP, 1)

                @blk.scalar
                def _(a):
                    for e in range(8):
                        if e > 0:
                            p.wait(a, sDv, val=bDv + 26 * e - 4)
                        for f in range(FT):
                            p.wait(a, sP, val=bP + 30 * e + f + 1)
                            p.wait(a, sSi, val=bSi + 11 * e + f)
                            p.inc(a.activation(t1[:, f, :], psl[f % 4][:],
                                               AF.Silu), sSi, 1)
                        for j, m in enumerate(range(0, KT, 2)):
                            p.wait(a, sP, val=bP + 30 * e + 22 + m + 1)
                            p.wait(a, sGa, val=bGa + 4 * e + j)
                            p.inc(a.activation(etm[:, j, :], psl[m % 4][:],
                                               AF.Copy), sCp, 1)

                @blk.vector
                def _(v):
                    for e in range(8):
                        sl = e % 2
                        g_idx = 8 * n + e
                        p.waitb(v, f"l1M{sl}~s", 16 * (g_idx // 2 + 1))
                        for f in range(FT):
                            p.wait(v, sSi, val=bSi + 11 * e + f + 1)
                            if e > 0 and f == 0:
                                p.wait(v, sP, val=bP + 30 * e)
                            p.inc(v.tensor_tensor(t1[:, f, :], t1[:, f, :],
                                                  msb[:, sl, :], ALU.mult),
                                  sDv, 1)
                        for f in range(FT):
                            p.wait(v, sP, val=bP + 30 * e + 11 + f + 1)
                            p.wait(v, sDv, val=bDv + 26 * e + f + 1)
                            p.inc(v.tensor_tensor(t1[:, f, :], psl[f % 4][:],
                                                  t1[:, f, :], ALU.mult),
                                  sDv, 1)
                        for m in range(1, KT, 2):
                            p.wait(v, sP, val=bP + 30 * e + 22 + m + 1)
                            p.wait(v, sDv, val=bDv + 26 * e + 22)
                            ps = psl[m % 4]
                            if e == 0:
                                ins = v.tensor_copy(oac[:, m, :], ps[:])
                            else:
                                ins = v.tensor_tensor(oac[:, m, :], oac[:, m, :],
                                                      ps[:], ALU.add)
                            p.inc(ins, sDv, 1)
                    p.waitb(v, "l1X~g", 16 * (n + 1))
                    p.wait(v, sDv, val=bDv + 26 * 8)
                    p.wait(v, sGa, val=bGa + 4 * 8)
                    for m in range(KT):
                        p.inc(v.tensor_tensor(xrb[:, m, :], xrb[:, m, :],
                                              oac[:, m, :], ALU.add), sFin, 1)
        with nc.Block() as blk:
            @blk.sync
            def _(s):
                p.wait(s, "l1O0~s")
                p.wait(s, "l1O1~s")
        p.phase_reset()



def emit_l1_attn(p, wqkvT_bf, bq_c_sb, bkv_bf):
    nc = p.nc
    u = p.uid()
    qTd = p.dram(f"laq{u}", [128, 8, T], BF16)
    ktm = p.dram(f"lak{u}", [T, D], BF16)
    vtm = p.dram(f"lav{u}", [T, D], BF16)
    g_ext = p.dram(f"lag{u}", [D, 2 * D], F32)
    s_ext = p.dram(f"las{u}", [3 * D], F32)
    s_shift = p.dram(f"lash{u}", [128, 2 * D], F32)

    jobs = [("cm", m, qTd, m) for m in range(8)]
    jobs += [("tm", D, ktm, tt, cc) for tt in range(4) for cc in range(2)]
    jobs += [("tm", 2 * D, vtm, tt, cc) for tt in range(4) for cc in range(2)]
    emit_proj_to_staging(p, wqkvT_bf, 3072, jobs)

    with ExitStack() as ctx:
        wkv = ctx.enter_context(nc.sbuf_tensor(f"lbw{u}", [128, KT, 2048],
                                               BF16))
        xhc = ctx.enter_context(nc.sbuf_tensor(f"lbx{u}", [128, 2, KT, TC],
                                               BF16))
        sxc = ctx.enter_context(nc.sbuf_tensor(f"lbs{u}", [128, 2, KT], F32))
        sxb = ctx.enter_context(nc.sbuf_tensor(f"lbsb{u}", [128, KT], BF16))
        rows = ctx.enter_context(nc.sbuf_tensor(f"lbr{u}", [1, 6, D], F32))
        rowb = ctx.enter_context(nc.sbuf_tensor(f"lbrb{u}", [1, 4, D], BF16))
        ps1 = ctx.enter_context(nc.psum_tensor(f"lbp1{u}", [1, TC], F32))
        ps2 = ctx.enter_context(nc.psum_tensor(f"lbp2{u}", [1, TC], F32))
        with nc.Block() as blk:
            @blk.sync
            def _(s):
                p.inc(s.dma_start(wkv[:], wqkvT_bf[:, D:3 * D].rearrange(
                    "(k q) m -> q k m", q=128)), "lbW~s", 16)
                p.inc(s.dma_start(xhc[:, 0, :, :], p.XHD[:, :, 1:1 + TC]),
                      "lbX0~s", 16)

            @blk.gpsimd
            def _(g):
                p.inc(g.dma_start(rowb[0:1, 0, :], bkv_bf[0:1, :]),
                      "lbB~g", 16)
                p.inc(g.dma_start(rowb[0:1, 1, :], bkv_bf[1:2, :]),
                      "lbB~g", 16)
        for n in range(NCH):
            with nc.Block() as blk:
                bV = p.mark("lbV")

                @blk.sync
                def _(s):
                    if n + 1 < NCH:
                        p.inc(s.dma_start(
                            xhc[:, (n + 1) % 2, :, :],
                            p.XHD[:, :, 1 + (n + 1) * TC:1 + (n + 2) * TC]),
                            "lbX" + str((n + 1) % 2) + "~s", 16)

                @blk.vector
                def _(v):
                    p.waitb(v, "lbX" + str(n % 2) + "~s", 16 * (n // 2 + 1))
                    for k in range(KT):
                        p.inc(v.tensor_reduce(sxc[:, 1, k:k + 1],
                                              xhc[:, n % 2, k, :],
                                              AX.X, ALU.add), "lbV", 1)
                        p.wait(v, "lbV")
                        if n == 0:
                            p.inc(v.tensor_copy(sxc[:, 0, k:k + 1],
                                                sxc[:, 1, k:k + 1]), "lbV", 1)
                        else:
                            p.inc(v.tensor_add(sxc[:, 0, k:k + 1],
                                               sxc[:, 0, k:k + 1],
                                               sxc[:, 1, k:k + 1]), "lbV", 1)
                        p.wait(v, "lbV")
        with nc.Block() as blk:
            @blk.vector
            def _(v):
                p.inc(v.tensor_copy(sxb[:], sxc[:, 0, :]), "lbV", 1)
                p.wait(v, "lbV")
        with nc.Block() as blk:
            bbP = p.mark("lbP"); bbE = p.mark("lbE")

            @blk.tensor
            def _(t):
                p.waitb(t, "lbW~s", 16)
                for cc in range(2):
                    if cc:
                        p.wait(t, "lbE", val=bbE + 2)
                    for k in range(KT):
                        ins = t.matmul(ps1[:], sxb[:, k:k + 1],
                                       wkv[:, k, cc * 512:(cc + 1) * 512],
                                       start=(k == 0), stop=(k == KT - 1))
                    p.inc(ins, "lbP", 1)
                    for k in range(KT):
                        ins = t.matmul(ps2[:], sxb[:, k:k + 1],
                                       wkv[:, k, D + cc * 512:D + (cc + 1) * 512],
                                       start=(k == 0), stop=(k == KT - 1))
                    p.inc(ins, "lbP", 1)

            @blk.vector
            def _(v):
                for cc in range(2):
                    p.wait(v, "lbP", val=bbP + 2 * cc + 2)
                    p.inc(v.tensor_copy(rows[0:1, 2, cc * 512:(cc + 1) * 512],
                                        ps1[:]), "lbE", 1)
                    p.inc(v.tensor_copy(rows[0:1, 3, cc * 512:(cc + 1) * 512],
                                        ps2[:]), "lbE", 1)
        with nc.Block() as blk:
            bbG = p.mark("lbG")

            @blk.gpsimd
            def _(g):
                p.wait(g, "lbB~g")
                p.inc(g.tensor_copy(rows[0:1, 0, :], rowb[0:1, 0, :]), "lbG", 1)
                p.inc(g.tensor_copy(rows[0:1, 1, :], rowb[0:1, 1, :]), "lbG", 1)

            @blk.vector
            def _(v):
                p.wait(v, "lbG", val=bbG + 2)
                p.inc(v.scalar_tensor_tensor(rows[0:1, 3, :], rows[0:1, 1, :],
                                             float(T), rows[0:1, 3, :],
                                             ALU.mult, ALU.add), "lbV", 1)
                p.wait(v, "lbV")
                p.inc(v.tensor_copy(rowb[0:1, 2, :], rows[0:1, 2, :]), "lbV", 1)
                p.inc(v.tensor_copy(rowb[0:1, 3, :], rows[0:1, 3, :]), "lbV", 1)
                p.wait(v, "lbV")

        NT = T // 128
        with ExitStack() as ctx2:
            vts = ctx2.enter_context(nc.sbuf_tensor(f"lgv{u}", [128, NT, TC],
                                                    BF16))
            kts = ctx2.enter_context(nc.sbuf_tensor(f"lgk{u}", [128, 2, NT, 128],
                                                    BF16))
            gsb = ctx2.enter_context(nc.sbuf_tensor(f"lgg{u}", [128, 2, TC],
                                                    F32))
            psG = [ctx2.enter_context(nc.psum_tensor(f"lgp{i}{u}", [128, TC],
                                                     F32)) for i in range(2)]
            for cc in range(2):
                with nc.Block() as blk:
                    bGP = p.mark("lgP"); bGE = p.mark("lgE")

                    @blk.sync
                    def _(s):
                        p.inc(s.dma_start(
                            vts[:],
                            vtm[:, cc * 512:(cc + 1) * 512].rearrange(
                                "(tt q) d -> q tt d", q=128)), "lgV~s", 16)
                        for jt in range(KT):
                            if jt == 0 and cc == 0:
                                p.inc(s.dma_start(
                                    kts[:, 0, :, :],
                                    ktm[:, 0:128].rearrange(
                                        "(tt q) d -> q tt d", q=128)), "lgK0~s",
                                    16)
                            jn = jt + 1
                            if jn < KT or cc == 0:
                                jn2 = jn % KT
                                p.wait(s, "lgP", val=bGP + max(0, 2 * jt))
                                p.inc(s.dma_start(
                                    kts[:, (jt + 1) % 2, :, :],
                                    ktm[:, jn2 * 128:(jn2 + 1) * 128].rearrange(
                                        "(tt q) d -> q tt d", q=128)),
                                    "lgK" + str((jt + 1) % 2) + "~s", 16)
                            p.wait(s, "lgE", val=bGE + jt + 1)
                            p.inc(s.dma_start(
                                g_ext[jt * 128:(jt + 1) * 128,
                                      cc * 512:(cc + 1) * 512],
                                gsb[:, jt % 2, :]), f"lgO{jt % 2}~s", 16)
                            p.inc(s.dma_start(
                                g_ext[jt * 128:(jt + 1) * 128,
                                      D + cc * 512:D + (cc + 1) * 512],
                                gsb[:, jt % 2, :]), f"lgO{jt % 2}~s", 16)

                    @blk.tensor
                    def _(t):
                        p.waitb(t, "lgV~s", 16 * (cc + 1))
                        for jt in range(KT):
                            g_i = 8 * cc + jt
                            p.waitb(t, "lgK" + str(jt % 2) + "~s",
                                    16 * (g_i // 2 + 1))
                            if jt >= 2:
                                p.wait(t, "lgE", val=bGE + jt - 1)
                            ps = psG[jt % 2]
                            for tt in range(NT):
                                ins = t.matmul(ps[:], kts[:, jt % 2, tt, :],
                                               vts[:, tt, :], start=(tt == 0),
                                               stop=False)
                            ins = t.matmul(ps[:], rowb[0:1, 0,
                                                       jt * 128:(jt + 1) * 128],
                                           rowb[0:1, 3, cc * 512:(cc + 1) * 512],
                                           start=False, stop=False)
                            ins = t.matmul(ps[:], rowb[0:1, 2,
                                                       jt * 128:(jt + 1) * 128],
                                           rowb[0:1, 1, cc * 512:(cc + 1) * 512],
                                           start=False, stop=True)
                            p.inc(ins, "lgP", 2)

                    @blk.vector
                    def _(v):
                        for jt in range(KT):
                            p.wait(v, "lgP", val=bGP + 2 * (jt + 1))
                            p.waitb(v, f"lgO{jt % 2}~s",
                                    32 * ((8 * cc + jt) // 2))
                            p.inc(v.tensor_copy(gsb[:, jt % 2, :],
                                                psG[jt % 2][:]), "lgE", 1)
            with nc.Block() as blk:
                @blk.sync
                def _(s):
                    p.wait(s, "lgO0~s")
                    p.wait(s, "lgO1~s")

        with ExitStack() as ctx3:
            rsb = ctx3.enter_context(nc.sbuf_tensor(f"lcr{u}", [128, 2, D],
                                                    F32))
            ssb = ctx3.enter_context(nc.sbuf_tensor(f"lcs{u}", [1, D], F32))
            psS = [ctx3.enter_context(nc.psum_tensor(f"lcp{i}{u}", [1, TC],
                                                     F32)) for i in range(2)]
            with nc.Block() as blk:
                bCP = p.mark("lcP"); bCE = p.mark("lcE")

                @blk.sync
                def _(s):
                    for jt in range(KT):
                        if jt >= 2:
                            p.wait(s, "lcP", val=bCP + 2 * (jt - 1))
                        ap = bass.AP(g_ext, jt * 128 * 2047 + D,
                                     [[2047, 128], [1, D]])
                        p.inc(s.dma_start(rsb[:, jt % 2, :], ap),
                              "lcR" + str(jt % 2) + "~s", 16)

                @blk.tensor
                def _(t):
                    for jt in range(KT):
                        p.waitb(t, "lcR" + str(jt % 2) + "~s",
                                16 * (jt // 2 + 1))
                        for cc in range(2):
                            ins = t.matmul(psS[cc][:], p.ones_col_f32[:, 0:1],
                                           rsb[:, jt % 2,
                                               cc * 512:(cc + 1) * 512],
                                           start=(jt == 0), stop=(jt == KT - 1))
                        p.inc(ins, "lcP", 2)

                @blk.vector
                def _(v):
                    p.wait(v, "lcP", val=bCP + 16)
                    p.inc(v.tensor_copy(ssb[0:1, 0:512], psS[0][:]), "lcE", 1)
                    p.inc(v.tensor_copy(ssb[0:1, 512:1024], psS[1][:]), "lcE", 1)

                @blk.sync
                def _(s2):
                    p.wait(s2, "lcE", val=bCE + 2)
                    p.inc(s2.dma_start(s_ext.ap()[None, 0:D], ssb[0:1, :]), "lcO~s", 16)
                    p.inc(s2.dma_start(s_ext.ap()[None, D:2 * D], ssb[0:1, :]), "lcO~s", 16)
                    p.inc(s2.dma_start(s_ext.ap()[None, 2 * D:3 * D],
                                        ssb[0:1, :]), "lcO~s", 16)
                    p.wait(s2, "lcO~s")
                    for r in range(128):
                        p.inc(s2.dma_start(s_shift[r:r + 1, :],
                                           s_ext.ap()[None, r:r + 2 * D]),
                              "lcS~s", 16)
                    p.wait(s2, "lcS~s")

        mt = ctx.enter_context(nc.sbuf_tensor(f"ldm{u}", [128, KT, D], BF16))
        bqb = ctx.enter_context(nc.sbuf_tensor(f"ldq{u}", [128, KT], BF16))
        obq = ctx.enter_context(nc.sbuf_tensor(f"ldo{u}", [1, D], F32))
        bias_sb = ctx.enter_context(nc.sbuf_tensor(f"ldb{u}", [128, KT], F32))
        psB = ctx.enter_context(nc.psum_tensor(f"ldp{u}", [1, TC], F32))
        psB2 = ctx.enter_context(nc.psum_tensor(f"ldp2{u}", [128, 1], F32))
        with nc.Block() as blk:
            @blk.gpsimd
            def _(g):
                for jt in range(KT):
                    p.inc(g.dma_start(mt[:, jt, :],
                                      s_shift[:, jt * 128:jt * 128 + D]),
                          "ldM~g", 16)

            @blk.vector
            def _(v):
                p.inc(v.tensor_copy(bqb[:], bq_c_sb[:]), "ldV", 1)
        with nc.Block() as blk:
            bDP = p.mark("ldP"); bDE = p.mark("ldE")

            @blk.tensor
            def _(t):
                p.waitb(t, "ldM~g", 16 * KT)
                for cc in range(2):
                    if cc:
                        p.wait(t, "ldE", val=bDE + 1)
                    for jt in range(KT):
                        ins = t.matmul(psB[:], bqb[:, jt:jt + 1],
                                       mt[:, jt, cc * 512:(cc + 1) * 512],
                                       start=(jt == 0), stop=(jt == KT - 1))
                    p.inc(ins, "ldP", 1)

            @blk.vector
            def _(v):
                for cc in range(2):
                    p.wait(v, "ldP", val=bDP + cc + 1)
                    p.inc(v.tensor_copy(obq[0:1, cc * 512:(cc + 1) * 512],
                                        psB[:]), "ldE", 1)
        with nc.Block() as blk:
            bDT = p.mark("ldT"); bDE2 = p.mark("ldE2")

            @blk.tensor
            def _(t):
                for m in range(KT):
                    if m >= 1:
                        p.wait(t, "ldE2", val=bDE2 + m)
                    p.inc(t.transpose(psB2[:], obq[0:1, m * 128:(m + 1) * 128],
                                      p.ident_f32[0:1, 0:1]), "ldT", 1)

            @blk.vector
            def _(v):
                for m in range(KT):
                    p.wait(v, "ldT", val=bDT + m + 1)
                    p.inc(v.tensor_copy(bias_sb[:, m:m + 1], psB2[:]), "ldE2", 1)

        emit_proj_residual(p, qTd, 0, mt, bias=bias_sb, scale=1.0)
    p.phase_reset()






def emit_l2_attn(p, W):
    nc = p.nc
    u = p.uid()
    H, DK, DV, CLEN = 4, 128, 256, 32
    NCK = T // CLEN
    rTd = p.dram(f"l2r{u}", [128, H, T], BF16)
    kTd = p.dram(f"l2k{u}", [128, H, T], BF16)
    eTd = p.dram(f"l2e{u}", [128, H, T], F32)
    vtd = p.dram(f"l2v{u}", [T, D], BF16)
    gsd = p.dram(f"l2g{u}", [128, KT, T], BF16)
    oTd = p.dram(f"l2o{u}", [128, KT, T], BF16)

    with ExitStack() as c1:
        load_l2_p1_weights(p, W, c1)
        xh = c1.enter_context(nc.sbuf_tensor(f"p1x{u}", [128, 2, KT, TC + 1],
                                             BF16))
        dd = c1.enter_context(nc.sbuf_tensor(f"p1d{u}", [128, KT, TC], BF16))
        mxp = c1.enter_context(nc.sbuf_tensor(f"p1m{u}", [128, KT, TC], BF16))
        xl5 = c1.enter_context(nc.sbuf_tensor(f"p1l{u}", [32, 5, TC], BF16))
        xm = c1.enter_context(nc.sbuf_tensor(f"p1xm{u}", [128, 5, KT, TC],
                                             BF16))
        wt64 = c1.enter_context(nc.sbuf_tensor(f"p1w6{u}", [64, 2, TC], BF16))
        stR = c1.enter_context(nc.sbuf_tensor(f"p1sr{u}", [128, 4, TC], BF16))
        stV = c1.enter_context(nc.sbuf_tensor(f"p1sv{u}", [128, 4, TC], BF16))
        stG = c1.enter_context(nc.sbuf_tensor(f"p1sg{u}", [128, 2, TC], BF16))
        stE = c1.enter_context(nc.sbuf_tensor(f"p1se{u}", [128, 2, TC], F32))
        tmp2 = c1.enter_context(nc.sbuf_tensor(f"p1t{u}", [128, 2, TC], F32))
        psXl = [c1.enter_context(nc.psum_tensor(f"p1pl{i}{u}", [32, TC], F32))
                for i in range(2)]
        ps64 = c1.enter_context(nc.psum_tensor(f"p1p6{u}", [64, TC], F32))
        psW = [c1.enter_context(nc.psum_tensor(f"p1pw{i}{u}", [128, TC], F32))
               for i in range(2)]
        psl = [c1.enter_context(nc.psum_tensor(f"p1p{i}{u}", [128, TC], F32))
               for i in range(3)]
        with nc.Block() as blk:
            @blk.sync
            def _(s):
                p.wait(s, "!initd")
                p.inc(s.dma_start(xh[:, 0, :, :], p.XHD[:, :, 0:TC + 1]),
                      "p1L0~s", 16)
        du = {k2: 0 for k2 in ["R0", "R1", "R2", "R3", "V0", "V1", "V2", "V3",
                               "G0", "G1", "E0", "E1"]}
        for n in range(NCH):
            lo = n * TC
            with nc.Block() as blk:
                bM = p.mark("sMix"); bLP = p.mark("sXlP"); bLE = p.mark("sXlE")
                bXP = p.mark("sX5P"); bXa = p.mark("sXa"); bXb = p.mark("sXb")
                bPP = p.mark("sPP"); bRKE = p.mark("sRKE"); bVE = p.mark("sVE")
                bGE = p.mark("sGE"); bW1 = p.mark("sW1"); bWT = p.mark("sWT")
                bEE = p.mark("sEE")
                du0 = dict(du)

                @blk.sync
                def _(s):
                    if n + 1 < NCH:
                        p.inc(s.dma_start(
                            xh[:, (n + 1) % 2, :, :],
                            p.XHD[:, :, lo + TC:lo + 2 * TC + 1]),
                            "p1L" + str((n + 1) % 2) + "~s", 16)
                    for i in range(8):
                        p.wait(s, "sRKE", val=bRKE + i + 1)
                        dstd = rTd if i < 4 else kTd
                        p.inc(s.dma_start(dstd[:, i % 4, lo:lo + TC],
                                          stR[:, i % 4, :]),
                              f"p1R{i % 4}~s", 16)
                        du[f"R{i % 4}"] += 1
                    for i in range(4):
                        p.wait(s, "sEE", val=bEE + i + 1)
                        p.inc(s.dma_start(eTd[:, i, lo:lo + TC],
                                          stE[:, i % 2, :]),
                              f"p1E{i % 2}~s", 16)
                        du[f"E{i % 2}"] += 1
                    for i in range(8):
                        p.wait(s, "sVE", val=bVE + i + 1)
                        tt, cc = i // 2, i % 2
                        p.inc(s.dma_start(
                            vtd[lo + tt * 128:lo + (tt + 1) * 128,
                                cc * 512:(cc + 1) * 512],
                            stV[:, i % 4, :]), f"p1V{i % 4}~s", 16)
                        du[f"V{i % 4}"] += 1
                    for i in range(8):
                        p.wait(s, "sGE", val=bGE + i + 1)
                        p.inc(s.dma_start(gsd[:, i, lo:lo + TC],
                                          stG[:, i % 2, :]),
                              f"p1G{i % 2}~s", 16)
                        du[f"G{i % 2}"] += 1

                @blk.vector
                def _(v):
                    p.waitb(v, "p1L" + str(n % 2) + "~s", 16 * (n // 2 + 1))
                    for k in range(KT):
                        p.wait(v, "sMix")
                        p.inc(v.tensor_tensor(dd[:, k, :],
                                              xh[:, n % 2, k, 0:TC],
                                              xh[:, n % 2, k, 1:TC + 1],
                                              ALU.subtract), "sMix", 1)
                        p.wait(v, "sMix")
                        p.inc(v.scalar_tensor_tensor(mxp[:, k, :], dd[:, k, :],
                                                     W["xp_mu"][:, k:k + 1],
                                                     xh[:, n % 2, k, 1:TC + 1],
                                                     ALU.mult, ALU.add),
                              "sMix", 1)
                    for idx in range(40):
                        ns, m = idx // 8, idx % 8
                        p.wait(v, "sX5P", val=bXP + idx + 1)
                        p.wait(v, "sXb", val=bXb + max(0, idx - 1))
                        p.inc(v.tensor_tensor(tmp2[:, idx % 2, :],
                                              psl[idx % 3][:], dd[:, m, :],
                                              ALU.mult), "sXa", 1)
                    for i in range(8):
                        p.wait(v, "sPP", val=bPP + i + 1)
                        p.waitb(v, f"p1R{i % 4}~s",
                                16 * (du0[f"R{i % 4}"] + i // 4))
                        p.inc(v.tensor_copy(stR[:, i % 4, :], psW[i % 2][:]),
                              "sRKE", 1)
                    for i in range(8):
                        p.wait(v, "sPP", val=bPP + 8 + i + 1)
                        p.waitb(v, f"p1V{i % 4}~s",
                                16 * (du0[f"V{i % 4}"] + i // 4))
                        p.inc(v.tensor_copy(stV[:, i % 4, :], psW[i % 2][:]),
                              "sVE", 1)

                @blk.gpsimd
                def _(g):
                    for idx in range(40):
                        ns, m = idx // 8, idx % 8
                        p.wait(g, "sXa", val=bXa + idx + 1)
                        p.inc(g.tensor_tensor(xm[:, ns, m, :],
                                              tmp2[:, idx % 2, :],
                                              xh[:, n % 2, m, 1:TC + 1],
                                              ALU.add), "sXb", 1)

                @blk.scalar
                def _(a):
                    for ns in range(5):
                        p.wait(a, "sXlP", val=bLP + ns + 1)
                        p.wait(a, "sXlE", val=bLE + ns)
                        p.inc(a.activation(xl5[:, ns, :], psXl[ns % 2][:],
                                           AF.Tanh), "sXlE", 1)
                    p.wait(a, "sW1", val=bW1 + 1)
                    p.inc(a.activation(wt64[:, 0, :], ps64[:], AF.Tanh),
                          "sWT", 1)
                    for m in range(4):
                        p.wait(a, "sW1", val=bW1 + 2 + m)
                        p.waitb(a, f"p1E{m % 2}~s",
                                16 * (du0[f"E{m % 2}"] + m // 2))
                        p.inc(a.activation(stE[:, m % 2, :], psl[m % 2][:],
                                           AF.Exp,
                                           bias=W["wb_c"][:, m:m + 1]),
                              "sEE", 1)
                    p.wait(a, "sW1", val=bW1 + 6)
                    p.inc(a.activation(wt64[:, 1, :], ps64[:], AF.Tanh),
                          "sWT", 1)
                    for m in range(KT):
                        p.wait(a, "sPP", val=bPP + 16 + m + 1)
                        p.waitb(a, f"p1G{m % 2}~s",
                                16 * (du0[f"G{m % 2}"] + m // 2))
                        p.inc(a.activation(stG[:, m % 2, :], psW[m % 2][:],
                                           AF.Silu,
                                           bias=W["gb_c"][:, m:m + 1]),
                              "sGE", 1)

                @blk.tensor
                def _(t):
                    p.wait(t, "sMix", val=bM + 16)
                    for ns in range(5):
                        if ns >= 2:
                            p.wait(t, "sXlE", val=bLE + ns - 1)
                        for k in range(KT):
                            ins = t.matmul(psXl[ns % 2][:],
                                           W["xpW1T"][:, k,
                                                      ns * 32:(ns + 1) * 32],
                                           mxp[:, k, :], start=(k == 0),
                                           stop=(k == KT - 1))
                        p.inc(ins, "sXlP", 1)
                    for ns in range(5):
                        p.wait(t, "sXlE", val=bLE + ns + 1)
                        for m in range(KT):
                            idx = 8 * ns + m
                            if idx >= 3:
                                p.wait(t, "sXa", val=bXa + idx - 2)
                            ins = t.matmul(psl[idx % 3][:],
                                           W["w2n"][:, ns,
                                                    m * 128:(m + 1) * 128],
                                           xl5[:, ns, :], start=True,
                                           stop=False)
                            ins = t.matmul(psl[idx % 3][:],
                                           W["xbias"][0:1,
                                                      ns * D + m * 128:
                                                      ns * D + (m + 1) * 128],
                                           p.ones_row_bf[0:1, 0:TC],
                                           start=False, stop=True)
                            p.inc(ins, "sX5P", 1)
                    p.wait(t, "sXb", val=bXb + 16)
                    for k in range(KT):
                        ins = t.matmul(ps64[:], W["wAT"][:, k, :],
                                       xm[:, 1, k, :], start=(k == 0),
                                       stop=(k == KT - 1))
                    p.inc(ins, "sW1", 1)
                    p.wait(t, "sWT", val=bWT + 1)
                    p.wait(t, "sXa", val=bXa + 40)
                    for m in range(4):
                        p.wait(t, "sEE", val=bEE + max(0, m - 1))
                        ins = t.matmul(psl[m % 2][:],
                                       W["wBT"][:, m * 128:(m + 1) * 128],
                                       wt64[:, 0, :], start=True, stop=True)
                        p.inc(ins, "sW1", 1)
                    p.wait(t, "sXb", val=bXb + 40)
                    for k in range(KT):
                        ins = t.matmul(ps64[:], W["gAT"][:, k, :],
                                       xm[:, 4, k, :], start=(k == 0),
                                       stop=(k == KT - 1))
                    p.inc(ins, "sW1", 1)
                    for i in range(8):
                        if i == 0:
                            p.wait(t, "sXb", val=bXb + 8)
                        if i == 4:
                            p.wait(t, "sXb", val=bXb + 24)
                        w_, ns = (W["rWT"], 0) if i < 4 else (W["kWT"], 2)
                        m = i % 4
                        if i >= 2:
                            p.wait(t, "sRKE", val=bRKE + i - 1)
                        for k in range(KT):
                            ins = t.matmul(psW[i % 2][:],
                                           w_[:, k, m * 128:(m + 1) * 128],
                                           xm[:, ns, k, :], start=(k == 0),
                                           stop=(k == KT - 1))
                        p.inc(ins, "sPP", 1)
                    p.wait(t, "sXb", val=bXb + 32)
                    for i in range(8):
                        tt, cc = i // 2, i % 2
                        if i < 2:
                            p.wait(t, "sRKE", val=bRKE + 7 + i)
                        p.wait(t, "sVE", val=bVE + max(0, i - 1))
                        for k in range(KT):
                            ins = t.matmul(
                                psW[i % 2][:],
                                xm[:, 3, k, tt * 128:(tt + 1) * 128],
                                W["vWT"][:, k, cc * 512:(cc + 1) * 512],
                                start=(k == 0), stop=(k == KT - 1))
                        p.inc(ins, "sPP", 1)
                    p.wait(t, "sWT", val=bWT + 2)
                    for m in range(KT):
                        if m < 2:
                            p.wait(t, "sVE", val=bVE + 7 + m)
                        p.wait(t, "sGE", val=bGE + max(0, m - 1))
                        ins = t.matmul(psW[m % 2][:],
                                       W["gBT"][:, m * 128:(m + 1) * 128],
                                       wt64[:, 1, :], start=True, stop=True)
                        p.inc(ins, "sPP", 1)
        with nc.Block() as blk:
            @blk.sync
            def _(s):
                for nm2 in du:
                    p.wait(s, f"p1{nm2}~s")
        p.phase_reset()

    with ExitStack() as c2:
        rT = c2.enter_context(nc.sbuf_tensor(f"p2r{u}", [128, T], BF16))
        kT = c2.enter_context(nc.sbuf_tensor(f"p2k{u}", [128, T], BF16))
        eb = c2.enter_context(nc.sbuf_tensor(f"p2e{u}", [128, T], F32))
        Fc = c2.enter_context(nc.sbuf_tensor(f"p2F{u}", [128, T], F32))
        Fsh = c2.enter_context(nc.sbuf_tensor(f"p2Fs{u}", [128, T], F32))
        SPRD = c2.enter_context(nc.sbuf_tensor(f"p2sp{u}", [128, T], F32))
        m32 = c2.enter_context(nc.sbuf_tensor(f"p2m3{u}", [32, TC], BF16))
        i32 = c2.enter_context(nc.sbuf_tensor(f"p2i3{u}", [32, TC], BF16))
        cols = c2.enter_context(nc.sbuf_tensor(f"p2c{u}", [128, 4, NCK], F32))
        colT = c2.enter_context(nc.sbuf_tensor(f"p2ct{u}", [128, 3, 128],
                                               F32))
        Aend = c2.enter_context(nc.sbuf_tensor(f"p2ae{u}", [128, NCK], F32))
        qt = c2.enter_context(nc.sbuf_tensor(f"p2qt{u}", [128, T], BF16))
        kt2 = c2.enter_context(nc.sbuf_tensor(f"p2kt{u}", [128, T], BF16))
        kp = c2.enter_context(nc.sbuf_tensor(f"p2kp{u}", [128, T], BF16))
        qi = c2.enter_context(nc.sbuf_tensor(f"p2qi{u}", [128, T], BF16))
        vts2 = c2.enter_context(nc.sbuf_tensor(f"p2v{u}", [32, 2, 16, DV],
                                               BF16))
        kpT = c2.enter_context(nc.sbuf_tensor(f"p2kpt{u}", [32, 2, 16, 128],
                                              BF16))
        qk = c2.enter_context(nc.sbuf_tensor(f"p2qk{u}", [128, T], BF16))
        drow = c2.enter_context(nc.sbuf_tensor(f"p2dr{u}", [8, T], F32))
        Phat = c2.enter_context(nc.sbuf_tensor(f"p2ph{u}", [32, 2, TC], BF16))
        Sf = c2.enter_context(nc.sbuf_tensor(f"p2S{u}", [128, DV], F32))
        Sb = c2.enter_context(nc.sbuf_tensor(f"p2Sb{u}", [128, DV], BF16))
        osb = c2.enter_context(nc.sbuf_tensor(f"p2o{u}", [128, 2, 2, TC],
                                              BF16))
        t1 = c2.enter_context(nc.sbuf_tensor(f"p2t1{u}", [128, 2, TC], F32))
        t2 = c2.enter_context(nc.sbuf_tensor(f"p2t2{u}", [128, 2, TC], F32))
        t3 = c2.enter_context(nc.sbuf_tensor(f"p2t3{u}", [32, TC], F32))
        t4 = c2.enter_context(nc.sbuf_tensor(f"p2t4{u}", [32, TC], F32))
        psSm = c2.enter_context(nc.psum_tensor(f"p2ps{u}", [128, TC], F32))
        psTr = c2.enter_context(nc.psum_tensor(f"p2pt{u}", [128, TC], F32))
        psTb = c2.enter_context(nc.psum_tensor(f"p2ptb{u}", [32, 128], BF16))
        psP = c2.enter_context(nc.psum_tensor(f"p2pp{u}", [32, TC], F32))
        psDn = c2.enter_context(nc.psum_tensor(f"p2pdn{u}", [32, TC], F32))
        psO = [c2.enter_context(nc.psum_tensor(f"p2po{i}{u}", [128, CLEN],
                                               F32)) for i in range(2)]
        psDS = c2.enter_context(nc.psum_tensor(f"p2pds{u}", [128, DV], F32))

        with nc.sbuf_tensor(f"p2ic{u}", [128, T], BF16) as iC, \
                nc.sbuf_tensor(f"p2ir{u}", [32, TC], BF16) as i32r:
            with nc.Block() as blk:
                @blk.gpsimd
                def _(g):
                    p.inc(g.iota(iC[:], [[1, NCK], [0, CLEN]],
                                 channel_multiplier=0,
                                 allow_small_or_imprecise_dtypes=True),
                          "c2I", 1)
                    p.inc(g.iota(i32r[:], [[0, TC // CLEN], [1, CLEN]],
                                 channel_multiplier=0,
                                 allow_small_or_imprecise_dtypes=True),
                          "c2I", 1)
            with nc.Block() as blk:
                bI = p.mark("c2V")

                @blk.vector
                def _(v):
                    p.inc(v.tensor_scalar(SPRD[:], iC[:], p.iota_p_g[:, 0:1],
                                          None, ALU.is_equal), "c2V", 1)
                    p.inc(v.tensor_scalar(m32[:], i32r[:],
                                          p.iota_p_g[0:32, 0:1],
                                          None, ALU.is_gt), "c2V", 1)
                    p.inc(v.tensor_scalar(i32[:], i32r[:],
                                          p.iota_p_g[0:32, 0:1],
                                          None, ALU.is_equal), "c2V", 1)
        for h in range(H):
            with nc.Block() as blk:
                @blk.sync
                def _(s):
                    p.inc(s.dma_start(rT[:], rTd[:, h, :]), "p2Lr~s", 16)
                    p.inc(s.dma_start(kT[:], kTd[:, h, :]), "p2Lk~s", 16)
                    p.inc(s.dma_start(eb[:], eTd[:, h, :]), "p2Le~s", 16)

                @blk.gpsimd
                def _(g):
                    p.inc(g.memset(Sf[:], 0.0), "p2ms", 1)
                    p.inc(g.memset(Sb[:], 0.0), "p2ms", 1)
                    p.inc(g.memset(Fsh[:, 0:1], 0.0), "p2ms", 1)
            with nc.Block() as blk:
                bV = p.mark("p2V"); bA = p.mark("p2A")

                @blk.vector
                def _(v):
                    p.waitb(v, "p2Le~s", 16 * (h + 1))
                    p.inc(v.tensor_tensor_scan(Fc[:], eb[:], eb[:], 0.0,
                                               ALU.add, ALU.bypass), "p2V", 1)
                    p.wait(v, "p2V")
                    p.inc(v.tensor_copy(Fsh[:, 1:T], Fc[:, 0:T - 1]),
                          "p2V", 1)
                    p.inc(v.tensor_copy(cols[:, 1, :], Fc.ap()[:, 30::32]),
                          "p2V", 1)
                    p.inc(v.tensor_copy(cols[:, 2, :], Fc.ap()[:, 31::32]),
                          "p2V", 1)
                    p.wait(v, "p2V")
                    p.inc(v.memset(cols[:, 0, 0:1], 0.0), "p2V", 1)
                    p.inc(v.tensor_copy(cols[:, 0, 1:NCK],
                                        cols[:, 2, 0:NCK - 1]), "p2V", 1)
                    p.wait(v, "p2V")
                    p.inc(v.tensor_add(cols[:, 3, :], cols[:, 0, :],
                                       cols[:, 1, :]), "p2V", 1)
                    p.wait(v, "p2V")
                    p.inc(v.tensor_scalar_mul(cols[:, 3, :], cols[:, 3, :],
                                              0.5), "p2V", 1)
                    p.inc(v.tensor_sub(Aend[:], cols[:, 0, :],
                                       cols[:, 2, :]), "p2V", 1)
                    p.wait(v, "p2V")
                    p.waitb(v, "p2Lr~s", 16 * (h + 1))
                    p.waitb(v, "p2Lk~s", 16 * (h + 1))
                    p.inc(v.tensor_mul(qk[:], rT[:], kT[:]), "p2V", 1)
                    p.wait(v, "p2V")
                    p.inc(v.tensor_scalar(qk[:], qk[:], W["u_c"][:, h:h + 1],
                                          None, ALU.mult), "p2V", 1)

                @blk.scalar
                def _(a):
                    p.wait(a, "p2V", val=bV + 9)
                    p.inc(a.activation(Aend[:], Aend[:], AF.Exp), "p2A", 1)
            with nc.Block() as blk:
                bT = p.mark("p2T"); bTE = p.mark("p2TE")

                @blk.tensor
                def _(t):
                    for j, ci in enumerate([3, 0, 2]):
                        if j:
                            p.wait(t, "p2TE", val=bTE + j)
                        p.inc(t.transpose(psTr[0:NCK, 0:128], cols[:, ci, :],
                                          p.ident_f32[:]), "p2T", 1)

                @blk.vector
                def _(v):
                    for j in range(3):
                        p.wait(v, "p2T", val=bT + j + 1)
                        p.inc(v.tensor_copy(colT[0:NCK, j, :],
                                            psTr[0:NCK, 0:128]), "p2TE", 1)
            dits = [(j, nn) for j in range(3) for nn in range(NCH)]
            ND = len(dits)
            nsub = [2 if j == 0 else 1 for (j, nn) in dits]
            if h == 0:
                p.l2d = {"S": 0, "X": 0, "E2": 0}
            C2 = p.l2d
            s_c = {}; xs_end = {}; xm_end = {}; e_end = {}
            for i, (j, nn) in enumerate(dits):
                C2["S"] += 1; s_c[i] = C2["S"]
                C2["X"] += nsub[i]; xs_end[i] = C2["X"]
                C2["E2"] += nsub[i]; e_end[i] = C2["E2"]
                C2["X"] += nsub[i]; xm_end[i] = C2["X"]
            psSm2 = [psSm, psTr]

            with nc.Block() as blk:
                @blk.tensor
                def _(t):
                    for i, (j, nn) in enumerate(dits):
                        nlo = nn * TC
                        if i >= 2:
                            p.waitb(t, "p2X", xs_end[i - 2])
                        p.inc(t.matmul(psSm2[i % 2][:], colT[0:NCK, j, :],
                                       SPRD[0:NCK, nlo:nlo + TC],
                                       start=True, stop=True), "p2S", 1)

                @blk.vector
                def _(v):
                    for i in range(ND + 1):
                        if i < ND:
                            j, nn = dits[i]
                            nlo = nn * TC
                            p.waitb(v, "p2S", s_c[i])
                            p.wait(v, "p2X")
                            ps = psSm2[i % 2]
                            if j == 0:
                                p.inc(v.tensor_tensor(t1[:, i % 2, :], ps[:],
                                                      Fsh[:, nlo:nlo + TC],
                                                      ALU.subtract), "p2X", 1)
                                p.inc(v.tensor_tensor(t2[:, i % 2, :],
                                                      Fc[:, nlo:nlo + TC],
                                                      ps[:], ALU.subtract),
                                      "p2X", 1)
                            elif j == 1:
                                p.inc(v.tensor_tensor(t1[:, i % 2, :], ps[:],
                                                      Fsh[:, nlo:nlo + TC],
                                                      ALU.subtract), "p2X", 1)
                            else:
                                p.inc(v.tensor_tensor(t1[:, i % 2, :],
                                                      Fc[:, nlo:nlo + TC],
                                                      ps[:], ALU.subtract),
                                      "p2X", 1)
                        if i >= 1:
                            j, nn = dits[i - 1]
                            nlo = nn * TC
                            r = (i - 1) % 2
                            p.waitb(v, "p2E2", e_end[i - 1])
                            if j == 0:
                                p.inc(v.tensor_tensor(
                                    qt[:, nlo:nlo + TC], rT[:, nlo:nlo + TC],
                                    t1[:, r, :], ALU.mult), "p2X", 1)
                                p.inc(v.tensor_tensor(
                                    kt2[:, nlo:nlo + TC], kT[:, nlo:nlo + TC],
                                    t2[:, r, :], ALU.mult), "p2X", 1)
                            elif j == 1:
                                p.inc(v.tensor_tensor(
                                    qi[:, nlo:nlo + TC], rT[:, nlo:nlo + TC],
                                    t1[:, r, :], ALU.mult), "p2X", 1)
                            else:
                                p.inc(v.tensor_tensor(
                                    kp[:, nlo:nlo + TC], kT[:, nlo:nlo + TC],
                                    t1[:, r, :], ALU.mult), "p2X", 1)

                @blk.scalar
                def _(a):
                    for i, (j, nn) in enumerate(dits):
                        p.waitb(a, "p2X", xs_end[i])
                        if i >= 2:
                            p.waitb(a, "p2X", xm_end[i - 2])
                        p.inc(a.activation(t1[:, i % 2, :], t1[:, i % 2, :],
                                           AF.Exp), "p2E2", 1)
                        if j == 0:
                            p.inc(a.activation(t2[:, i % 2, :],
                                               t2[:, i % 2, :],
                                               AF.Exp), "p2E2", 1)
            with nc.Block() as blk:
                bDP = p.mark("p2DP"); bDR = p.mark("p2DR")

                @blk.tensor
                def _(t):
                    for nn in range(NCH):
                        if nn:
                            p.wait(t, "p2DR", val=bDR + nn)
                        p.inc(t.matmul(psDn[0:8, :], p.ones_col_bf[:],
                                       qk[:, nn * TC:(nn + 1) * TC],
                                       start=True, stop=True), "p2DP", 1)

                @blk.vector
                def _(v):
                    for nn in range(NCH):
                        p.wait(v, "p2DP", val=bDP + nn + 1)
                        p.inc(v.tensor_copy(drow[:, nn * TC:(nn + 1) * TC],
                                            psDn[0:8, :]), "p2DR", 1)
            if h == 0:
                p.od2 = [0, 0]
                p.l2c = {"PP": 0, "PD": 0, "PE": 0, "KT": 0, "KE": 0,
                         "O": 0, "Ev": 0, "Up": 0, "Vs": [0, 0]}
            C = p.l2c
            pp_c = {}; pd_c = {}; t3_c = {}; t4_c = {}; add_c = {}
            ktb = {}; keb = {}; o_end = {}; ev_end = {}; vs_c = {}
            for cg in range(NCH):
                C["PP"] += 1; pp_c[cg] = C["PP"]
                C["PD"] += 1; pd_c[cg] = C["PD"]
                t3_c[cg] = C["PE"] + 1; t4_c[cg] = C["PE"] + 2
                add_c[cg] = C["PE"] + 3; C["PE"] += 3
                ktb[cg] = C["KT"]; C["KT"] += 16
                keb[cg] = C["KE"]; C["KE"] += 16
                C["O"] += 32; o_end[cg] = C["O"]
                C["Ev"] += 32; ev_end[cg] = C["Ev"]
                C["Vs"][cg % 2] += 1; vs_c[cg] = C["Vs"][cg % 2]
            up0 = C["Up"]
            C["Up"] += 2 * 16 * NCH
            od2v0 = list(p.od2)

            with nc.Block() as blk:
                @blk.tensor
                def _(t):
                    for sl in range(NCH + 1):
                        pi = sl if sl < NCH else None
                        si = sl - 1 if sl >= 1 else None
                        pops = []
                        if pi is not None:
                            for jj in range(16):
                                pops.append(("psP", jj))
                            pops.append(("psDn", 0))
                            for jj in range(16):
                                pops.append(("T", jj))

                        def emit_prod(op):
                            kind, jj = op
                            c = 16 * pi + jj
                            if kind == "psP":
                                if jj == 0 and pi >= 1:
                                    p.waitb(t, "p2PE", t3_c[pi - 1])
                                ins = t.matmul(
                                    psP[:, jj * 32:(jj + 1) * 32],
                                    kt2[:, c * 32:(c + 1) * 32],
                                    qt[:, c * 32:(c + 1) * 32],
                                    start=True, stop=True)
                                if jj == 15:
                                    p.inc(ins, "p2PP", 1)
                            elif kind == "psDn":
                                if pi >= 1:
                                    p.waitb(t, "p2PE", t4_c[pi - 1])
                                p.inc(t.matmul(
                                    psDn[:], p.ones_row_f32[0:1, 0:32],
                                    drow[0:1, pi * TC:(pi + 1) * TC],
                                    start=True, stop=True), "p2PD", 1)
                            else:
                                p.waitb(t, "p2KE", keb[pi] + jj)
                                p.inc(t.transpose(
                                    psTb[:], kp[:, c * 32:(c + 1) * 32],
                                    p.ident_bf[:]), "p2KT", 1)

                        pidx = 0
                        for jj in range(16):
                            if si is not None:
                                c = 16 * si + jj
                                M = up0 // 2 + 16 * si + jj
                                if jj == 0:
                                    p.waitb(t, f"p2Vs{si % 2}~s", 16 * vs_c[si])
                                    p.waitb(t, "p2PE", add_c[si])
                                p.waitb(t, "p2Up", 2 * M)
                                p.waitb(t, "p2Ev", max(0, 2 * M - 2))
                                for i in range(2):
                                    ins = t.matmul(
                                        psO[i][:],
                                        Sb[:, i * 128:(i + 1) * 128],
                                        qi[:, c * 32:(c + 1) * 32],
                                        start=True, stop=False)
                                    ins = t.matmul(
                                        psO[i][:],
                                        vts2[:, si % 2, jj,
                                             i * 128:(i + 1) * 128],
                                        Phat[:, si % 2,
                                             jj * 32:(jj + 1) * 32],
                                        start=False, stop=True)
                                p.inc(ins, "p2O", 1)
                                p.waitb(t, "p2KE", keb[si] + jj + 1)
                                ins = t.matmul(psDS[:],
                                               kpT[:, si % 2, jj, :],
                                               vts2[:, si % 2, jj, :],
                                               start=True, stop=True)
                                p.inc(ins, "p2O", 1)
                            for _r in range(2):
                                if pidx < len(pops):
                                    emit_prod(pops[pidx])
                                    pidx += 1
                        while pidx < len(pops):
                            emit_prod(pops[pidx])
                            pidx += 1

                @blk.vector
                def _(v):
                    for sl in range(NCH + 1):
                        pi = sl if sl < NCH else None
                        si = sl - 1 if sl >= 1 else None

                        def emit_copy(kj):
                            p.waitb(v, "p2KT", ktb[pi] + kj + 1)
                            p.inc(v.tensor_copy(kpT[:, pi % 2, kj, :],
                                                psTb[:]), "p2KE", 1)

                        for jj in range(16):
                            if si is not None:
                                c = 16 * si + jj
                                M = up0 // 2 + 16 * si + jj
                                p.waitb(v, "p2O", 2 * M + 1)
                                if jj == 0:
                                    p.waitb(v, f"p2OD{si % 2}~s",
                                            32 * (od2v0[si % 2] + si // 2))
                                    p.od2[si % 2] += 1
                                for i in range(2):
                                    p.inc(v.tensor_copy(
                                        osb[:, si % 2, i,
                                            jj * 32:(jj + 1) * 32],
                                        psO[i][:]), "p2Ev", 1)
                                p.waitb(v, "p2O", 2 * M + 2)
                                p.inc(v.scalar_tensor_tensor(
                                    Sf[:], Sf[:], Aend[:, c:c + 1], psDS[:],
                                    ALU.mult, ALU.add), "p2Up", 1)
                                p.wait(v, "p2Up")
                                p.inc(v.tensor_copy(Sb[:], Sf[:]), "p2Up", 1)
                            if pi is not None and jj == 8:
                                if pi >= 2:
                                    p.waitb(v, "p2O", o_end[pi - 2])
                                p.waitb(v, "p2PP", pp_c[pi])
                                p.wait(v, "p2PE")
                                p.inc(v.tensor_tensor(t3[:], psP[:], m32[:],
                                                      ALU.mult), "p2PE", 1)
                                p.waitb(v, "p2PD", pd_c[pi])
                                p.inc(v.tensor_tensor(t4[:], psDn[:], i32[:],
                                                      ALU.mult), "p2PE", 1)
                                p.wait(v, "p2PE")
                                p.inc(v.tensor_add(Phat[:, pi % 2, :], t3[:],
                                                   t4[:]), "p2PE", 1)
                            if pi is not None:
                                for kj in range(16):
                                    if (17 + kj) // 2 == jj:
                                        emit_copy(kj)
                        if pi is not None:
                            emit_copy(15)

                @blk.sync
                def _(s):
                    for sl in range(NCH + 1):
                        pi = sl if sl < NCH else None
                        si = sl - 1 if sl >= 1 else None
                        if pi is not None:
                            if pi >= 2:
                                p.waitb(s, "p2O", o_end[pi - 2])
                            glo = pi * TC
                            p.inc(s.dma_start(
                                vts2[:, pi % 2, :, :],
                                vtd[glo:glo + TC,
                                    h * DV:(h + 1) * DV].rearrange(
                                    "(c q) d -> q c d", q=32)),
                                f"p2Vs{pi % 2}~s", 16)
                        if si is not None:
                            p.waitb(s, "p2Ev", ev_end[si])
                            for i in range(2):
                                p.inc(s.dma_start(
                                    oTd[:, 2 * h + i,
                                        si * TC:(si + 1) * TC],
                                    osb[:, si % 2, i, :]),
                                    f"p2OD{si % 2}~s", 16)
        with nc.Block() as blk:
            @blk.sync
            def _(s):
                p.wait(s, "p2OD0~s")
                p.wait(s, "p2OD1~s")
        p.phase_reset()

    emit_ln(p, src=oTd, gnw=W["gnw"], gnb=W["gnb"], dst=p.XHD, dst_off=1,
            src_bf=True, src_off=0, mul_dram=gsd)
    p.phase_reset()
    with nc.sbuf_tensor(f"p3w{u}", [128, KT, D], BF16) as wout:
        with nc.Block() as blk:
            @blk.sync
            def _(s):
                p.inc(s.dma_start(wout[:],
                                  W["oWT_bf"].rearrange("(k q) m -> q k m",
                                                        q=128)), "p3W~s", 16)
                p.wait(s, "p3W~s")
        emit_proj_residual(p, p.XHD, 1, wout, bias=None, scale=1.0)
    p.phase_reset()



def convert_weights(p, conv_list, sem="!wconv"):
    nc = p.nc
    with nc.Block() as blk:
        @blk.gpsimd
        def _(g):
            for src, dst in conv_list:
                p.inc(g.dma_start(dst, src), sem, 16)


def load_weight_sb(p, s, sb, bf_dram, sem):
    p.inc(s.dma_start(sb[:], bf_dram.rearrange("(k p) m -> p k m", p=128)), sem, 16)


def load_col_vec(p, s, sb, f32_dram, sem):
    p.inc(s.dma_start(sb[:], f32_dram[:]), sem, 16)



def build_l0_ffn_program():
    nc = bass.Bass("TRN2", target_bir_lowering=False)
    p = Prog(nc)
    x_in = p.dram("x_in", [T, D], F32, kind="ExternalInput")
    wr = p.dram("wrT", [D, D], F32, kind="ExternalInput")
    wk = p.dram("wkT", [D, D], F32, kind="ExternalInput")
    wv = p.dram("wvT", [D, D], F32, kind="ExternalInput")
    mur = p.dram("mu_r", [128, KT], F32, kind="ExternalInput")
    muk = p.dram("mu_k", [128, KT], F32, kind="ExternalInput")
    x_out = p.dram("x_out", [T, D], F32, kind="ExternalOutput")
    wr_bf = p.dram("wr_bf", [D, D], BF16)
    wk_bf = p.dram("wk_bf", [D, D], BF16)
    wv_bf = p.dram("wv_bf", [D, D], BF16)

    with p.ctx:
        build_globals(p)
        W = {
            "wrT": p.sbuf("w_wrT", [128, KT, D], BF16),
            "wkT": p.sbuf("w_wkT", [128, KT, D], BF16),
            "wvT": p.sbuf("w_wvT", [128, KT, D], BF16),
            "mu_r": p.sbuf("w_mur", [128, KT], F32),
            "mu_k": p.sbuf("w_muk", [128, KT], F32),
        }
        convert_weights(p, [(wr[:], wr_bf[:]), (wk[:], wk_bf[:]),
                            (wv[:], wv_bf[:])])
        with nc.Block() as blk:
            @blk.sync
            def _(s):
                p.wait(s, "!wconv")
                load_weight_sb(p, s, W["wrT"], wr_bf, "!wload")
                load_weight_sb(p, s, W["wkT"], wk_bf, "!wload")
                load_weight_sb(p, s, W["wvT"], wv_bf, "!wload")
                load_col_vec(p, s, W["mu_r"], mur, "!wload")
                load_col_vec(p, s, W["mu_k"], muk, "!wload")
                p.wait(s, "!wload")
        emit_load_x(p, x_in)
        emit_ln(p)
        emit_l0_ffn(p, W, scale=0.5)
        emit_store_x(p, x_out)
    return nc


def build_l0_attn_program():
    nc = bass.Bass("TRN2", target_bir_lowering=False)
    p = Prog(nc)
    x_in = p.dram("x_in", [T, D], F32, kind="ExternalInput")
    wqkvT = p.dram("wqkvT", [D, 3072], F32, kind="ExternalInput")
    woutT = p.dram("woutT", [D, D], F32, kind="ExternalInput")
    bout = p.dram("b_out", [128, KT], F32, kind="ExternalInput")
    x_out = p.dram("x_out", [T, D], F32, kind="ExternalOutput")
    wqkv_bf = p.dram("wqkv_bf", [D, 3072], BF16)
    wout_bf = p.dram("wout_bf", [D, D], BF16)

    with p.ctx:
        build_globals(p)
        bout_sb = p.sbuf("bout_sb", [128, KT], F32)
        convert_weights(p, [(wqkvT[:], wqkv_bf[:]), (woutT[:], wout_bf[:])])
        with nc.Block() as blk:
            @blk.sync
            def _(s):
                p.wait(s, "!wconv")
                load_col_vec(p, s, bout_sb, bout, "!wload")
                p.wait(s, "!wload")
        emit_load_x(p, x_in)
        emit_ln(p)
        p.phase_reset()
        emit_mha(p, wqkv_bf, wout_bf, b_out=bout_sb, causal=False,
                 scale_qk=0.125)
        emit_store_x(p, x_out)
    return nc


def build_l3_attn_program():
    nc = bass.Bass("TRN2", target_bir_lowering=False)
    p = Prog(nc)
    x_in = p.dram("x_in", [T, D], F32, kind="ExternalInput")
    wqkvT = p.dram("wqkvT", [D, 3072], F32, kind="ExternalInput")
    wqkswT = p.dram("wqkswT", [D, 2048], F32, kind="ExternalInput")
    woutT = p.dram("woutT", [D, D], F32, kind="ExternalInput")
    ropet = p.dram("ropet", [2, 128, T], BF16, kind="ExternalInput")
    x_out = p.dram("x_out", [T, D], F32, kind="ExternalOutput")
    wqkv_bf = p.dram("wqkv_bf", [D, 3072], BF16)
    wqksw_bf = p.dram("wqksw_bf", [D, 2048], BF16)
    wout_bf = p.dram("wout_bf", [D, D], BF16)

    with p.ctx:
        build_globals(p)
        convert_weights(p, [(wqkvT[:], wqkv_bf[:]), (woutT[:], wout_bf[:]),
                            (wqkswT[:], wqksw_bf[:])])
        with nc.Block() as blk:
            @blk.sync
            def _(s):
                p.wait(s, "!wconv")
        emit_load_x(p, x_in)
        emit_ln(p)
        p.phase_reset()
        emit_mha(p, wqkv_bf, wout_bf, b_out=None, causal=True,
                 rope2_dram=ropet, qkswT_bf_dram=wqksw_bf, scale_qk=0.125)
        emit_store_x(p, x_out)
    return nc


def rope_tables(Tn=None):
    import ml_dtypes
    Tn = Tn or T
    r = np.arange(64)
    f = 10000.0 ** (-2.0 * (r // 2) / 64.0)
    t = np.arange(Tn)
    th = t[None, :] * f[:, None]
    cos = np.cos(th)
    sin = np.sin(th)
    sgn = np.where(r % 2 == 0, -1.0, 1.0)[:, None]
    tab = np.stack([cos, sgn * sin])
    tab = np.concatenate([tab, tab], axis=1)
    return tab.astype(ml_dtypes.bfloat16)


def build_l3_ffn_program():
    nc = bass.Bass("TRN2", target_bir_lowering=False)
    p = Prog(nc)
    x_in = p.dram("x_in", [T, D], F32, kind="ExternalInput")
    wg = p.dram("Wg", [D, 8], F32, kind="ExternalInput")
    we = p.dram("We", [8, D, D], F32, kind="ExternalInput")
    x_out = p.dram("x_out", [T, D], F32, kind="ExternalOutput")
    we_bf = p.dram("we_bf", [8, D, D], BF16)
    mrowd = p.dram("mrowd", [8, T], BF16)

    with p.ctx:
        build_globals(p)
        wg_sb = p.sbuf("wg_sb", [128, KT, 8], BF16)
        convert_weights(p, [(we[:], we_bf[:])])
        with nc.Block() as blk:
            @blk.gpsimd
            def _(g):
                p.inc(g.dma_start(wg_sb[:],
                                  wg.rearrange("(k p) m -> p k m", p=128)),
                      "!wlg", 16)
                p.wait(g, "!wlg")

            @blk.sync
            def _(s):
                p.wait(s, "!wconv")
        emit_load_x(p, x_in)
        emit_ln(p)
        p.phase_reset()
        emit_moe_router_l3(p, wg_sb, mrowd, scale=0.5)
        emit_moe_experts(p, we_bf, mrowd, relu=False)
        emit_store_x(p, x_out)
    return nc


def build_l2_ffn_program():
    nc = bass.Bass("TRN2", target_bir_lowering=False)
    p = Prog(nc)
    x_in = p.dram("x_in", [T, D], F32, kind="ExternalInput")
    wg = p.dram("Wg", [D, D], F32, kind="ExternalInput")
    bg = p.dram("bg", [1, D], F32, kind="ExternalInput")
    we = p.dram("We", [8, D, D], F32, kind="ExternalInput")
    be = p.dram("be", [8, D], F32, kind="ExternalInput")
    x_out = p.dram("x_out", [T, D], F32, kind="ExternalOutput")
    we_bf = p.dram("we_bf", [8, D, D], BF16)
    wg_bf = p.dram("wg_bf", [D, D], BF16)
    be_bf = p.dram("be_bf", [8, D], BF16)
    mrowd = p.dram("mrowd", [8, T], BF16)

    with p.ctx:
        build_globals(p)
        bg_sb = p.sbuf("bg_sb", [1, D], BF16)
        convert_weights(p, [(we[:], we_bf[:]), (wg[:], wg_bf[:]),
                            (be[:], be_bf[:])])
        with nc.Block() as blk:
            @blk.gpsimd
            def _(g):
                p.inc(g.dma_start(bg_sb[:], bg[:]), "!wlg", 16)
                p.wait(g, "!wlg")

            @blk.sync
            def _(s):
                p.wait(s, "!wconv")
        emit_load_x(p, x_in)
        emit_ln(p)
        p.phase_reset()
        emit_moe_router_l2(p, wg_bf, bg_sb, mrowd, scale=0.5)
        emit_moe_experts(p, we_bf, mrowd, relu=True, be_bf_dram=be_bf)
        emit_store_x(p, x_out)
    return nc


def build_l1_attn_program():
    nc = bass.Bass("TRN2", target_bir_lowering=False)
    p = Prog(nc)
    x_in = p.dram("x_in", [T, D], F32, kind="ExternalInput")
    wqkv = p.dram("wqkv", [D, 3072], F32, kind="ExternalInput")
    bq_c = p.dram("bq_c", [128, KT], F32, kind="ExternalInput")
    bkv = p.dram("bkv", [2, D], F32, kind="ExternalInput")
    x_out = p.dram("x_out", [T, D], F32, kind="ExternalOutput")
    wqkv_bf = p.dram("wqkv_bf", [D, 3072], BF16)
    bkv_bf = p.dram("bkv_bf", [2, D], BF16)

    with p.ctx:
        build_globals(p)
        bq_sb = p.sbuf("bq_sb", [128, KT], F32)
        convert_weights(p, [(wqkv[:], wqkv_bf[:]), (bkv[:], bkv_bf[:])])
        with nc.Block() as blk:
            @blk.gpsimd
            def _(g):
                p.inc(g.dma_start(bq_sb[:], bq_c[:]), "!wlg", 16)
                p.wait(g, "!wlg")

            @blk.sync
            def _(s):
                p.wait(s, "!wconv")
        emit_load_x(p, x_in)
        emit_ln(p)
        p.phase_reset()
        emit_l1_attn(p, wqkv_bf, bq_sb, bkv_bf)
        emit_store_x(p, x_out)
    return nc


def build_l1_ffn_program():
    nc = bass.Bass("TRN2", target_bir_lowering=False)
    p = Prog(nc)
    x_in = p.dram("x_in", [T, D], F32, kind="ExternalInput")
    rw = p.dram("rw", [D, 8], F32, kind="ExternalInput")
    w1 = p.dram("w1", [8, D, 2816], F32, kind="ExternalInput")
    w2 = p.dram("w2", [8, 1408, D], F32, kind="ExternalInput")
    x_out = p.dram("x_out", [T, D], F32, kind="ExternalOutput")
    w1_bf = p.dram("w1_bf", [8, D, 2816], BF16)
    w2_bf = p.dram("w2_bf", [8, 1408, D], BF16)
    mrowd = p.dram("mrowd", [8, T], BF16)

    with p.ctx:
        build_globals(p)
        wg_sb = p.sbuf("wg_sb", [128, KT, 8], BF16)
        convert_weights(p, [(w1[:], w1_bf[:]), (w2[:], w2_bf[:])])
        with nc.Block() as blk:
            @blk.gpsimd
            def _(g):
                p.inc(g.dma_start(wg_sb[:],
                                  rw.rearrange("(k q) m -> q k m", q=128)),
                      "!wlg", 16)
                p.wait(g, "!wlg")

            @blk.sync
            def _(s):
                p.wait(s, "!wconv")
        emit_load_x(p, x_in)
        emit_ln(p)
        p.phase_reset()
        emit_moe_router_l3(p, wg_sb, mrowd, scale=0.5, renorm=False)
        emit_l1_ffn(p, w1_bf, w2_bf, mrowd)
        emit_store_x(p, x_out)
    return nc


def load_l2_weights(p, ins_dram):
    nc = p.nc
    W = {}
    W["u_c"] = p.sbuf("l2w_u", [128, 4], F32)
    W["gnw"] = p.sbuf("l2w_gw", [128, KT], F32)
    W["gnb"] = p.sbuf("l2w_gn", [128, KT], F32)
    W["oWT_bf"] = ins_dram["oWT_bf"]
    W["_dram"] = ins_dram
    with nc.Block() as blk:
        @blk.gpsimd
        def _(g):
            for nm2 in ["u_c", "gnw", "gnb"]:
                p.inc(g.dma_start(W[nm2][:], ins_dram[nm2][:]), "!l2wl", 16)
            p.wait(g, "!l2wl")
    return W


def load_l2_p1_weights(p, W, c1):
    nc = p.nc
    ins_dram = W["_dram"]
    for nm2, key, shp, dt in [
            ("xp_mu", "l2w_mu", [128, KT], F32),
            ("xpW1T", "l2w_w1", [128, KT, 160], BF16),
            ("w2n", "l2w_w2", [32, 5, D], BF16),
            ("xbias", "l2w_xb", [1, 5 * D], BF16),
            ("rWT", "l2w_r", [128, KT, 512], BF16),
            ("wAT", "l2w_wa", [128, KT, 64], BF16),
            ("wBT", "l2w_wb", [64, 512], BF16),
            ("wb_c", "l2w_wbc", [128, 4], F32),
            ("kWT", "l2w_k", [128, KT, 512], BF16),
            ("vWT", "l2w_v", [128, KT, D], BF16),
            ("gAT", "l2w_ga", [128, KT, 64], BF16),
            ("gBT", "l2w_gb", [64, D], BF16),
            ("gb_c", "l2w_gbc", [128, KT], F32)]:
        W[nm2] = c1.enter_context(nc.sbuf_tensor(key, shp, dt))
    with nc.Block() as blk:
        @blk.gpsimd
        def _(g):
            for nm2, rearr in [
                    ("xp_mu", None), ("xpW1T", "kp"), ("w2n", None),
                    ("xbias", None), ("rWT", "kp"), ("wAT", "kp"),
                    ("wBT", None), ("wb_c", None), ("kWT", "kp"),
                    ("vWT", "kp"), ("gAT", "kp"), ("gBT", None),
                    ("gb_c", None)]:
                srcd = ins_dram[nm2]
                ap = (srcd.rearrange("(k q) m -> q k m", q=128)
                      if rearr == "kp" else srcd[:])
                p.inc(g.dma_start(W[nm2][:], ap), "!l2w1", 16)
            p.wait(g, "!l2w1")


def build_l2_attn_program():
    nc = bass.Bass("TRN2", target_bir_lowering=False)
    p = Prog(nc)
    x_in = p.dram("x_in", [T, D], F32, kind="ExternalInput")
    ins = {}
    for nm2, shp in [("xp_mu", [128, KT]), ("xpW1T", [D, 160]),
                     ("w2n", [32, 5, D]), ("xbias", [1, 5 * D]),
                     ("rWT", [D, 512]), ("wAT", [D, 64]), ("wBT", [64, 512]),
                     ("wb_c", [128, 4]), ("kWT", [D, 512]), ("vWT", [D, D]),
                     ("gAT", [D, 64]), ("gBT", [64, D]), ("gb_c", [128, KT]),
                     ("u_c", [128, 4]), ("gnw", [128, KT]),
                     ("gnb", [128, KT]), ("oWT", [D, D])]:
        ins[nm2] = p.dram(nm2, shp, F32, kind="ExternalInput")
    x_out = p.dram("x_out", [T, D], F32, kind="ExternalOutput")
    ins["oWT_bf"] = p.dram("oWT_bf", [D, D], BF16)

    with p.ctx:
        build_globals(p)
        convert_weights(p, [(ins["oWT"][:], ins["oWT_bf"][:])])
        W = load_l2_weights(p, ins)
        with nc.Block() as blk:
            @blk.sync
            def _(s):
                p.wait(s, "!wconv")
        emit_load_x(p, x_in)
        emit_ln(p)
        p.phase_reset()
        emit_l2_attn(p, W)
        emit_store_x(p, x_out)
    return nc


def build_megakernel_program():
    nc = bass.Bass("TRN2", target_bir_lowering=False)
    p = Prog(nc)
    x_in = p.dram("x_in", [128, KT, T], F32, kind="ExternalInput")
    x_out = p.dram("x_out", [128, KT, T], F32, kind="ExternalOutput")
    I = {}

    def din(name, shp, dt=F32):
        I[name] = p.dram(name, shp, dt, kind="ExternalInput")
        return I[name]

    din("l0f_wrT", [D, D]); din("l0f_wkT", [D, D]); din("l0f_wvT", [D, D])
    din("l0f_mur", [128, KT]); din("l0f_muk", [128, KT])
    din("l0a_wqkvT", [D, 3072]); din("l0a_woutT", [D, D])
    din("l0a_bout", [128, KT])
    din("l1f_rw", [D, 8]); din("l1f_w1", [8, D, 2816])
    din("l1f_w2", [8, 1408, D])
    din("l1a_wqkv", [D, 3072]); din("l1a_bq", [128, KT]); din("l1a_bkv", [2, D])
    din("l2f_wg", [D, D]); din("l2f_bg", [1, D]); din("l2f_we", [8, D, D])
    din("l2f_be", [8, D])
    l2a = {}
    for nm2, shp in [("xp_mu", [128, KT]), ("xpW1T", [D, 160]),
                     ("w2n", [32, 5, D]), ("xbias", [1, 5 * D]),
                     ("rWT", [D, 512]), ("wAT", [D, 64]), ("wBT", [64, 512]),
                     ("wb_c", [128, 4]), ("kWT", [D, 512]), ("vWT", [D, D]),
                     ("gAT", [D, 64]), ("gBT", [64, D]), ("gb_c", [128, KT]),
                     ("u_c", [128, 4]), ("gnw", [128, KT]),
                     ("gnb", [128, KT]), ("oWT", [D, D])]:
        l2a[nm2] = din("l2a_" + nm2, shp)
    din("l3f_wg", [D, 8]); din("l3f_we", [8, D, D])
    din("l3a_wqkvT", [D, 3072]); din("l3a_wqkswT", [D, 2048])
    din("l3a_woutT", [D, D])
    din("l3a_ropet", [2, 128, T], BF16)

    B = {}
    for nm2, shp in [("l0f_wrT", [D, D]), ("l0f_wkT", [D, D]),
                     ("l0f_wvT", [D, D]), ("l0a_wqkvT", [D, 3072]),
                     ("l0a_woutT", [D, D]),
                     ("l1f_w1", [8, D, 2816]), ("l1f_w2", [8, 1408, D]),
                     ("l1a_wqkv", [D, 3072]), ("l1a_bkv", [2, D]),
                     ("l2f_wg", [D, D]), ("l2f_we", [8, D, D]),
                     ("l2f_be", [8, D]), ("l2a_oWT", [D, D]),
                     ("l3f_we", [8, D, D]), ("l3a_wqkvT", [D, 3072]),
                     ("l3a_wqkswT", [D, 2048]), ("l3a_woutT", [D, D])]:
        B[nm2] = p.dram(nm2 + "_bf", shp, BF16)
    l2a["oWT_bf"] = B["l2a_oWT"]
    mrowd = p.dram("mrowd", [8, T], BF16)

    grp = {"!cv0": ["l0f_wrT", "l0f_wkT", "l0f_wvT", "l0a_wqkvT",
                    "l0a_woutT"],
           "!cv1": ["l1f_w1", "l1f_w2", "l1a_wqkv", "l1a_bkv"],
           "!cv2": ["l2f_wg", "l2f_we", "l2f_be", "l2a_oWT"],
           "!cv3": ["l3f_we", "l3a_wqkvT", "l3a_wqkswT", "l3a_woutT"]}

    def cv_gate(sem):
        with nc.Block() as blk:
            @blk.sync
            def _(s):
                p.wait(s, sem)

    with p.ctx:
        build_globals(p, xrd=x_out)
        bout_sb = p.sbuf("bout_sb", [128, KT], F32)
        wg1_sb = p.sbuf("wg1_sb", [128, KT, 8], BF16)
        bq_sb = p.sbuf("bq_sb", [128, KT], F32)
        wg3_sb = p.sbuf("wg3_sb", [128, KT, 8], BF16)
        with nc.Block() as blk:
            @blk.gpsimd
            def _(g):
                p.inc(g.dma_start(wg1_sb[:],
                                  I["l1f_rw"].rearrange("(k q) m -> q k m",
                                                        q=128)), "!wlg", 16)
                p.inc(g.dma_start(wg3_sb[:],
                                  I["l3f_wg"].rearrange("(k q) m -> q k m",
                                                        q=128)), "!wlg", 16)
                p.wait(g, "!wlg")

            @blk.sync
            def _(s):
                p.inc(s.dma_start(x_out[:], x_in[:]), "!xcp", 16)
                load_col_vec(p, s, bout_sb, I["l0a_bout"], "!wload")
                load_col_vec(p, s, bq_sb, I["l1a_bq"], "!wload")
                p.wait(s, "!wload")
        W2 = load_l2_weights(p, l2a)
        for sem in ["!cv0", "!cv1", "!cv2", "!cv3"]:
            convert_weights(p, [(I[nm2][:], B[nm2][:]) for nm2 in grp[sem]],
                            sem=sem)
        with nc.Block() as blk:
            @blk.sync
            def _(s):
                p.wait(s, "!xcp")

        cv_gate("!cv0")
        with ExitStack() as s0:
            W0 = {
                "wrT": s0.enter_context(
                    nc.sbuf_tensor("w_wrT", [128, KT, D], BF16)),
                "wkT": s0.enter_context(
                    nc.sbuf_tensor("w_wkT", [128, KT, D], BF16)),
                "wvT": s0.enter_context(
                    nc.sbuf_tensor("w_wvT", [128, KT, D], BF16)),
                "mu_r": s0.enter_context(
                    nc.sbuf_tensor("w_mur", [128, KT], F32)),
                "mu_k": s0.enter_context(
                    nc.sbuf_tensor("w_muk", [128, KT], F32)),
            }
            with nc.Block() as blk:
                @blk.sync
                def _(s):
                    load_weight_sb(p, s, W0["wrT"], B["l0f_wrT"], "!w0l")
                    load_weight_sb(p, s, W0["wkT"], B["l0f_wkT"], "!w0l")
                    load_weight_sb(p, s, W0["wvT"], B["l0f_wvT"], "!w0l")
                    load_col_vec(p, s, W0["mu_r"], I["l0f_mur"], "!w0l")
                    load_col_vec(p, s, W0["mu_k"], I["l0f_muk"], "!w0l")
                    p.wait(s, "!w0l")
            emit_ln(p)
            emit_l0_ffn(p, W0, scale=0.5)
            p.phase_reset()
            emit_ln(p)
            p.phase_reset()
            emit_mha(p, B["l0a_wqkvT"], B["l0a_woutT"], b_out=bout_sb,
                     causal=False, scale_qk=0.125)
            p.phase_reset()
            emit_ln(p)
            emit_l0_ffn(p, W0, scale=0.5)
            p.phase_reset()

        cv_gate("!cv1")
        for _rep in range(2):
            emit_ln(p)
            p.phase_reset()
            emit_moe_router_l3(p, wg1_sb, mrowd, scale=0.5, renorm=False)
            emit_l1_ffn(p, B["l1f_w1"], B["l1f_w2"], mrowd)
            p.phase_reset()
            if _rep == 0:
                emit_ln(p)
                p.phase_reset()
                emit_l1_attn(p, B["l1a_wqkv"], bq_sb, B["l1a_bkv"])
                p.phase_reset()

        cv_gate("!cv2")
        for _rep in range(2):
            emit_ln(p)
            p.phase_reset()
            with nc.sbuf_tensor(f"bg_sb{_rep}", [1, D], BF16) as bg_sb:
                with nc.Block() as blk:
                    @blk.gpsimd
                    def _(g):
                        p.inc(g.dma_start(bg_sb[:], I["l2f_bg"][:]),
                              "!wlg", 16)
                        p.wait(g, "!wlg")
                emit_moe_router_l2(p, B["l2f_wg"], bg_sb, mrowd, scale=0.5)
                emit_moe_experts(p, B["l2f_we"], mrowd, relu=True,
                                 be_bf_dram=B["l2f_be"])
                p.phase_reset()
            if _rep == 0:
                emit_ln(p)
                p.phase_reset()
                emit_l2_attn(p, W2)

        cv_gate("!cv3")
        for _rep in range(2):
            emit_ln(p)
            p.phase_reset()
            emit_moe_router_l3(p, wg3_sb, mrowd, scale=0.5)
            emit_moe_experts(p, B["l3f_we"], mrowd, relu=False)
            p.phase_reset()
            if _rep == 0:
                emit_ln(p)
                p.phase_reset()
                emit_mha(p, B["l3a_wqkvT"], B["l3a_woutT"], b_out=None,
                         causal=True, rope2_dram=I["l3a_ropet"],
                         qkswT_bf_dram=B["l3a_wqkswT"], scale_qk=0.125)
                p.phase_reset()
    return nc


def kernel(**inputs):
    from concourse.bass_utils import run_bass_kernel_spmd
    ca = np.ascontiguousarray
    ii = {k: np.asarray(v, dtype=np.float32) for k, v in inputs.items()}
    x = ca(ii["L0_attn_x"])
    perm = np.arange(D) ^ 1
    wqT = ii["L3_attn_wq"].T
    wkT = ii["L3_attn_wk"].T

    im = {
        "l0f_wrT": ca(ii["L0_ffn_W_r"].T),
        "l0f_wkT": ca(ii["L0_ffn_W_k"].T),
        "l0f_wvT": ca(ii["L0_ffn_W_v"].T),
        "l0f_mur": ca(ii["L0_ffn_mu_r"].reshape(8, 128).T),
        "l0f_muk": ca(ii["L0_ffn_mu_k"].reshape(8, 128).T),
        "l0a_wqkvT": ca(ii["L0_attn_w_qkv"].T),
        "l0a_woutT": ca(ii["L0_attn_w_out"].T),
        "l0a_bout": ca(ii["L0_attn_b_out"].reshape(8, 128).T),
        "l1f_rw": ca(ii["L1_ffn_router_weight"]),
        "l1f_w1": ca(ii["L1_ffn_w1"]),
        "l1f_w2": ca(ii["L1_ffn_w2"]),
        "l1a_wqkv": ca(np.concatenate([ii["L1_attn_Wq"], ii["L1_attn_Wk"],
                                       ii["L1_attn_Wv"]], axis=1)),
        "l1a_bq": ca(ii["L1_attn_bq"].reshape(8, 128).T),
        "l1a_bkv": ca(np.stack([ii["L1_attn_bk"], ii["L1_attn_bv"]])),
        "l2f_wg": ca(ii["L2_ffn_Wg"]),
        "l2f_bg": ca(ii["L2_ffn_bg"].reshape(1, D)),
        "l2f_we": ca(ii["L2_ffn_We"]),
        "l2f_be": ca(ii["L2_ffn_be"]),
        "l2a_xp_mu": ca(ii["L2_attn_xp_mu"].reshape(8, 128).T),
        "l2a_xpW1T": ca(ii["L2_attn_xp_W1"].T),
        "l2a_w2n": ca(ii["L2_attn_xp_W2"].reshape(D, 5, 32)
                      .transpose(2, 1, 0)),
        "l2a_xbias": ca(ii["L2_attn_x_bias"].reshape(1, 5 * D)),
        "l2a_rWT": ca(ii["L2_attn_r_W"].T),
        "l2a_wAT": ca(ii["L2_attn_w_A"].T),
        "l2a_wBT": ca(ii["L2_attn_w_B"].T),
        "l2a_wb_c": ca(ii["L2_attn_w_b"].reshape(4, 128).T),
        "l2a_kWT": ca(ii["L2_attn_k_W"].T),
        "l2a_vWT": ca(ii["L2_attn_v_W"].T),
        "l2a_gAT": ca(ii["L2_attn_g_A"].T),
        "l2a_gBT": ca(ii["L2_attn_g_B"].T),
        "l2a_gb_c": ca(ii["L2_attn_g_b"].reshape(8, 128).T),
        "l2a_u_c": ca(ii["L2_attn_bonus"].T),
        "l2a_gnw": ca(ii["L2_attn_gn_w"].reshape(8, 128).T),
        "l2a_gnb": ca(ii["L2_attn_gn_b"].reshape(8, 128).T),
        "l2a_oWT": ca(ii["L2_attn_o_W"].T),
        "l3f_wg": ca(ii["L3_ffn_Wg"]),
        "l3f_we": ca(ii["L3_ffn_We"]),
        "l3a_wqkvT": ca(np.concatenate([wqT, wkT, ii["L3_attn_wv"].T],
                                       axis=1)),
        "l3a_wqkswT": ca(np.concatenate([wqT[:, perm], wkT[:, perm]],
                                        axis=1)),
        "l3a_woutT": ca(ii["L3_attn_wo"].T),
        "l3a_ropet": rope_tables(),
    }

    nc = build_megakernel_program()
    in_maps = []
    for c in range(8):
        d = {"x_in": ca(x[c].T.reshape(8, 128, T).transpose(1, 0, 2))}
        d.update(im)
        in_maps.append(d)
    res = run_bass_kernel_spmd(nc, in_maps, core_ids=list(range(8)))
    out = []
    for c in range(8):
        xo = np.asarray(res.results[c]["x_out"], dtype=np.float32)
        out.append(xo.transpose(1, 0, 2).reshape(D, T).T)
    return np.stack(out)
